# version 11
# baseline (speedup 1.0000x reference)
import numpy as np
import concourse.bass as bass
import concourse.mybir as mybir
from concourse.bass_utils import run_bass_kernel_spmd
from contextlib import ExitStack

F32 = mybir.dt.float32
BF16 = mybir.dt.bfloat16
AF = mybir.ActivationFunctionType
ALU = mybir.AluOpType
AX = mybir.AxisListType

D = 1024
K32 = 32
EPS = 1e-5
FP4_MID = [0.25, 0.75, 1.25, 1.75, 2.5, 3.5, 5.0]



_SEMCNT = [0]


def _semname():
    _SEMCNT[0] += 1
    return f"sem{_SEMCNT[0]}"


def _tname(n):
    _SEMCNT[0] += 1
    return f"{n}_{_SEMCNT[0]}"

def bc(ap_small, ap_big):
    a, _ = bass.broadcast_tensor_aps(ap_small, ap_big)
    return a


def _weight_pieces():
    P = []
    for b in range(6):
        for r in range(4 * b, min(4 * b + 4, 22)):
            P.append(("g", r, 0, 1024))
        for r in range(4 * b, min(4 * b + 4, 22)):
            P.append(("u", r, 0, 1024))
    for r in range(8):
        for off, pw in ((0, 1024), (1024, 1024), (2048, 768)):
            P.append(("d", r, off, pw))
    return P


def emit_stageA(nc, io, T_loc, EX, sst=None, wio=None):
    n_ch = (T_loc + EX) // 128
    WP = _weight_pieces() if wio is not None else []

    def wpieces_of(ci):
        if wio is None or ci < 0:
            return []
        lo = (len(WP) * ci) // n_ch
        hi = (len(WP) * (ci + 1)) // n_ch
        return list(range(lo, hi))
    x_in = io["x_in"]
    c_WT = io["c_WT"]
    c_b32 = io["c_b32"]
    z1_out = io["z1_out"]
    qxb_out = io["qxb_out"]
    flatbuf = io["flatbuf"]

    PE_N, ACT_N, DVE_N = 6, 7, 7
    IPE, IACT, IDVE = 0, 0, 2

    def m(ci, k):
        return IPE + ci * PE_N + k

    def a(ci, k):
        return IACT + ci * ACT_N + k

    def d(ci, k):
        return IDVE + ci * DVE_N + k

    with ExitStack() as st:
        def sb(name, shape, dt=F32):
            return st.enter_context(nc.sbuf_tensor(_tname(name), shape, dt))

        def psum(name, shape, dt=F32):
            return st.enter_context(nc.psum_tensor(_tname(name), shape, dt))

        x_a = sb("x_a", [128, 3 * D])
        x_ab = sb("x_ab", [128, 3 * D], BF16)
        x_b = sb("x_b", [128, D], BF16)
        wt_sb = sb("wt_sb", [128, 8 * K32], BF16)
        b32_sb = sb("b32_sb", [K32, 1])
        e_b = sb("e_b", [K32, 128])
        e_pad = sb("e_pad", [128, 2 * 161])
        bt_sb = sb("bt_sb", [128, 2 * 160])
        b1_sb = sb("b1_sb", [128, 128], BF16)
        b2_sb = sb("b2_sb", [K32, 128], BF16)
        rinv = sb("rinv", [128, 1])
        zpre = sb("zpre", [128, D])
        z1n = sb("z1n", [128, 2 * D])
        sqdump = sb("sqdump", [128, D])
        stat = sb("stat", [128, 16])
        au = sb("au", [128, D])
        un = sb("un", [128, D])
        lad = sb("lad", [128, D])
        lad2 = sb("lad2", [128, D])
        sg = sb("sg", [128, D])
        bm = sb("bm", [128, 64])
        rb6 = sb("rb6", [128, 64])
        sc6 = sb("sc6", [128, 64])
        qx_a = sb("qx_a", [128, D], BF16)
        qxb_sb = sb("qxb_sb", [128, D], BF16)
        ident_f = sb("ident_f", [128, 128])
        ident_b = sb("ident_b", [128, 128], BF16)

        pA = psum("pA", [128, 512])
        pB = psum("pB", [128, 512])
        pS = psum("pS", [128, 128])
        pC = psum("pC", [128, 128])
        pAt0 = psum("pAt0", [128, 512])
        pAt1 = psum("pAt1", [128, 512])
        pQ0 = psum("pQ0", [128, 512], BF16)
        pQ1 = psum("pQ1", [128, 512], BF16)

        s_x = [(sst if sst is not None else st).enter_context(nc.semaphore(_semname())) for i in range(3)]
        s_fw = [(sst if sst is not None else st).enter_context(nc.semaphore(_semname())) for i in range(2)]
        s_fr = [(sst if sst is not None else st).enter_context(nc.semaphore(_semname())) for i in range(2)]
        s_pe = (sst if sst is not None else st).enter_context(nc.semaphore(_semname()))
        s_act = (sst if sst is not None else st).enter_context(nc.semaphore(_semname()))
        s_dve = (sst if sst is not None else st).enter_context(nc.semaphore(_semname()))
        s_stz = [(sst if sst is not None else st).enter_context(nc.semaphore(_semname())) for i in range(2)]
        s_stq = (sst if sst is not None else st).enter_context(nc.semaphore(_semname()))
        s_c = (sst if sst is not None else st).enter_context(nc.semaphore(_semname()))
        if wio is not None:
            wloadw = sb("wloadw", [128, 2 * 1024])
            auw = sb("auw", [128, 4 * 1024])
            sgw = sb("sgw", [128, 4 * 1024])
            qww = sb("qww", [128, 4 * 1024], BF16)
            s_wl = [st.enter_context(nc.semaphore(_semname())) for _ in range(2)]
            s_wo = [st.enter_context(nc.semaphore(_semname())) for _ in range(2)]
            s_wab = st.enter_context(nc.semaphore(_semname()))
            s_wq = st.enter_context(nc.semaphore(_semname()))
            w_src = {"g": wio["w_gate"], "u": wio["w_up"], "d": wio["w_down"]}
            qwhbm = wio["qwhbm"]

        block = st.enter_context(nc.Block())

        @block.sync
        def _(sync):
            for ci in range(n_ch + 3):
                if ci < n_ch:
                    if ci >= 3:
                        sync.wait_ge(s_pe, m(ci - 3, 1))
                        sync.wait_ge(s_dve, d(ci - 3, 4))
                    sync.dma_start(out=x_a[:, (ci % 3) * D:(ci % 3 + 1) * D],
                                   in_=x_in[ci * 128:(ci + 1) * 128, :]).then_inc(s_x[ci % 3], 16)
                elif ci == n_ch:
                    sync.wait_ge(s_pe, m(n_ch - 3, 1))
                    sync.wait_ge(s_dve, d(n_ch - 3, 4))
                    sync.dma_start(out=x_a[0:32, (n_ch % 3) * D:(n_ch % 3) * D + D],
                                   in_=x_in[n_ch * 128:n_ch * 128 + 32, :]).then_inc(s_x[n_ch % 3], 16)
                cj = ci - 3
                if 0 <= cj < n_ch:
                    sync.wait_ge(s_dve, d(cj, 6))
                    sync.dma_start(out=z1_out[cj * 128:(cj + 1) * 128, :],
                                   in_=z1n[:, (cj % 2) * D:(cj % 2 + 1) * D]).then_inc(s_stz[cj % 2], 16)

            sync.wait_ge(s_stz[0], 16 * ((n_ch + 1) // 2))
            sync.wait_ge(s_stz[1], 16 * (n_ch // 2))
            sync.wait_ge(s_stq, 16 * n_ch)

        @block.gpsimd
        def _(g):
            g.dma_start(out=wt_sb[:].rearrange("p (j k) -> p j k", k=K32),
                        in_=c_WT.rearrange("(j p) k -> p j k", p=128)).then_inc(s_c, 16)
            g.dma_start(out=b32_sb[:], in_=c_b32).then_inc(s_c, 16)
            g.memset(e_pad[:], 0.0)
            g.iota(sqdump[:, 0:128], [[1, 128]], base=0, channel_multiplier=-1,
                   allow_small_or_imprecise_dtypes=True)
            g.drain().then_inc(s_dve, 1)
            for ci in range(n_ch):
                g.wait_ge(s_dve, d(ci, 2))
                g.dma_start(out=flatbuf[ci % 2, :],
                            in_=e_pad[:, (ci % 2) * 161:(ci % 2) * 161 + 161]).then_inc(s_fw[ci % 2], 16)
                g.wait_ge(s_fw[ci % 2], 16 * (ci // 2 + 1))
                src = bass.AP(flatbuf.tensor, flatbuf.offset + (ci % 2) * (128 * 161),
                              [[160, 128], [1, 160]])
                if ci >= 2:
                    g.wait_ge(s_pe, m(ci - 2, 4))
                g.dma_start(out=bt_sb[:, (ci % 2) * 160:(ci % 2) * 160 + 160],
                            in_=src).then_inc(s_fr[ci % 2], 16)
                def w_out(q):
                    kq, rq, offq, pwq = WP[q]
                    g.wait_ge(s_wq, q + 1)
                    g.dma_start(
                        out=qwhbm[q, :, 0:pwq],
                        in_=qww[:, (q % 4) * 1024:(q % 4) * 1024 + pwq],
                    ).then_inc(s_wo[q % 2], 16)

                for p in wpieces_of(ci):
                    kind, r, off, pw = WP[p]
                    if p >= 2:
                        g.wait_ge(s_wab, p - 1)
                    g.dma_start(
                        out=wloadw[:, (p % 2) * 1024:(p % 2) * 1024 + pw],
                        in_=w_src[kind][r * 128:(r + 1) * 128, off:off + pw],
                    ).then_inc(s_wl[p % 2], 16)
                for q in wpieces_of(ci - 2):
                    w_out(q)
                g.wait_ge(s_act, a(ci, 7))
                g.dma_start(
                    out=qxb_out[:, ci * 128:(ci + 1) * 128].rearrange("(j p) t -> p j t", p=128),
                    in_=qxb_sb[:].rearrange("p (j t) -> p j t", t=128),
                ).then_inc(s_stq, 16)
            if wio is not None:
                for q in wpieces_of(n_ch - 2) + wpieces_of(n_ch - 1):
                    w_out(q)
            g.wait_ge(s_stq, 16 * n_ch)
            if wio is not None:
                g.wait_ge(s_wo[0], 16 * ((len(WP) + 1) // 2))
                g.wait_ge(s_wo[1], 16 * (len(WP) // 2))

        @block.vector
        def _(v):
            v.wait_ge(s_c, 32)
            v.wait_ge(s_dve, 1)
            v.tensor_scalar(out=ident_f[:], in0=sqdump[:, 0:128], scalar1=0.0,
                            scalar2=None, op0=ALU.is_equal)
            v.drain()
            v.tensor_copy(ident_b[:], ident_f[:])
            v.drain().then_inc(s_dve, 1)
            def emit_wlad(p):
                kind, r, off, pw = WP[p]
                v.wait_ge(s_wab, p + 1)
                if p >= 4:
                    v.wait_ge(s_wo[p % 2], 16 * ((p - 4) // 2 + 1))
                emit_fp4_ladder(
                    v, pw, auw[:, (p % 4) * 1024:(p % 4) * 1024 + pw],
                    sgw[:, (p % 4) * 1024:(p % 4) * 1024 + pw],
                    qww[:, (p % 4) * 1024:(p % 4) * 1024 + pw],
                    {"un": un, "lad": lad, "lad2": lad2,
                     "bm": bm, "rb6": rb6, "sc6": sc6},
                ).then_inc(s_wq, 1)

            for ci in range(n_ch):
                wl = wpieces_of(ci - 1) if ci >= 1 else []
                if ci == n_ch - 1:
                    wl = wl + wpieces_of(ci)
                wlq = [[], [], wl]
                if ci == 0:
                    v.wait_ge(s_x[0], 16)
                    v.tensor_copy(x_ab[:, 0:D], x_a[:, 0:D])
                nxt = ci + 1
                v.wait_ge(s_x[nxt % 3], 16 * (nxt // 3 + 1))
                if nxt >= 3:
                    v.wait_ge(s_pe, m(nxt - 3, 5))
                if nxt < n_ch:
                    v.tensor_copy(x_ab[:, (nxt % 3) * D:(nxt % 3 + 1) * D],
                                  x_a[:, (nxt % 3) * D:(nxt % 3 + 1) * D])
                else:
                    v.tensor_copy(x_ab[0:32, (nxt % 3) * D:(nxt % 3) * D + D],
                                  x_a[0:32, (nxt % 3) * D:(nxt % 3) * D + D])
                v.drain().then_inc(s_dve, 1)
                v.wait_ge(s_pe, m(ci, 3))
                if ci >= 2:
                    v.wait_ge(s_fw[ci % 2], 16 * (ci // 2))
                ep = e_pad[:, (ci % 2) * 161:(ci % 2) * 161 + 161]
                v.tensor_copy(ep[:, 0:K32], pS[:, 0:K32])
                v.drain()
                v.tensor_reduce(out=stat[:, 0:1], in_=ep[:, 0:K32], axis=AX.X, op=ALU.add)
                v.drain()
                v.reciprocal(out=rinv[:], in_=stat[:, 0:1]).then_inc(s_dve, 1)
                v.wait_ge(s_pe, m(ci, 4))
                v.tensor_copy(b1_sb[:], pC[:])
                v.tensor_copy(b2_sb[:], pS[0:K32, :]).then_inc(s_dve, 1)
                for p in wlq[0]:
                    emit_wlad(p)
                v.wait_ge(s_pe, m(ci, 5))
                if ci >= 1:
                    v.wait_ge(s_act, a(ci - 1, 3))
                xa = x_a[:, (ci % 3) * D:(ci % 3 + 1) * D]
                v.drain()
                v.scalar_tensor_tensor(out=zpre[:, 0:512], in0=pAt0[:], scalar=rinv[:],
                                       in1=xa[:, 0:512], op0=ALU.mult, op1=ALU.add,
                                       accum_out=stat[:, 0:1])
                v.scalar_tensor_tensor(out=zpre[:, 512:1024], in0=pAt1[:], scalar=rinv[:],
                                       in1=xa[:, 512:1024], op0=ALU.mult, op1=ALU.add,
                                       accum_out=stat[:, 1:2])
                v.drain()
                v.tensor_tensor(out=stat[:, 6:7], in0=stat[:, 0:1], in1=stat[:, 1:2], op=ALU.add)
                v.drain()
                v.tensor_scalar(out=stat[:, 6:7], in0=stat[:, 6:7], scalar1=1.0 / D,
                                scalar2=None, op0=ALU.mult)
                v.drain().then_inc(s_dve, 1)
                v.wait_ge(s_act, a(ci, 3))
                v.tensor_scalar(out=stat[:, 3:4], in0=stat[:, 2:3], scalar1=1.0 / D,
                                scalar2=None, op0=ALU.mult)
                v.tensor_tensor(out=stat[:, 7:8], in0=stat[:, 6:7], in1=stat[:, 6:7], op=ALU.mult)
                v.drain()
                v.tensor_tensor(out=stat[:, 3:4], in0=stat[:, 3:4], in1=stat[:, 7:8], op=ALU.subtract)
                v.drain()
                v.tensor_scalar(out=stat[:, 3:4], in0=stat[:, 3:4], scalar1=EPS,
                                scalar2=None, op0=ALU.add)
                v.drain()
                v.reciprocal(out=stat[:, 3:4], in_=stat[:, 3:4]).then_inc(s_dve, 1)
                for p in wlq[1]:
                    emit_wlad(p)
                v.wait_ge(s_act, a(ci, 4))
                if ci >= 2:
                    v.wait_ge(s_stz[ci % 2], 16 * (ci // 2))
                zn = z1n[:, (ci % 2) * D:(ci % 2 + 1) * D]
                v.tensor_scalar(out=zn, in0=zpre[:], scalar1=stat[:, 6:7],
                                scalar2=stat[:, 5:6], op0=ALU.subtract, op1=ALU.mult).then_inc(s_dve, 1)
                v.wait_ge(s_act, a(ci, 5))
                if ci >= 1:
                    v.wait_ge(s_pe, m(ci - 1, 6))
                emit_fp4_ladder(
                    v, D, au[:], sg[:], qx_a[:],
                    {"un": un, "lad": lad, "lad2": lad2,
                     "bm": bm, "rb6": rb6, "sc6": sc6},
                    out_wait=lambda ci=ci: v.wait_ge(s_act, a(ci, 6)),
                ).then_inc(s_dve, 1)
                for p in wlq[2]:
                    emit_wlad(p)

        @block.scalar
        def _(sc):
            for ci in range(n_ch):
                sc.wait_ge(s_pe, m(ci, 1))
                sc.copy(out=x_b[:, 0:512], in_=pA[:])
                sc.copy(out=x_b[:, 512:1024], in_=pB[:]).then_inc(s_act, 1)
                sc.wait_ge(s_pe, m(ci, 2))
                if ci >= 1:
                    sc.wait_ge(s_pe, m(ci - 1, 3))
                sc.activation(out=e_b[:], in_=pS[0:K32, :], func=AF.Exp,
                              bias=b32_sb[:], scale=1.0).then_inc(s_act, 1)
                sc.wait_ge(s_dve, d(ci, 4))
                sc.activation(out=sqdump[:], in_=zpre[:], func=AF.Square,
                              accum_out=stat[:, 2:3]).then_inc(s_act, 1)
                sc.wait_ge(s_dve, d(ci, 5))
                sc.activation(out=stat[:, 4:5], in_=stat[:, 3:4], func=AF.Ln)
                sc.drain()
                sc.activation(out=stat[:, 5:6], in_=stat[:, 4:5], func=AF.Exp,
                              scale=0.5).then_inc(s_act, 1)
                sc.wait_ge(s_dve, d(ci, 6))
                if ci >= 1:
                    sc.wait_ge(s_dve, d(ci - 1, 7))
                zn = z1n[:, (ci % 2) * D:(ci % 2 + 1) * D]
                sc.activation(out=au[:], in_=zn, func=AF.Abs).then_inc(s_act, 1)
                sc.activation(out=sg[:], in_=zn, func=AF.Sign).then_inc(s_act, 1)
                sc.wait_ge(s_pe, m(ci, 6))
                if ci >= 1:
                    sc.wait_ge(s_stq, 16 * ci)
                sc.copy(out=qxb_sb[:, 0:512], in_=pQ0[:])
                sc.copy(out=qxb_sb[:, 512:1024], in_=pQ1[:]).then_inc(s_act, 1)
                for p in wpieces_of(ci):
                    kind, r, off, pw = WP[p]
                    sc.wait_ge(s_wl[p % 2], 16 * (p // 2 + 1))
                    if p >= 4:
                        sc.wait_ge(s_wq, p - 3)
                    wlv = wloadw[:, (p % 2) * 1024:(p % 2) * 1024 + pw]
                    sc.activation(out=auw[:, (p % 4) * 1024:(p % 4) * 1024 + pw],
                                  in_=wlv, func=AF.Abs)
                    sc.activation(out=sgw[:, (p % 4) * 1024:(p % 4) * 1024 + pw],
                                  in_=wlv, func=AF.Sign).then_inc(s_wab, 1)

        @block.tensor
        def _(te):
            te.wait_ge(s_c, 32)
            te.wait_ge(s_dve, 2)
            for ci in range(n_ch):
                te.wait_ge(s_x[ci % 3], 16 * (ci // 3 + 1))
                if ci >= 1:
                    te.wait_ge(s_act, a(ci - 1, 1))
                xa = x_a[:, (ci % 3) * D:(ci % 3 + 1) * D]
                for j in range(8):
                    dst = (pA if j < 4 else pB)[:, (j % 4) * 128:(j % 4) * 128 + 128]
                    last = te.transpose(dst, xa[:, j * 128:(j + 1) * 128], ident_f[:])
                last.then_inc(s_pe, 1)
                te.wait_ge(s_act, a(ci, 1))
                if ci >= 1:
                    te.wait_ge(s_dve, d(ci - 1, 3))
                    te.wait_ge(s_act, a(ci - 1, 2))
                for j in range(8):
                    last = te.matmul(pS[0:K32, :], wt_sb[:, j * K32:(j + 1) * K32],
                                     x_b[:, j * 128:(j + 1) * 128],
                                     start=(j == 0), stop=(j == 7))
                last.then_inc(s_pe, 1)
                te.wait_ge(s_act, a(ci, 2))
                te.transpose(pS[:, 0:K32], e_b[:], ident_f[0:K32, 0:K32]).then_inc(s_pe, 1)
                te.wait_ge(s_fr[ci % 2], 16 * (ci // 2 + 1))
                te.wait_ge(s_dve, d(ci, 2))
                bt = bt_sb[:, (ci % 2) * 160:(ci % 2) * 160 + 160]
                te.transpose(pC[:], bt[:, 0:128], ident_f[:])
                te.transpose(pS[0:K32, :], bt[:, 128:160], ident_f[:]).then_inc(s_pe, 1)
                te.wait_ge(s_dve, d(ci, 3))
                if ci >= 1:
                    te.wait_ge(s_dve, d(ci - 1, 4))
                xm = x_ab[:, (ci % 3) * D:(ci % 3 + 1) * D]
                xh = x_ab[0:32, ((ci + 1) % 3) * D:((ci + 1) % 3) * D + D]
                for h in range(2):
                    dst = (pAt0 if h == 0 else pAt1)[:]
                    te.matmul(dst, b1_sb[:], xm[:, h * 512:(h + 1) * 512], start=True, stop=False)
                    last = te.matmul(dst, b2_sb[:], xh[:, h * 512:(h + 1) * 512], start=False, stop=True)
                last.then_inc(s_pe, 1)
                te.wait_ge(s_dve, d(ci, 7))
                if ci >= 1:
                    te.wait_ge(s_act, a(ci - 1, 7))
                for j in range(8):
                    dst = (pQ0 if j < 4 else pQ1)[:, (j % 4) * 128:(j % 4) * 128 + 128]
                    last = te.transpose(dst, qx_a[:, j * 128:(j + 1) * 128], ident_b[:])
                last.then_inc(s_pe, 1)


def build_stageA(T_loc, EX):
    nx = T_loc + EX + 32
    nc = bass.Bass("TRN2", target_bir_lowering=False)
    io = {
        "x_in": nc.dram_tensor("x_in", [nx, D], F32, kind="ExternalInput").ap(),
        "c_WT": nc.dram_tensor("c_WT", [D, K32], F32, kind="ExternalInput").ap(),
        "c_b32": nc.dram_tensor("c_b32", [K32, 1], F32, kind="ExternalInput").ap(),
        "z1_out": nc.dram_tensor("z1_out", [T_loc + EX, D], F32, kind="ExternalOutput").ap(),
        "qxb_out": nc.dram_tensor("qxb_out", [D, T_loc + EX], BF16, kind="ExternalOutput").ap(),
        "flatbuf": nc.dram_tensor("flatbuf", [2, 128 * 161], F32, kind="Internal").ap(),
    }
    emit_stageA(nc, io, T_loc, EX)
    return nc



EX = 128


def shard_tokens(T, cid):
    T_loc = T // 4
    b = cid // 4
    pos = cid % 4
    t0 = pos * T_loc
    return b, t0, T_loc


def stageA_inputs(inp, T, cid):
    b, t0, T_loc = shard_tokens(T, cid)
    x = inp["L0_attn_x"][b]
    lo = t0 - EX
    hi = t0 + T_loc + 32
    xs = np.zeros((T_loc + EX + 32, D), np.float32)
    clo = max(lo, 0)
    chi = min(hi, T)
    xs[clo - lo: chi - lo] = x[clo:chi]
    WT = np.ascontiguousarray(inp["L0_attn_W"].T)
    b32 = np.ascontiguousarray(inp["L0_attn_b"].reshape(K32, 1))
    return {"x_in": xs, "c_WT": WT, "c_b32": b32}


def run_stageA(inp, T, n_cores=8, trace=False):
    T_loc = T // 4
    nc = build_stageA(T_loc, EX)
    in_maps = [stageA_inputs(inp, T, cid) for cid in range(n_cores)]
    res = run_bass_kernel_spmd(nc, in_maps, core_ids=list(range(n_cores)), trace=trace)
    import ml_dtypes
    B = inp["L0_attn_x"].shape[0]
    z1 = np.zeros((B, T, D), np.float32)
    qxb = np.zeros((B, D, T), ml_dtypes.bfloat16)
    for cid in range(n_cores):
        b, t0, T_loc = shard_tokens(T, cid)
        r = res.results[cid]
        z1[b, t0:t0 + T_loc] = r["z1_out"][EX:]
        qxb[b, :, t0:t0 + T_loc] = np.asarray(r["qxb_out"])[:, EX:]
    return z1, qxb, res


class Ev:

    def __init__(self):
        self.n = 0
        self.tags = {}

    def inc(self, tag=None):
        self.n += 1
        if tag is not None:
            assert tag not in self.tags, tag
            self.tags[tag] = self.n
        return self.n

    def at(self, tag):
        return self.tags[tag]




def emit_fp4_scales(v, width, au_ap, scr):
    nb = width // 16
    bm = scr["bm"][:, 0:nb]
    rb6 = scr["rb6"][:, 0:nb]
    sc6 = scr["sc6"][:, 0:nb]
    au3 = au_ap.rearrange("p (nb w) -> p nb w", w=16)
    v.tensor_reduce(out=bm, in_=au3, axis=AX.X, op=ALU.max)
    v.drain()
    v.tensor_scalar(out=bm, in0=bm, scalar1=6e-12, scalar2=None, op0=ALU.max)
    v.drain()
    v.reciprocal(out=rb6, in_=bm)
    v.tensor_scalar(out=sc6, in0=bm, scalar1=1.0 / 6.0, scalar2=None, op0=ALU.mult)
    v.drain()
    v.tensor_scalar(out=rb6, in0=rb6, scalar1=6.0, scalar2=None, op0=ALU.mult)
    v.drain()


def emit_fp4_grid(e, width, c0, au_ap, sg_ap, out_ap, scr, un_ap, lad_ap, lad2_ap,
                  use_stt=True):
    nb0, nb = c0 // 16, width // 16
    un = un_ap
    lad = lad_ap
    lad2 = lad2_ap
    au3 = au_ap.rearrange("p (nb w) -> p nb w", w=16)
    rb63 = scr["rb6"][:, nb0:nb0 + nb].rearrange("p (nb w) -> p nb w", w=1)
    sc63 = scr["sc6"][:, nb0:nb0 + nb].rearrange("p (nb w) -> p nb w", w=1)
    un3 = un.rearrange("p (nb w) -> p nb w", w=16)
    e.tensor_tensor(out=un3, in0=au3, in1=bc(rb63, au3), op=ALU.mult)
    if use_stt:
        e.memset(lad2, 0.0)
        e.drain()
        e.scalar_tensor_tensor(out=lad, in0=un, scalar=FP4_MID[0], in1=lad2,
                               op0=ALU.is_gt, op1=ALU.add)
        e.drain()
        for thr in FP4_MID[1:4]:
            e.scalar_tensor_tensor(out=lad, in0=un, scalar=thr, in1=lad,
                                   op0=ALU.is_gt, op1=ALU.add)
            e.drain()
        e.scalar_tensor_tensor(out=lad, in0=lad, scalar=0.5, in1=lad2,
                               op0=ALU.mult, op1=ALU.add)
        e.drain()
        for thr in (FP4_MID[4], FP4_MID[5], FP4_MID[6], FP4_MID[6]):
            e.scalar_tensor_tensor(out=lad, in0=un, scalar=thr, in1=lad,
                                   op0=ALU.is_gt, op1=ALU.add)
            e.drain()
    else:
        e.drain()
        e.tensor_scalar(out=lad, in0=un, scalar1=FP4_MID[0], scalar2=0.5,
                        op0=ALU.is_gt, op1=ALU.mult)
        e.drain()
        steps = [(FP4_MID[1], 0.5), (FP4_MID[2], 0.5), (FP4_MID[3], 0.5),
                 (FP4_MID[4], None), (FP4_MID[5], None), (FP4_MID[6], 2.0)]
        for thr, w in steps:
            if w is None:
                e.tensor_scalar(out=lad2, in0=un, scalar1=thr, scalar2=None,
                                op0=ALU.is_gt)
            else:
                e.tensor_scalar(out=lad2, in0=un, scalar1=thr, scalar2=w,
                                op0=ALU.is_gt, op1=ALU.mult)
            e.drain()
            e.tensor_tensor(out=lad, in0=lad, in1=lad2, op=ALU.add)
            e.drain()
    lad3 = lad.rearrange("p (nb w) -> p nb w", w=16)
    e.tensor_tensor(out=lad3, in0=lad3, in1=bc(sc63, lad3), op=ALU.mult)
    e.drain()
    e.tensor_tensor(out=out_ap, in0=lad, in1=sg_ap, op=ALU.mult)
    e.drain()


def emit_fp4_ladder(v, width, au_ap, sg_ap, out_ap, scr, out_wait=None):
    nb = width // 16
    un = scr["un"][:, 0:width]
    lad = scr["lad"][:, 0:width]
    lad2 = scr["lad2"][:, 0:width]
    bm = scr["bm"][:, 0:nb]
    rb6 = scr["rb6"][:, 0:nb]
    sc6 = scr["sc6"][:, 0:nb]
    au3 = au_ap.rearrange("p (nb w) -> p nb w", w=16)
    v.tensor_reduce(out=bm, in_=au3, axis=AX.X, op=ALU.max)
    v.drain()
    v.tensor_scalar(out=bm, in0=bm, scalar1=6e-12, scalar2=None, op0=ALU.max)
    v.drain()
    v.reciprocal(out=rb6, in_=bm)
    v.tensor_scalar(out=sc6, in0=bm, scalar1=1.0 / 6.0, scalar2=None, op0=ALU.mult)
    v.drain()
    v.tensor_scalar(out=rb6, in0=rb6, scalar1=6.0, scalar2=None, op0=ALU.mult)
    v.drain()
    un3 = un.rearrange("p (nb w) -> p nb w", w=16)
    rb63 = rb6.rearrange("p (nb w) -> p nb w", w=1)
    v.tensor_tensor(out=un3, in0=au3, in1=bc(rb63, au3), op=ALU.mult)
    v.drain()
    v.tensor_scalar(out=lad, in0=un, scalar1=FP4_MID[0], scalar2=None, op0=ALU.is_gt)
    v.tensor_scalar(out=lad2, in0=un, scalar1=FP4_MID[4], scalar2=None, op0=ALU.is_gt)
    v.drain()
    for ta, tb in ((FP4_MID[1], FP4_MID[5]), (FP4_MID[2], FP4_MID[6]),
                   (FP4_MID[3], FP4_MID[6])):
        v.scalar_tensor_tensor(out=lad, in0=un, scalar=ta, in1=lad,
                               op0=ALU.is_gt, op1=ALU.add)
        v.scalar_tensor_tensor(out=lad2, in0=un, scalar=tb, in1=lad2,
                               op0=ALU.is_gt, op1=ALU.add)
        v.drain()
    v.scalar_tensor_tensor(out=lad, in0=lad, scalar=0.5, in1=lad2,
                           op0=ALU.mult, op1=ALU.add)
    v.drain()
    lad3 = lad.rearrange("p (nb w) -> p nb w", w=16)
    sc63 = sc6.rearrange("p (nb w) -> p nb w", w=1)
    v.tensor_tensor(out=lad3, in0=lad3, in1=bc(sc63, lad3), op=ALU.mult)
    v.drain()
    if out_wait is not None:
        out_wait()
    return v.tensor_tensor(out=out_ap, in0=lad, in1=sg_ap, op=ALU.mult)

class _MockIns:
    def __init__(self, book):
        self._book = book

    def then_inc(self, sem, n=1):
        self._book._bump(sem, n)
        return self


class _MockEng:
    def __init__(self, book):
        self._book = book

    def __getattr__(self, name):
        book = self._book

        def f(*a, **k):
            return _MockIns(book)
        return f


class Book:

    def __init__(self):
        self.counts = {}
        self.tags = {}
        self.pass2 = False

    def _bump(self, sem, n):
        if not self.pass2:
            self.counts[id(sem)] = self.counts.get(id(sem), 0) + n

    def tag(self, sem, name):
        if not self.pass2:
            assert name not in self.tags, name
            self.tags[name] = self.counts.get(id(sem), 0)

    def at(self, name):
        if not self.pass2:
            return 0
        return self.tags[name]

    def total(self, sem):
        return self.counts.get(id(sem), 0)


def two_pass(block, engines, builders):
    B = Book()
    for name, fn in builders:
        fn(_MockEng(B), B)
    B.pass2 = True
    for name, fn in builders:
        dec = getattr(block, name)
        dec(lambda eng, fn=fn: fn(eng, B))
    return B


def emit_stageB(nc, io, T_loc, EX, sst=None, wio=None):
    DF = 2816
    NJT = DF // 128
    SL = [512, 512, 512, 512, 512, 256]
    SOFF = [0, 512, 1024, 1536, 2048, 2560]
    n_ch = (T_loc + EX) // 128
    z1_in = io["z1_in"]
    qxb_in = io["qxb_in"]
    w_gate = io["w_gate"]
    w_up = io["w_up"]
    w_down = io["w_down"]
    x2_out = io["x2_out"]

    jobs = ([("g", r, 1024) for r in range(NJT)]
            + [("u", r, 1024) for r in range(NJT)]
            + [("d", r, DF) for r in range(8)])
    WP = _weight_pieces() if wio is not None else []

    def pieces(wd):
        return [(i * 1024, min(1024, wd - i * 1024)) for i in range((wd + 1023) // 1024)]

    def nrounds(wd):
        return (wd // 128 + 7) // 8

    with ExitStack() as st:
        def sb(name, shape, dt=F32):
            return st.enter_context(nc.sbuf_tensor(_tname(name), shape, dt))

        def psum(name, shape, dt=F32):
            return st.enter_context(nc.psum_tensor(_tname(name), shape, dt))

        wgT = sb("wgT", [128, 8 * DF], BF16)
        wuT = sb("wuT", [128, 8 * DF], BF16)
        wdT = sb("wdT", [128, NJT * D], BF16)
        if wio is None:
            wload = sb("wload", [128, DF])
            qw = sb("qw", [128, DF], BF16)
        else:
            qw = sb("qw", [128, 2 * 1024], BF16)
            s_w2 = [(sst if sst is not None else st).enter_context(
                nc.semaphore(_semname())) for _ in range(2)]
            qwhbm = wio["qwhbm"]
        qxc = sb("qxc", [128, 2 * D], BF16)
        z1c = sb("z1c", [128, D])
        tt_sb = sb("tt_sb", [128, 512])
        u2_sb = sb("u2_sb", [128, 512])
        h_sb = sb("h_sb", [128, 1024])
        auq = sb("auq", [128, 1024])
        sgq = sb("sgq", [128, 1024])
        unq = sb("unq", [128, 1024])
        ladq = sb("ladq", [128, 1024])
        lad2q = sb("lad2q", [128, 1024])
        bmq = sb("bmq", [128, 64])
        rb6q = sb("rb6q", [128, 64])
        sc6q = sb("sc6q", [128, 64])
        hq_a = sb("hq_a", [128, DF], BF16)
        hq_b = sb("hq_b", [128, NJT * 128], BF16)
        zpre2 = sb("zpre2", [128, D])
        x2n = sb("x2n", [128, D])
        stat2 = sb("stat2", [128, 16])
        ident_b2 = sb("ident_b2", [128, 128], BF16)
        scr = dict(un=unq, lad=ladq, lad2=lad2q, bm=bmq, rb6=rb6q, sc6=sc6q)

        pG_ = [psum("pG", [128, 512]), psum("pGb", [128, 512])]
        pU_ = [psum("pU", [128, 512]), psum("pUb", [128, 512])]
        pD0 = psum("pD0", [128, 512])
        pD1 = psum("pD1", [128, 512])
        pT0 = psum("pT0", [128, 512], BF16)
        pT1 = psum("pT1", [128, 512], BF16)

        s_w = (sst if sst is not None else st).enter_context(nc.semaphore(_semname()))
        s_q = [(sst if sst is not None else st).enter_context(nc.semaphore(_semname())) for i in range(2)]
        s_z = (sst if sst is not None else st).enter_context(nc.semaphore(_semname()))
        s_pe = (sst if sst is not None else st).enter_context(nc.semaphore(_semname()))
        s_act = (sst if sst is not None else st).enter_context(nc.semaphore(_semname()))
        s_dve = (sst if sst is not None else st).enter_context(nc.semaphore(_semname()))
        s_st = (sst if sst is not None else st).enter_context(nc.semaphore(_semname()))
        s_pool = (sst if sst is not None else st).enter_context(nc.semaphore(_semname()))
        block = st.enter_context(nc.Block())

        def wslot(wi):
            return 0

        def dsplit(w):
            wd = (int(w * 0.64) // 16) * 16
            return wd, w - wd

        def wdest(kind, r, j):
            if kind == "g":
                return wgT[:, j * DF + r * 128: j * DF + r * 128 + 128]
            if kind == "u":
                return wuT[:, j * DF + r * 128: j * DF + r * 128 + 128]
            return wdT[:, j * D + r * 128: j * D + r * 128 + 128]

        def f_sync(sync, B):
            if wio is None:
                for wi, (kind, r, wd) in enumerate(jobs):
                    if wi >= 1:
                        sync.wait_ge(s_dve, B.at(("wq", wi - 1)))
                    srcw = {"g": w_gate, "u": w_up, "d": w_down}[kind][r * 128:(r + 1) * 128, :]
                    sync.dma_start(out=wload[:, wslot(wi):wslot(wi) + wd],
                                   in_=srcw).then_inc(s_w, 16)
            else:
                for p, (kind, r, off, pw) in enumerate(WP):
                    if p >= 2:
                        sync.wait_ge(s_pe, B.at(("wtr", p - 2)))
                    sync.dma_start(out=qw[:, (p % 2) * 1024:(p % 2) * 1024 + pw],
                                   in_=qwhbm[p, :, 0:pw]).then_inc(s_w2[p % 2], 16)
            for ci in range(n_ch):
                if ci >= 1:
                    sync.wait_ge(s_dve, B.at(("zpre", ci - 1)))
                sync.dma_start(out=z1c[:],
                               in_=z1_in[ci * 128:(ci + 1) * 128, :]).then_inc(s_z, 16)
                cj = ci - 1
                if cj >= 0:
                    sync.wait_ge(s_dve, B.at(("x2n", cj)))
                    sync.dma_start(out=x2_out[cj * 128:(cj + 1) * 128, :],
                                   in_=x2n[:]).then_inc(s_st, 16)
            cj = n_ch - 1
            sync.wait_ge(s_dve, B.at(("x2n", cj)))
            sync.dma_start(out=x2_out[cj * 128:(cj + 1) * 128, :],
                           in_=x2n[:]).then_inc(s_st, 16)
            sync.wait_ge(s_st, 16 * n_ch)

        def f_gpsimd(g, B):
            g.iota(unq[:, 0:128], [[1, 128]], base=0, channel_multiplier=-1,
                   allow_small_or_imprecise_dtypes=True)
            g.drain().then_inc(s_dve, 1)
            B.tag(s_dve, "iota")
            for ci in range(n_ch):
                if ci >= 2:
                    g.wait_ge(s_pe, B.at(("gu", ci - 2, len(SL) - 1)))
                g.dma_start(
                    out=qxc[:, (ci % 2) * D:(ci % 2 + 1) * D].rearrange("p (j t) -> p j t", t=128),
                    in_=qxb_in[:, ci * 128:(ci + 1) * 128].rearrange("(j p) t -> p j t", p=128),
                ).then_inc(s_q[ci % 2], 16)

        def f_vector(v, B):
            v.wait_ge(s_dve, B.at("iota"))
            v.tensor_scalar(out=unq[:, 128:256], in0=unq[:, 0:128], scalar1=0.0,
                            scalar2=None, op0=ALU.is_equal)
            v.drain()
            v.tensor_copy(ident_b2[:], unq[:, 128:256])
            v.drain().then_inc(s_dve, 1)
            B.tag(s_dve, "ident")
            if wio is None:
                for wi, (kind, r, wd) in enumerate(jobs):
                    v.wait_ge(s_w, 16 * (wi + 1))
                    if wi >= 1:
                        v.wait_ge(s_pe, B.at(("wt", wi - 1)))
                    for pi, (off, pw) in enumerate(pieces(wd)):
                        v.wait_ge(s_act, B.at(("wabs", wi, pi)))
                        emit_fp4_ladder(v, pw, auq[:, 0:pw], sgq[:, 0:pw],
                                        qw[:, off:off + pw], scr).then_inc(s_dve, 1)
                        B.tag(s_dve, ("wqp", wi, pi))
                    B.tag(s_dve, ("wq", wi))
            for ci in range(n_ch):
                for si, sw in enumerate(SL):
                    v.wait_ge(s_pe, B.at(("gu", ci, si)))
                    v.wait_ge(s_act, B.at(("tanh", ci, si)))
                    hof = (si % 2) * 512
                    v.scalar_tensor_tensor(out=u2_sb[:, 0:sw], in0=tt_sb[:, 0:sw],
                                           scalar=1.0, in1=u2_sb[:, 0:sw],
                                           op0=ALU.add, op1=ALU.mult)
                    v.drain()
                    v.tensor_tensor(out=h_sb[:, hof:hof + sw], in0=u2_sb[:, 0:sw],
                                    in1=pG_[si % 2][:, 0:sw], op=ALU.mult).then_inc(s_dve, 1)
                    B.tag(s_dve, ("h", ci, si))
                    if si % 2 == 1:
                        g_ = si // 2
                        gw = SL[si - 1] + sw
                        v.wait_ge(s_act, B.at(("hasg", ci, g_)))
                        if ci >= 1:
                            v.wait_ge(s_pe, B.at(("ht", ci - 1, g_)))
                        emit_fp4_ladder(v, gw, auq[:, 0:gw], sgq[:, 0:gw],
                                        hq_a[:, SOFF[si - 1]:SOFF[si - 1] + gw], scr).then_inc(s_dve, 1)
                        B.tag(s_dve, ("hqg", ci, g_))
                v.wait_ge(s_pe, B.at(("down", ci)))
                v.wait_ge(s_z, 16 * (ci + 1))
                if ci >= 1:
                    v.wait_ge(s_act, B.at(("sq", ci - 1)))
                z1v = z1c[:]
                v.scalar_tensor_tensor(out=zpre2[:, 0:512], in0=pD0[:], scalar=1.0,
                                       in1=z1v[:, 0:512], op0=ALU.mult, op1=ALU.add,
                                       accum_out=stat2[:, 0:1])
                v.scalar_tensor_tensor(out=zpre2[:, 512:1024], in0=pD1[:], scalar=1.0,
                                       in1=z1v[:, 512:1024], op0=ALU.mult, op1=ALU.add,
                                       accum_out=stat2[:, 1:2])
                v.drain()
                v.tensor_tensor(out=stat2[:, 6:7], in0=stat2[:, 0:1], in1=stat2[:, 1:2], op=ALU.add)
                v.drain()
                v.tensor_scalar(out=stat2[:, 6:7], in0=stat2[:, 6:7], scalar1=1.0 / D,
                                scalar2=None, op0=ALU.mult)
                v.drain().then_inc(s_dve, 1)
                B.tag(s_dve, ("zpre", ci))
                v.wait_ge(s_act, B.at(("sq", ci)))
                v.tensor_scalar(out=stat2[:, 3:4], in0=stat2[:, 2:3], scalar1=1.0 / D,
                                scalar2=None, op0=ALU.mult)
                v.tensor_tensor(out=stat2[:, 7:8], in0=stat2[:, 6:7], in1=stat2[:, 6:7], op=ALU.mult)
                v.drain()
                v.tensor_tensor(out=stat2[:, 3:4], in0=stat2[:, 3:4], in1=stat2[:, 7:8], op=ALU.subtract)
                v.drain()
                v.tensor_scalar(out=stat2[:, 3:4], in0=stat2[:, 3:4], scalar1=EPS,
                                scalar2=None, op0=ALU.add)
                v.drain()
                v.reciprocal(out=stat2[:, 3:4], in_=stat2[:, 3:4]).then_inc(s_dve, 1)
                B.tag(s_dve, ("var", ci))
                v.wait_ge(s_act, B.at(("rstd", ci)))
                if ci >= 1:
                    v.wait_ge(s_st, 16 * ci)
                v.tensor_scalar(out=x2n[:], in0=zpre2[:],
                                scalar1=stat2[:, 6:7], scalar2=stat2[:, 5:6],
                                op0=ALU.subtract, op1=ALU.mult).then_inc(s_dve, 1)
                B.tag(s_dve, ("x2n", ci))

        def f_scalar(sc, B):
            if wio is None:
                for wi, (kind, r, wd) in enumerate(jobs):
                    sc.wait_ge(s_w, 16 * (wi + 1))
                    for pi, (off, pw) in enumerate(pieces(wd)):
                        if pi >= 1:
                            sc.wait_ge(s_dve, B.at(("wqp", wi, pi - 1)))
                        elif wi >= 1:
                            sc.wait_ge(s_dve, B.at(("wq", wi - 1)))
                        src_ap = wload[:, wslot(wi) + off:wslot(wi) + off + pw]
                        sc.activation(out=auq[:, 0:pw], in_=src_ap, func=AF.Abs)
                        sc.activation(out=sgq[:, 0:pw], in_=src_ap, func=AF.Sign).then_inc(s_act, 1)
                        B.tag(s_act, ("wabs", wi, pi))
                    for rd in range(nrounds(wd)):
                        sc.wait_ge(s_pe, B.at(("wtr", wi, rd)))
                        ji = list(range(rd * 8, min(rd * 8 + 8, wd // 128)))
                        for j in ji:
                            bank = pT0 if (j % 8) < 4 else pT1
                            slot = (j % 4) * 128
                            ins = sc.copy(out=wdest(kind, r, j), in_=bank[:, slot:slot + 128])
                            if j == ji[-1]:
                                ins.then_inc(s_act, 1)
                        B.tag(s_act, ("wev", wi, rd))
            else:
                for p, (kind, r, off, pw) in enumerate(WP):
                    sc.wait_ge(s_pe, B.at(("wtr", p)))
                    ji = list(range(pw // 128))
                    for jl in ji:
                        bank = pT0 if jl < 4 else pT1
                        slot = (jl % 4) * 128
                        ins = sc.copy(out=wdest(kind, r, off // 128 + jl),
                                      in_=bank[:, slot:slot + 128])
                        if jl == ji[-1]:
                            ins.then_inc(s_act, 1)
                    B.tag(s_act, ("wev", p))
            for ci in range(n_ch):
                for si, sw in enumerate(SL):
                    sc.wait_ge(s_pe, B.at(("gu", ci, si)))
                    t = ci * len(SL) + si
                    if t >= 1:
                        p1 = ((t - 1) // len(SL), (t - 1) % len(SL))
                        sc.wait_ge(s_dve, B.at(("h",) + p1))
                    sc.activation(out=tt_sb[:, 0:sw], in_=pG_[si % 2][:, 0:sw],
                                  func=AF.Tanh, scale=0.5)
                    sc.activation(out=u2_sb[:, 0:sw], in_=pU_[si % 2][:, 0:sw],
                                  func=AF.Identity, scale=0.5).then_inc(s_act, 1)
                    B.tag(s_act, ("tanh", ci, si))
                    if si % 2 == 1:
                        g_ = si // 2
                        gw = SL[si - 1] + sw
                        sc.wait_ge(s_dve, B.at(("h", ci, si)))
                        if (ci, g_) != (0, 0):
                            prev = (ci, g_ - 1) if g_ > 0 else (ci - 1, 2)
                            sc.wait_ge(s_dve, B.at(("hqg",) + prev))
                        elif wio is None:
                            sc.wait_ge(s_dve, B.at(("wq", len(jobs) - 1)))
                        sc.activation(out=auq[:, 0:gw], in_=h_sb[:, 0:gw], func=AF.Abs)
                        sc.activation(out=sgq[:, 0:gw], in_=h_sb[:, 0:gw], func=AF.Sign).then_inc(s_act, 1)
                        B.tag(s_act, ("hasg", ci, g_))
                for rd in range(3):
                    sc.wait_ge(s_pe, B.at(("ht", ci, rd)))
                    if ci >= 1 and rd == 0:
                        sc.wait_ge(s_pe, B.at(("down", ci - 1)))
                    ji = list(range(rd * 8, min(rd * 8 + 8, NJT)))
                    for j in ji:
                        bank = pT0 if (j % 8) < 4 else pT1
                        slot = (j % 4) * 128
                        ins = sc.copy(out=hq_b[:, j * 128:(j + 1) * 128], in_=bank[:, slot:slot + 128])
                        if j == ji[-1]:
                            ins.then_inc(s_act, 1)
                    B.tag(s_act, ("hev", ci, rd))
                sc.wait_ge(s_dve, B.at(("zpre", ci)))
                sc.activation(out=lad2q[:, 0:D], in_=zpre2[:], func=AF.Square,
                              accum_out=stat2[:, 2:3]).then_inc(s_act, 1)
                B.tag(s_act, ("sq", ci))
                sc.wait_ge(s_dve, B.at(("var", ci)))
                sc.activation(out=stat2[:, 4:5], in_=stat2[:, 3:4], func=AF.Ln)
                sc.drain()
                sc.activation(out=stat2[:, 5:6], in_=stat2[:, 4:5], func=AF.Exp,
                              scale=0.5).then_inc(s_act, 1)
                B.tag(s_act, ("rstd", ci))

        def f_tensor(te, B):
            te.wait_ge(s_dve, B.at("ident"))
            prev_ev = None
            if wio is None:
                for wi, (kind, r, wd) in enumerate(jobs):
                    njt = wd // 128
                    te.wait_ge(s_dve, B.at(("wq", wi)))
                    for rd in range(nrounds(wd)):
                        if prev_ev is not None:
                            te.wait_ge(s_act, B.at(prev_ev))
                        last = None
                        for j in range(rd * 8, min(rd * 8 + 8, njt)):
                            bank = pT0 if (j % 8) < 4 else pT1
                            slot = (j % 4) * 128
                            last = te.transpose(bank[:, slot:slot + 128],
                                                qw[:, j * 128:(j + 1) * 128], ident_b2[:])
                        last.then_inc(s_pe, 1)
                        B.tag(s_pe, ("wtr", wi, rd))
                        prev_ev = ("wev", wi, rd)
                    B.tag(s_pe, ("wt", wi))
            else:
                for p, (kind, r, off, pw) in enumerate(WP):
                    te.wait_ge(s_w2[p % 2], 16 * (p // 2 + 1))
                    if prev_ev is not None:
                        te.wait_ge(s_act, B.at(prev_ev))
                    last = None
                    for jl in range(pw // 128):
                        bank = pT0 if jl < 4 else pT1
                        slot = (jl % 4) * 128
                        last = te.transpose(
                            bank[:, slot:slot + 128],
                            qw[:, (p % 2) * 1024 + jl * 128:(p % 2) * 1024 + (jl + 1) * 128],
                            ident_b2[:])
                    last.then_inc(s_pe, 1)
                    B.tag(s_pe, ("wtr", p))
                    prev_ev = ("wev", p)
            for ci in range(n_ch):
                for si, sw in enumerate(SL):
                    te.wait_ge(s_q[ci % 2], 16 * (ci // 2 + 1))
                    t = ci * len(SL) + si
                    if t >= 2:
                        p2 = ((t - 2) // len(SL), (t - 2) % len(SL))
                        te.wait_ge(s_dve, B.at(("h",) + p2))
                    for j in range(8):
                        lhs = qxc[:, (ci % 2) * D + j * 128:(ci % 2) * D + (j + 1) * 128]
                        te.matmul(pG_[si % 2][:, 0:sw], lhs,
                                  wgT[:, j * DF + SOFF[si]:j * DF + SOFF[si] + sw],
                                  start=(j == 0), stop=(j == 7))
                    last = None
                    for j in range(8):
                        lhs = qxc[:, (ci % 2) * D + j * 128:(ci % 2) * D + (j + 1) * 128]
                        last = te.matmul(pU_[si % 2][:, 0:sw], lhs,
                                         wuT[:, j * DF + SOFF[si]:j * DF + SOFF[si] + sw],
                                         start=(j == 0), stop=(j == 7))
                    last.then_inc(s_pe, 1)
                    B.tag(s_pe, ("gu", ci, si))
                for rd in range(3):
                    te.wait_ge(s_dve, B.at(("hqg", ci, rd)))
                    te.wait_ge(s_act, B.at(prev_ev))
                    last = None
                    for j in range(rd * 8, min(rd * 8 + 8, NJT)):
                        bank = pT0 if (j % 8) < 4 else pT1
                        slot = (j % 4) * 128
                        last = te.transpose(bank[:, slot:slot + 128],
                                            hq_a[:, j * 128:(j + 1) * 128], ident_b2[:])
                    last.then_inc(s_pe, 1)
                    B.tag(s_pe, ("ht", ci, rd))
                    prev_ev = ("hev", ci, rd)
                te.wait_ge(s_act, B.at(("hev", ci, 2)))
                if ci >= 1:
                    te.wait_ge(s_dve, B.at(("zpre", ci - 1)))
                last = None
                for half in range(2):
                    for j in range(NJT):
                        lhs = hq_b[:, j * 128:(j + 1) * 128]
                        last = te.matmul((pD0 if half == 0 else pD1)[:], lhs,
                                         wdT[:, j * D + half * 512:j * D + half * 512 + 512],
                                         start=(j == 0), stop=(j == NJT - 1))
                last.then_inc(s_pe, 1)
                B.tag(s_pe, ("down", ci))

        two_pass(block, None, [("sync", f_sync), ("gpsimd", f_gpsimd),
                               ("vector", f_vector), ("scalar", f_scalar),
                               ("tensor", f_tensor)])


def build_stageB(T_loc, EX):
    DF = 2816
    nc = bass.Bass("TRN2", target_bir_lowering=False)
    io = {
        "z1_in": nc.dram_tensor("z1_in", [T_loc + EX, D], F32, kind="ExternalInput").ap(),
        "qxb_in": nc.dram_tensor("qxb_in", [D, T_loc + EX], BF16, kind="ExternalInput").ap(),
        "w_gate": nc.dram_tensor("w_gate", [DF, D], F32, kind="ExternalInput").ap(),
        "w_up": nc.dram_tensor("w_up", [DF, D], F32, kind="ExternalInput").ap(),
        "w_down": nc.dram_tensor("w_down", [D, DF], F32, kind="ExternalInput").ap(),
        "x2_out": nc.dram_tensor("x2_out", [T_loc + EX, D], F32, kind="ExternalOutput").ap(),
    }
    emit_stageB(nc, io, T_loc, EX)
    return nc


def stageB_inputs(inp, z1_glob, qxb_glob, T, cid):
    b, t0, T_loc = shard_tokens(T, cid)
    lo = t0 - EX
    z1s = np.zeros((T_loc + EX, D), np.float32)
    clo = max(lo, 0)
    z1s[clo - lo:] = z1_glob[b, clo:t0 + T_loc]
    qs = np.zeros((D, T_loc + EX), qxb_glob.dtype)
    qs[:, clo - lo:] = qxb_glob[b, :, clo:t0 + T_loc]
    return {"z1_in": z1s, "qxb_in": qs,
            "w_gate": inp["L0_ffn_w_gate"], "w_up": inp["L0_ffn_w_up"],
            "w_down": inp["L0_ffn_w_down"]}


def chunk_list(T_loc, EX, W=512):
    out = []
    if EX:
        out.append((0, EX))
    for s in range(EX, T_loc + EX, W):
        out.append((s, min(W, T_loc + EX - s)))
    return out


def emit_stageC1(nc, io, T_loc, EX, sst=None):
    CH = chunk_list(T_loc, EX)
    NCH = len(CH)
    x2_in = io["x2_in"]
    WkT = io["WkT"]
    WvT = io["WvT"]
    WrT = io["WrT"]
    c5 = io["c5"]
    bmask = io["bmask"]
    k_out = io["k_out"]
    v_out = io["v_out"]
    sr_out = io["sr_out"]

    with ExitStack() as st:
        def sb(name, shape, dt=F32):
            return st.enter_context(nc.sbuf_tensor(_tname(name), shape, dt))

        def psum(name, shape, dt=F32):
            return st.enter_context(nc.psum_tensor(_tname(name), shape, dt))

        x2c = sb("x2c", [128, 2 * 4 * D])
        hn = sb("hn", [128, 4 * D])
        hb = sb("hb", [128, 8 * 512])
        dcol = sb("dcol", [128, 8 * 512])
        xkb = sb("xkb", [128, 8 * 512], BF16)
        xvb = sb("xvb", [128, 8 * 512], BF16)
        xrb = sb("xrb", [128, 8 * 512], BF16)
        wk_sb = sb("wk_sb", [128, 8 * D], BF16)
        wv_sb = sb("wv_sb", [128, 8 * D], BF16)
        wr_sb = sb("wr_sb", [128, 8 * D], BF16)
        cv = sb("cv", [128, 40])
        maskc = sb("maskc", [128, 1])
        mcol = sb("mcol", [128, 8])
        stat3 = sb("stat3", [128, 16])
        dump3 = sb("dump3", [128, D])
        hprev = sb("hprev", [128, 8])
        st_k = sb("st_k", [128, 8 * 512])
        st_v = sb("st_v", [128, 8 * 512])
        st_s = sb("st_s", [128, 8 * 512], BF16)
        tt3 = sb("tt3", [128, 512])
        ident3 = sb("ident3", [128, 128])

        pX0 = psum("pX0", [128, 512])
        pX1 = psum("pX1", [128, 512])
        pK = psum("pK", [128, 512])
        pV = psum("pV", [128, 512])
        pR = psum("pR", [128, 512])

        s_x = [(sst if sst is not None else st).enter_context(nc.semaphore(_semname())) for i in range(2)]
        s_c = (sst if sst is not None else st).enter_context(nc.semaphore(_semname()))
        s_pe = (sst if sst is not None else st).enter_context(nc.semaphore(_semname()))
        s_act = (sst if sst is not None else st).enter_context(nc.semaphore(_semname()))
        s_dve = (sst if sst is not None else st).enter_context(nc.semaphore(_semname()))
        s_stk = (sst if sst is not None else st).enter_context(nc.semaphore(_semname()))
        s_stv = (sst if sst is not None else st).enter_context(nc.semaphore(_semname()))
        s_str = (sst if sst is not None else st).enter_context(nc.semaphore(_semname()))
        block = st.enter_context(nc.Block())

        def emit_store(sync, B, cj):
            cjs, cjw = CH[cj]
            sync.wait_ge(s_act, B.at(("ev", cj, 7)))
            sync.wait_ge(s_dve, B.at(("sr", cj, 7)))
            sync.dma_start(out=k_out[:, cjs:cjs + cjw].rearrange("(j p) t -> p j t", p=128),
                           in_=st_k[:, 0:8 * cjw].rearrange("p (j t) -> p j t", t=cjw)
                           ).then_inc(s_stk, 16)
            sync.dma_start(out=v_out[:, cjs:cjs + cjw].rearrange("(j p) t -> p j t", p=128),
                           in_=st_v[:, 0:8 * cjw].rearrange("p (j t) -> p j t", t=cjw)
                           ).then_inc(s_stv, 16)
            sync.dma_start(out=sr_out[:, cjs:cjs + cjw].rearrange("(j p) t -> p j t", p=128),
                           in_=st_s[:, 0:8 * cjw].rearrange("p (j t) -> p j t", t=cjw)
                           ).then_inc(s_str, 16)

        def f_sync(sync, B):
            for ci, (cs, cw) in enumerate(CH):
                if ci >= 2:
                    sync.wait_ge(s_dve, B.at(("hn", ci - 2)))
                sync.dma_start(out=x2c[:, (ci % 2) * 4 * D:(ci % 2) * 4 * D + (cw // 128) * D]
                               .rearrange("p (s c) -> p s c", c=D),
                               in_=x2_in[cs:cs + cw, :].rearrange("(s p) c -> p s c", p=128)
                               ).then_inc(s_x[ci % 2], 16)
                if ci >= 1:
                    emit_store(sync, B, ci - 1)
            emit_store(sync, B, NCH - 1)
            sync.wait_ge(s_stk, 16 * NCH)
            sync.wait_ge(s_stv, 16 * NCH)
            sync.wait_ge(s_str, 16 * NCH)

        def f_gpsimd(g, B):
            g.dma_start(out=wk_sb[:].rearrange("p (j c) -> p j c", c=D), in_=WkT.rearrange("j p c -> p j c")).then_inc(s_c, 16)
            g.dma_start(out=wv_sb[:].rearrange("p (j c) -> p j c", c=D), in_=WvT.rearrange("j p c -> p j c")).then_inc(s_c, 16)
            g.dma_start(out=wr_sb[:].rearrange("p (j c) -> p j c", c=D), in_=WrT.rearrange("j p c -> p j c")).then_inc(s_c, 16)
            g.dma_start(out=cv[:], in_=c5).then_inc(s_c, 16)
            g.dma_start(out=maskc[:], in_=bmask).then_inc(s_c, 16)
            g.iota(dump3[:, 0:128], [[1, 128]], base=0, channel_multiplier=-1,
                   allow_small_or_imprecise_dtypes=True)
            g.memset(hprev[:], 0.0)
            g.drain().then_inc(s_dve, 1)
            B.tag(s_dve, "init")

        def f_vector(v, B):
            v.wait_ge(s_dve, B.at("init"))
            v.wait_ge(s_c, 16 * 5)
            v.tensor_scalar(out=ident3[:], in0=dump3[:, 0:128], scalar1=0.0,
                            scalar2=None, op0=ALU.is_equal)
            v.drain().then_inc(s_dve, 1)
            B.tag(s_dve, "ident")
            for ci, (cs, cw) in enumerate(CH):
                ns = cw // 128
                xoff = (ci % 2) * 4 * D
                v.wait_ge(s_x[ci % 2], 16 * (ci // 2 + 1))
                for s in range(ns):
                    xa = x2c[:, xoff + s * D:xoff + (s + 1) * D]
                    v.tensor_reduce(out=mcol[:, s:s + 1], in_=xa, axis=AX.X, op=ALU.add)
                v.drain()
                v.tensor_scalar(out=mcol[:, 0:ns], in0=mcol[:, 0:ns], scalar1=1.0 / D,
                                scalar2=None, op0=ALU.mult)
                v.drain().then_inc(s_dve, 1)
                B.tag(s_dve, ("mean", ci))
                v.wait_ge(s_act, B.at(("sq", ci)))
                v.tensor_scalar(out=stat3[:, 0:ns], in0=stat3[:, 8:8 + ns], scalar1=1.0 / D,
                                scalar2=None, op0=ALU.mult)
                v.tensor_tensor(out=stat3[:, 4:4 + ns], in0=mcol[:, 0:ns], in1=mcol[:, 0:ns],
                                op=ALU.mult)
                v.drain()
                v.tensor_tensor(out=stat3[:, 0:ns], in0=stat3[:, 0:ns], in1=stat3[:, 4:4 + ns],
                                op=ALU.subtract)
                v.drain()
                v.tensor_scalar(out=stat3[:, 0:ns], in0=stat3[:, 0:ns], scalar1=EPS,
                                scalar2=None, op0=ALU.add)
                v.drain()
                v.reciprocal(out=stat3[:, 0:ns], in_=stat3[:, 0:ns])
                v.drain().then_inc(s_dve, 1)
                B.tag(s_dve, ("var", ci))
                v.wait_ge(s_act, B.at(("rstd", ci)))
                if ci >= 1:
                    v.wait_ge(s_pe, B.at(("xT", ci - 1, CH[ci - 1][1] // 128 - 1)))
                for s in range(ns):
                    xa = x2c[:, xoff + s * D:xoff + (s + 1) * D]
                    v.tensor_scalar(out=hn[:, s * D:(s + 1) * D], in0=xa,
                                    scalar1=mcol[:, s:s + 1], scalar2=stat3[:, 4 + s:5 + s],
                                    op0=ALU.subtract, op1=ALU.mult)
                v.drain().then_inc(s_dve, 1)
                B.tag(s_dve, ("hn", ci))
                v.wait_ge(s_act, B.at(("gb", ci, ns - 1)))
                if ci >= 1:
                    v.wait_ge(s_pe, B.at(("kvr", ci - 1, 7)))
                for j in range(8):
                    hj = hb[:, j * cw:(j + 1) * cw]
                    dj = dcol[:, j * cw:(j + 1) * cw]
                    v.tensor_tensor(out=dj[:, 1:cw], in0=hj[:, 1:cw], in1=hj[:, 0:cw - 1],
                                    op=ALU.subtract)
                    v.tensor_tensor(out=dj[:, 0:1], in0=hj[:, 0:1], in1=hprev[:, j:j + 1],
                                    op=ALU.subtract)
                v.drain()
                for mi, outb in [(2, xkb), (3, xvb), (4, xrb)]:
                    for j in range(8):
                        hj = hb[:, j * cw:(j + 1) * cw]
                        dj = dcol[:, j * cw:(j + 1) * cw]
                        ob = outb[:, j * cw:(j + 1) * cw]
                        tm = cv[:, mi * 8 + j:mi * 8 + j + 1]
                        v.scalar_tensor_tensor(out=ob[:, 1:cw], in0=dj[:, 1:cw], scalar=tm,
                                               in1=hj[:, 0:cw - 1], op0=ALU.mult, op1=ALU.add)
                        v.scalar_tensor_tensor(out=ob[:, 0:1], in0=dj[:, 0:1], scalar=tm,
                                               in1=hprev[:, j:j + 1], op0=ALU.mult, op1=ALU.add)
                v.drain()
                for j in range(8):
                    v.tensor_copy(hprev[:, j:j + 1], hb[:, j * cw + cw - 1:j * cw + cw])
                v.drain()
                if ci == 0 and EX:
                    v.tensor_scalar(out=hprev[:], in0=hprev[:], scalar1=maskc[:],
                                    scalar2=None, op0=ALU.mult)
                    v.drain()
                v.engine_nop().then_inc(s_dve, 1)
                B.tag(s_dve, ("mix", ci))
                for dj in range(8):
                    v.wait_ge(s_act, B.at(("ev", ci, dj)))
                    if ci >= 1 and dj == 0:
                        v.wait_ge(s_str, 16 * ci)
                    v.tensor_scalar(out=st_s[:, dj * cw:(dj + 1) * cw], in0=tt3[:, 0:cw],
                                    scalar1=0.5, scalar2=0.5, op0=ALU.mult, op1=ALU.add)
                    v.drain().then_inc(s_dve, 1)
                    B.tag(s_dve, ("sr", ci, dj))

        def f_scalar(sc, B):
            for ci, (cs, cw) in enumerate(CH):
                ns = cw // 128
                xoff = (ci % 2) * 4 * D
                sc.wait_ge(s_x[ci % 2], 16 * (ci // 2 + 1))
                sc.wait_ge(s_dve, B.at(("mean", ci)))
                for s in range(ns):
                    xa = x2c[:, xoff + s * D:xoff + (s + 1) * D]
                    sc.activation(out=dump3[:], in_=xa, func=AF.Square,
                                  accum_out=stat3[:, 8 + s:9 + s])
                    sc.drain()
                sc.drain().then_inc(s_act, 1)
                B.tag(s_act, ("sq", ci))
                sc.wait_ge(s_dve, B.at(("var", ci)))
                sc.activation(out=stat3[:, 4:4 + ns], in_=stat3[:, 0:ns], func=AF.Ln)
                sc.drain()
                sc.activation(out=stat3[:, 4:4 + ns], in_=stat3[:, 4:4 + ns], func=AF.Exp,
                              scale=0.5)
                sc.drain().then_inc(s_act, 1)
                B.tag(s_act, ("rstd", ci))
                for s in range(ns):
                    sc.wait_ge(s_pe, B.at(("xT", ci, s)))
                    for j in range(8):
                        bank = pX0 if j < 4 else pX1
                        slot = (j % 4) * 128
                        sc.activation(out=hb[:, j * cw + s * 128:j * cw + s * 128 + 128],
                                      in_=bank[:, slot:slot + 128], func=AF.Identity,
                                      scale=cv[:, j:j + 1], bias=cv[:, 8 + j:8 + j + 1])
                    sc.drain().then_inc(s_act, 1)
                    B.tag(s_act, ("gb", ci, s))
                for dj in range(8):
                    sc.wait_ge(s_pe, B.at(("kvr", ci, dj)))
                    prevd = (ci, dj - 1) if dj > 0 else ((ci - 1, 7) if ci > 0 else None)
                    if prevd is not None:
                        sc.wait_ge(s_dve, B.at(("sr",) + prevd))
                    if ci >= 1 and dj == 0:
                        sc.wait_ge(s_stk, 16 * ci)
                        sc.wait_ge(s_stv, 16 * ci)
                    sc.copy(out=st_k[:, dj * cw:(dj + 1) * cw], in_=pK[:, 0:cw])
                    sc.copy(out=st_v[:, dj * cw:(dj + 1) * cw], in_=pV[:, 0:cw])
                    sc.drain()
                    sc.activation(out=tt3[:, 0:cw], in_=pR[:, 0:cw], func=AF.Tanh, scale=0.5)
                    sc.drain().then_inc(s_act, 1)
                    B.tag(s_act, ("ev", ci, dj))

        def f_tensor(te, B):
            te.wait_ge(s_c, 16 * 5)
            te.wait_ge(s_dve, B.at("ident"))
            for ci, (cs, cw) in enumerate(CH):
                ns = cw // 128
                te.wait_ge(s_dve, B.at(("hn", ci)))
                for s in range(ns):
                    if ci > 0 or s > 0:
                        prev = (ci, s - 1) if s > 0 else (ci - 1, CH[ci - 1][1] // 128 - 1)
                        te.wait_ge(s_act, B.at(("gb",) + prev))
                    last = None
                    for j in range(8):
                        bank = pX0 if j < 4 else pX1
                        slot = (j % 4) * 128
                        last = te.transpose(bank[:, slot:slot + 128],
                                            hn[:, s * D + j * 128:s * D + (j + 1) * 128],
                                            ident3[:])
                    last.then_inc(s_pe, 1)
                    B.tag(s_pe, ("xT", ci, s))
                te.wait_ge(s_dve, B.at(("mix", ci)))
                for dj in range(8):
                    if True:
                        prev = (ci, dj - 1) if dj > 0 else ((ci - 1, 7) if ci > 0 else None)
                        if prev is not None:
                            te.wait_ge(s_act, B.at(("ev",) + prev))
                            te.wait_ge(s_dve, B.at(("sr",) + prev))
                    for j in range(8):
                        te.matmul(pK[:, 0:cw], wk_sb[:, j * D + dj * 128:j * D + dj * 128 + 128],
                                  xkb[:, j * cw:j * cw + cw], start=(j == 0), stop=(j == 7))
                    for j in range(8):
                        te.matmul(pV[:, 0:cw], wv_sb[:, j * D + dj * 128:j * D + dj * 128 + 128],
                                  xvb[:, j * cw:j * cw + cw], start=(j == 0), stop=(j == 7))
                    last = None
                    for j in range(8):
                        last = te.matmul(pR[:, 0:cw], wr_sb[:, j * D + dj * 128:j * D + dj * 128 + 128],
                                         xrb[:, j * cw:j * cw + cw], start=(j == 0), stop=(j == 7))
                    last.then_inc(s_pe, 1)
                    B.tag(s_pe, ("kvr", ci, dj))

        two_pass(block, None, [("sync", f_sync), ("gpsimd", f_gpsimd),
                               ("vector", f_vector), ("scalar", f_scalar),
                               ("tensor", f_tensor)])


def build_stageC1(T_loc, EX):
    nc = bass.Bass("TRN2", target_bir_lowering=False)
    io = {
        "x2_in": nc.dram_tensor("x2_in", [T_loc + EX, D], F32, kind="ExternalInput").ap(),
        "WkT": nc.dram_tensor("WkT", [8, 128, D], F32, kind="ExternalInput").ap(),
        "WvT": nc.dram_tensor("WvT", [8, 128, D], F32, kind="ExternalInput").ap(),
        "WrT": nc.dram_tensor("WrT", [8, 128, D], F32, kind="ExternalInput").ap(),
        "c5": nc.dram_tensor("c5", [128, 40], F32, kind="ExternalInput").ap(),
        "bmask": nc.dram_tensor("bmask", [128, 1], F32, kind="ExternalInput").ap(),
        "k_out": nc.dram_tensor("k_out", [D, T_loc + EX], F32, kind="ExternalOutput").ap(),
        "v_out": nc.dram_tensor("v_out", [D, T_loc + EX], F32, kind="ExternalOutput").ap(),
        "sr_out": nc.dram_tensor("sr_out", [D, T_loc + EX], BF16, kind="ExternalOutput").ap(),
    }
    emit_stageC1(nc, io, T_loc, EX)
    return nc


def stageC1_inputs(inp, x2_glob, T, cid):
    b, t0, T_loc = shard_tokens(T, cid)
    lo = t0 - EX
    clo = max(lo, 0)
    x2s = np.zeros((T_loc + EX, D), np.float32)
    x2s[clo - lo:] = x2_glob[b, clo:t0 + T_loc]

    def tileT(w):
        return np.ascontiguousarray(w.T.reshape(8, 128, D))

    def col8(vec):
        return np.ascontiguousarray(vec.reshape(8, 128).T)

    c5v = np.concatenate([col8(inp["L1_attn_ln1_g"]), col8(inp["L1_attn_ln1_b"]),
                          col8(inp["L1_attn_tm_k"]), col8(inp["L1_attn_tm_v"]),
                          col8(inp["L1_attn_tm_r"])], axis=1)
    msk = np.full((128, 1), 0.0 if cid % 4 == 0 else 1.0, np.float32)
    return {"x2_in": x2s, "WkT": tileT(inp["L1_attn_Wk"]), "WvT": tileT(inp["L1_attn_Wv"]),
            "WrT": tileT(inp["L1_attn_Wr"]), "c5": np.ascontiguousarray(c5v), "bmask": msk}


def build_stageC2(T, PIECE=4096, sst=None):
    NP = T // PIECE

    nc = bass.Bass("TRN2", target_bir_lowering=False)
    k_in = nc.dram_tensor("k_in", [2, 128, T], F32, kind="ExternalInput").ap()
    v_in = nc.dram_tensor("v_in", [2, 128, T], F32, kind="ExternalInput").ap()
    td = nc.dram_tensor("td", [128, 1], F32, kind="ExternalInput").ap()
    tf = nc.dram_tensor("tf", [128, 1], F32, kind="ExternalInput").ap()
    y_out = nc.dram_tensor("y_out", [2, 128, T], BF16, kind="ExternalOutput").ap()

    with ExitStack() as st:
        def sb(name, shape, dt=F32):
            return st.enter_context(nc.sbuf_tensor(_tname(name), shape, dt))

        kp = sb("kp", [128, 2 * PIECE])
        vp = sb("vp", [128, 2 * PIECE])
        ek = sb("ek", [128, PIECE])
        ekv = sb("ekv", [128, PIECE])
        euk = sb("euk", [128, PIECE])
        As = sb("As", [128, PIECE])
        Bs = sb("Bs", [128, PIECE])
        num = sb("num", [128, PIECE])
        den = sb("den", [128, PIECE])
        yb = sb("yb", [128, PIECE], BF16)
        dmat = sb("dmat", [128, PIECE])
        cvec = sb("cvec", [128, 8])
        s_k = [(sst if sst is not None else st).enter_context(nc.semaphore(_semname())) for i in range(2)]
        s_v = [(sst if sst is not None else st).enter_context(nc.semaphore(_semname())) for i in range(2)]
        s_dve = (sst if sst is not None else st).enter_context(nc.semaphore(_semname()))
        s_act = (sst if sst is not None else st).enter_context(nc.semaphore(_semname()))
        s_st = (sst if sst is not None else st).enter_context(nc.semaphore(_semname()))
        s_c = (sst if sst is not None else st).enter_context(nc.semaphore(_semname()))
        block = st.enter_context(nc.Block())

        def f_sync(sync, B):
            sync.dma_start(out=cvec[:, 0:1], in_=td).then_inc(s_c, 16)
            sync.dma_start(out=cvec[:, 1:2], in_=tf).then_inc(s_c, 16)
            for pi in range(2 * NP):
                b, pp = divmod(pi, NP)
                if pi >= 2:
                    sync.wait_ge(s_dve, B.at(("used", pi - 2)))
                sync.dma_start(out=kp[:, (pi % 2) * PIECE:(pi % 2 + 1) * PIECE],
                               in_=k_in[b, :, pp * PIECE:(pp + 1) * PIECE]).then_inc(s_k[pi % 2], 16)
                sync.dma_start(out=vp[:, (pi % 2) * PIECE:(pi % 2 + 1) * PIECE],
                               in_=v_in[b, :, pp * PIECE:(pp + 1) * PIECE]).then_inc(s_v[pi % 2], 16)
                cj = pi - 1
                if cj >= 0:
                    bj, ppj = divmod(cj, NP)
                    sync.wait_ge(s_dve, B.at(("y", cj)))
                    sync.dma_start(out=y_out[bj, :, ppj * PIECE:(ppj + 1) * PIECE],
                                   in_=yb[:]).then_inc(s_st, 16)
            cj = 2 * NP - 1
            bj, ppj = divmod(cj, NP)
            sync.wait_ge(s_dve, B.at(("y", cj)))
            sync.dma_start(out=y_out[bj, :, ppj * PIECE:(ppj + 1) * PIECE],
                           in_=yb[:]).then_inc(s_st, 16)
            sync.wait_ge(s_st, 16 * 2 * NP)

        def f_scalar(sc, B):
            sc.wait_ge(s_c, 32)
            sc.activation(out=cvec[:, 2:3], in_=cvec[:, 0:1], func=AF.Exp)
            sc.drain()
            sc.activation(out=cvec[:, 2:3], in_=cvec[:, 2:3], func=AF.Exp, scale=-1.0)
            sc.drain()
            sc.activation(out=cvec[:, 3:4], in_=cvec[:, 1:2], func=AF.Exp)
            sc.drain().then_inc(s_act, 1)
            B.tag(s_act, "consts")
            for pi in range(2 * NP):
                sc.wait_ge(s_k[pi % 2], 16 * (pi // 2 + 1))
                if pi >= 1:
                    sc.wait_ge(s_dve, B.at(("ekdone", pi - 1)))
                sc.activation(out=ek[:], in_=kp[:, (pi % 2) * PIECE:(pi % 2 + 1) * PIECE],
                              func=AF.Exp)
                sc.drain().then_inc(s_act, 1)
                B.tag(s_act, ("ek", pi))

        def f_gpsimd(g, B):
            g.memset(dmat[:], 1.0)
            g.memset(cvec[:, 4:7], 0.0)
            g.drain().then_inc(s_dve, 1)
            B.tag(s_dve, "dinit")

        def f_vector(v, B):
            v.wait_ge(s_dve, B.at("dinit"))
            v.wait_ge(s_act, B.at("consts"))
            v.tensor_scalar(out=dmat[:], in0=dmat[:], scalar1=cvec[:, 2:3],
                            scalar2=None, op0=ALU.mult)
            v.drain().then_inc(s_dve, 1)
            B.tag(s_dve, "dmat")
            for pi in range(2 * NP):
                b, pp = divmod(pi, NP)
                slot = (pi % 2) * PIECE
                v.wait_ge(s_act, B.at(("ek", pi)))
                v.wait_ge(s_v[pi % 2], 16 * (pi // 2 + 1))
                v.tensor_scalar(out=euk[:], in0=ek[:], scalar1=cvec[:, 3:4],
                                scalar2=None, op0=ALU.mult)
                v.tensor_tensor(out=ekv[:], in0=ek[:], in1=vp[:, slot:slot + PIECE],
                                op=ALU.mult)
                v.drain()
                if pp == 0:
                    v.tensor_scalar(out=cvec[:, 4:6], in0=cvec[:, 4:6], scalar1=0.0,
                                    scalar2=None, op0=ALU.mult)
                    v.drain()
                v.tensor_tensor_scan(out=As[:], data0=dmat[:], data1=ekv[:],
                                     initial=cvec[:, 4:5], op0=ALU.mult, op1=ALU.add)
                v.drain()
                v.tensor_tensor_scan(out=Bs[:], data0=dmat[:], data1=ek[:],
                                     initial=cvec[:, 5:6], op0=ALU.mult, op1=ALU.add)
                v.drain().then_inc(s_dve, 1)
                B.tag(s_dve, ("ekdone", pi))
                v.tensor_tensor(out=num[:], in0=euk[:], in1=vp[:, slot:slot + PIECE],
                                op=ALU.mult)
                v.drain()
                v.tensor_tensor(out=num[:, 1:PIECE], in0=num[:, 1:PIECE],
                                in1=As[:, 0:PIECE - 1], op=ALU.add)
                v.tensor_tensor(out=num[:, 0:1], in0=num[:, 0:1], in1=cvec[:, 4:5], op=ALU.add)
                v.drain()
                v.tensor_tensor(out=den[:, 1:PIECE], in0=euk[:, 1:PIECE],
                                in1=Bs[:, 0:PIECE - 1], op=ALU.add)
                v.tensor_tensor(out=den[:, 0:1], in0=euk[:, 0:1], in1=cvec[:, 5:6], op=ALU.add)
                v.drain()
                v.tensor_copy(cvec[:, 4:5], As[:, PIECE - 1:PIECE])
                v.tensor_copy(cvec[:, 5:6], Bs[:, PIECE - 1:PIECE])
                v.drain()
                v.reciprocal(out=den[:], in_=den[:])
                v.drain()
                if pi >= 1:
                    v.wait_ge(s_st, 16 * pi)
                v.tensor_tensor(out=yb[:], in0=num[:], in1=den[:], op=ALU.mult)
                v.drain().then_inc(s_dve, 1)
                B.tag(s_dve, ("y", pi))
                v.engine_nop().then_inc(s_dve, 1)
                B.tag(s_dve, ("used", pi))

        two_pass(block, None, [("sync", f_sync), ("gpsimd", f_gpsimd),
                               ("scalar", f_scalar), ("vector", f_vector)])
    return nc


def stageC2_inputs(k_glob, v_glob, inp, cid):
    ch0 = cid * 128
    return {"k_in": np.ascontiguousarray(k_glob[:, ch0:ch0 + 128, :]),
            "v_in": np.ascontiguousarray(v_glob[:, ch0:ch0 + 128, :]),
            "td": np.ascontiguousarray(inp["L1_attn_time_decay"][ch0:ch0 + 128].reshape(128, 1)),
            "tf": np.ascontiguousarray(inp["L1_attn_time_first"][ch0:ch0 + 128].reshape(128, 1))}


def emit_stageC3a(nc, io, T_loc, EX, sst=None):
    CH = chunk_list(T_loc, EX)
    NCH = len(CH)
    x2_in = io["x2_in"]
    y_in = io["y_in"]
    sr_in = io["sr_in"]
    WoT = io["WoT"]
    FrT = io["FrT"]
    c4 = io["c4"]
    bmask = io["bmask"]
    s2m_out = io["s2m_out"]
    gk_out = io["gk_out"]
    sig_out = io["sig_out"]

    with ExitStack() as st:
        def sb(name, shape, dt=F32):
            return st.enter_context(nc.sbuf_tensor(_tname(name), shape, dt))

        def psum(name, shape, dt=F32):
            return st.enter_context(nc.psum_tensor(_tname(name), shape, dt))

        x2c = sb("x2c", [128, 2 * 4 * D])
        yc = sb("yc", [128, 8 * 512], BF16)
        src_ = sb("src_", [128, 8 * 512], BF16)
        rwkvb = sb("rwkvb", [128, 8 * 512], BF16)
        x2b = sb("x2b", [128, 8 * 512])
        xmidb = sb("xmidb", [128, 8 * 512])
        s2m = sb("s2m", [128, 8 * 512])
        gb_ = sb("gb_", [128, 8 * 512])
        dcol = sb("dcol", [128, 8 * 512])
        gkb = sb("gkb", [128, 8 * 512], BF16)
        grb = sb("grb", [128, 8 * 512], BF16)
        sigb = sb("sigb", [128, 8 * 512], BF16)
        wo_sb = sb("wo_sb", [128, 8 * D], BF16)
        fr_sb = sb("fr_sb", [128, 8 * D], BF16)
        cv = sb("cv", [128, 32])
        maskc = sb("maskc", [128, 1])
        gprev = sb("gprev", [128, 8])
        sq512 = sb("sq512", [128, 512])
        rows = sb("rows", [1, 2 * 512])
        mrow2 = sb("mrow2", [1, 1024])
        mbc = sb("mbc", [128, 1024])
        onescol = sb("onescol", [128, 1])
        ones1 = sb("ones1", [1, 128])
        ident4 = sb("ident4", [128, 128])

        pX0 = psum("pX0", [128, 512])
        pX1 = psum("pX1", [128, 512])
        pA0 = psum("pA0", [128, 512])
        pA1 = psum("pA1", [128, 512])
        pSt = psum("pSt", [1, 1024])
        pM = psum("pM", [128, 1024])

        s_x = [(sst if sst is not None else st).enter_context(nc.semaphore(_semname())) for i in range(2)]
        s_y = (sst if sst is not None else st).enter_context(nc.semaphore(_semname()))
        s_r = (sst if sst is not None else st).enter_context(nc.semaphore(_semname()))
        s_c = (sst if sst is not None else st).enter_context(nc.semaphore(_semname()))
        s_pe = (sst if sst is not None else st).enter_context(nc.semaphore(_semname()))
        s_act = (sst if sst is not None else st).enter_context(nc.semaphore(_semname()))
        s_dve = (sst if sst is not None else st).enter_context(nc.semaphore(_semname()))
        s_st1 = (sst if sst is not None else st).enter_context(nc.semaphore(_semname()))
        s_st2 = (sst if sst is not None else st).enter_context(nc.semaphore(_semname()))
        s_st3 = (sst if sst is not None else st).enter_context(nc.semaphore(_semname()))
        block = st.enter_context(nc.Block())

        def emit_store(sync, B, cj):
            cjs, cjw = CH[cj]
            sync.wait_ge(s_dve, B.at(("s2m", cj)))
            sync.dma_start(out=s2m_out[:, cjs:cjs + cjw].rearrange("(j p) t -> p j t", p=128),
                           in_=s2m[:, 0:8 * cjw].rearrange("p (j t) -> p j t", t=cjw)
                           ).then_inc(s_st1, 16)
            sync.wait_ge(s_dve, B.at(("mix", cj)))
            sync.dma_start(out=gk_out[:, cjs:cjs + cjw].rearrange("(j p) t -> p j t", p=128),
                           in_=gkb[:, 0:8 * cjw].rearrange("p (j t) -> p j t", t=cjw)
                           ).then_inc(s_st2, 16)
            sync.wait_ge(s_dve, B.at(("sig", cj, 7)))
            sync.dma_start(out=sig_out[:, cjs:cjs + cjw].rearrange("(j p) t -> p j t", p=128),
                           in_=sigb[:, 0:8 * cjw].rearrange("p (j t) -> p j t", t=cjw)
                           ).then_inc(s_st3, 16)

        def f_sync(sync, B):
            for ci, (cs, cw) in enumerate(CH):
                if ci >= 2:
                    sync.wait_ge(s_dve, B.at(("xmid", ci - 2, 7)))
                sync.dma_start(out=x2c[:, (ci % 2) * 4 * D:(ci % 2) * 4 * D + (cw // 128) * D]
                               .rearrange("p (s c) -> p s c", c=D),
                               in_=x2_in[cs:cs + cw, :].rearrange("(s p) c -> p s c", p=128)
                               ).then_inc(s_x[ci % 2], 16)
                if ci >= 1:
                    sync.wait_ge(s_dve, B.at(("rwkv", ci - 1)))
                sync.dma_start(out=yc[:, 0:8 * cw]
                               .rearrange("p (j t) -> p j t", t=cw),
                               in_=y_in[:, cs:cs + cw].rearrange("(j p) t -> p j t", p=128)
                               ).then_inc(s_y, 16)
                sync.dma_start(out=src_[:, 0:8 * cw]
                               .rearrange("p (j t) -> p j t", t=cw),
                               in_=sr_in[:, cs:cs + cw].rearrange("(j p) t -> p j t", p=128)
                               ).then_inc(s_r, 16)
                if ci >= 1:
                    emit_store(sync, B, ci - 1)
            emit_store(sync, B, NCH - 1)
            sync.wait_ge(s_st1, 16 * NCH)
            sync.wait_ge(s_st2, 16 * NCH)
            sync.wait_ge(s_st3, 16 * NCH)

        def f_gpsimd(g, B):
            g.dma_start(out=wo_sb[:].rearrange("p (j c) -> p j c", c=D),
                        in_=WoT.rearrange("j p c -> p j c")).then_inc(s_c, 16)
            g.dma_start(out=fr_sb[:].rearrange("p (j c) -> p j c", c=D),
                        in_=FrT.rearrange("j p c -> p j c")).then_inc(s_c, 16)
            g.dma_start(out=cv[:], in_=c4).then_inc(s_c, 16)
            g.dma_start(out=maskc[:], in_=bmask).then_inc(s_c, 16)
            g.iota(sq512[:, 0:128], [[1, 128]], base=0, channel_multiplier=-1,
                   allow_small_or_imprecise_dtypes=True)
            g.memset(gprev[:], 0.0)
            g.memset(onescol[:], 1.0)
            g.memset(ones1[:], 1.0)
            g.memset(mrow2[:], 0.0)
            g.memset(rows[:], 0.0)
            g.drain().then_inc(s_dve, 1)
            B.tag(s_dve, "init")

        def f_vector(v, B):
            v.wait_ge(s_dve, B.at("init"))
            v.wait_ge(s_c, 64)
            v.tensor_scalar(out=ident4[:], in0=sq512[:, 0:128], scalar1=0.0,
                            scalar2=None, op0=ALU.is_equal)
            v.drain().then_inc(s_dve, 1)
            B.tag(s_dve, "ident")
            for ci, (cs, cw) in enumerate(CH):
                ns = cw // 128
                v.wait_ge(s_y, 16 * (ci + 1))
                v.wait_ge(s_r, 16 * (ci + 1))
                if ci >= 1:
                    v.wait_ge(s_pe, B.at(("att", ci - 1, 7)))
                yv = yc[:, 0:8 * cw]
                sv = src_[:, 0:8 * cw]
                v.tensor_tensor(out=rwkvb[:, 0:8 * cw], in0=yv, in1=sv,
                                op=ALU.mult).then_inc(s_dve, 1)
                B.tag(s_dve, ("rwkv", ci))
                for c8 in range(8):
                    v.wait_ge(s_pe, B.at(("att", ci, c8)))
                    v.wait_ge(s_act, B.at(("x2ev", ci, ns - 1)))
                    if ci >= 1 and c8 == 0:
                        v.wait_ge(s_pe, B.at(("colsq", ci - 1, 7)))
                    pa = pA0 if c8 % 2 == 0 else pA1
                    v.tensor_tensor(out=xmidb[:, c8 * cw:(c8 + 1) * cw],
                                    in0=x2b[:, c8 * cw:(c8 + 1) * cw], in1=pa[:, 0:cw],
                                    op=ALU.add)
                    v.drain()
                    if ci >= 1 and c8 == 0:
                        v.wait_ge(s_st1, 16 * ci)
                    v.tensor_tensor(out=s2m[:, c8 * cw:(c8 + 1) * cw],
                                    in0=x2b[:, c8 * cw:(c8 + 1) * cw],
                                    in1=xmidb[:, c8 * cw:(c8 + 1) * cw],
                                    op=ALU.add).then_inc(s_dve, 1)
                    B.tag(s_dve, ("xmid", ci, c8))
                B.tag(s_dve, ("s2m", ci))
                v.wait_ge(s_pe, B.at(("colsq", ci, 7)))
                v.tensor_copy(rows[0:1, 0:cw], pSt[0:1, 0:cw])
                v.tensor_copy(rows[0:1, 512:512 + cw], pSt[0:1, 512:512 + cw])
                v.drain()
                v.tensor_scalar(out=mrow2[0:1, 0:cw], in0=rows[0:1, 0:cw], scalar1=1.0 / D,
                                scalar2=None, op0=ALU.mult)
                v.drain()
                v.tensor_scalar(out=rows[0:1, 512:512 + cw], in0=rows[0:1, 512:512 + cw],
                                scalar1=1.0 / D, scalar2=None, op0=ALU.mult)
                v.tensor_tensor(out=rows[0:1, 0:cw], in0=mrow2[0:1, 0:cw],
                                in1=mrow2[0:1, 0:cw], op=ALU.mult)
                v.drain()
                v.tensor_tensor(out=rows[0:1, 512:512 + cw], in0=rows[0:1, 512:512 + cw],
                                in1=rows[0:1, 0:cw], op=ALU.subtract)
                v.drain()
                v.tensor_scalar(out=rows[0:1, 512:512 + cw], in0=rows[0:1, 512:512 + cw],
                                scalar1=EPS, scalar2=None, op0=ALU.add)
                v.drain()
                v.reciprocal(out=rows[0:1, 512:512 + cw],
                             in_=rows[0:1, 512:512 + cw]).then_inc(s_dve, 1)
                B.tag(s_dve, ("rowvar", ci))
                v.wait_ge(s_pe, B.at(("bc", ci)))
                v.tensor_copy(mbc[:], pM[:])
                v.drain()
                mb3 = mbc[:, 0:512].rearrange("p (o t) -> p o t", o=1)[:, :, 0:cw]
                rb3 = mbc[:, 512:1024].rearrange("p (o t) -> p o t", o=1)[:, :, 0:cw]
                xb3 = xmidb[:, 0:8 * cw].rearrange("p (j t) -> p j t", t=cw)
                gb3 = gb_[:, 0:8 * cw].rearrange("p (j t) -> p j t", t=cw)
                v.tensor_tensor(out=gb3, in0=xb3, in1=bc(mb3, xb3), op=ALU.subtract)
                v.drain()
                v.tensor_tensor(out=gb3, in0=gb3, in1=bc(rb3, gb3),
                                op=ALU.mult).then_inc(s_dve, 1)
                B.tag(s_dve, ("gnorm", ci))
                v.wait_ge(s_act, B.at(("gb2", ci)))
                if ci >= 1:
                    v.wait_ge(s_pe, B.at(("fr", ci - 1, 7)))
                    v.wait_ge(s_st2, 16 * ci)
                for j in range(8):
                    hj = gb_[:, j * cw:(j + 1) * cw]
                    dj = dcol[:, j * cw:(j + 1) * cw]
                    v.tensor_tensor(out=dj[:, 1:cw], in0=hj[:, 1:cw], in1=hj[:, 0:cw - 1],
                                    op=ALU.subtract)
                    v.tensor_tensor(out=dj[:, 0:1], in0=hj[:, 0:1], in1=gprev[:, j:j + 1],
                                    op=ALU.subtract)
                v.drain()
                for mi, outb in [(2, gkb), (3, grb)]:
                    for j in range(8):
                        hj = gb_[:, j * cw:(j + 1) * cw]
                        dj = dcol[:, j * cw:(j + 1) * cw]
                        ob = outb[:, j * cw:(j + 1) * cw]
                        tm = cv[:, mi * 8 + j:mi * 8 + j + 1]
                        v.scalar_tensor_tensor(out=ob[:, 1:cw], in0=dj[:, 1:cw], scalar=tm,
                                               in1=hj[:, 0:cw - 1], op0=ALU.mult, op1=ALU.add)
                        v.scalar_tensor_tensor(out=ob[:, 0:1], in0=dj[:, 0:1], scalar=tm,
                                               in1=gprev[:, j:j + 1], op0=ALU.mult, op1=ALU.add)
                v.drain()
                for j in range(8):
                    v.tensor_copy(gprev[:, j:j + 1], gb_[:, j * cw + cw - 1:j * cw + cw])
                v.drain()
                if ci == 0 and EX:
                    v.tensor_scalar(out=gprev[:], in0=gprev[:], scalar1=maskc[:],
                                    scalar2=None, op0=ALU.mult)
                    v.drain()
                v.engine_nop().then_inc(s_dve, 1)
                B.tag(s_dve, ("mix", ci))
                for d8 in range(8):
                    v.wait_ge(s_act, B.at(("frt", ci, d8)))
                    if ci >= 1 and d8 == 0:
                        v.wait_ge(s_st3, 16 * ci)
                    v.tensor_scalar(out=sigb[:, d8 * cw:(d8 + 1) * cw], in0=dcol[:, 0:cw],
                                    scalar1=0.5, scalar2=0.5, op0=ALU.mult,
                                    op1=ALU.add).then_inc(s_dve, 1)
                    B.tag(s_dve, ("sig", ci, d8))

        def f_scalar(sc, B):
            for ci, (cs, cw) in enumerate(CH):
                ns = cw // 128
                for s in range(ns):
                    sc.wait_ge(s_pe, B.at(("xT", ci, s)))
                    for j in range(8):
                        bank = pX0 if j < 4 else pX1
                        slot = (j % 4) * 128
                        ins = sc.copy(out=x2b[:, j * cw + s * 128:j * cw + s * 128 + 128],
                                      in_=bank[:, slot:slot + 128])
                        if j == 7:
                            ins.then_inc(s_act, 1)
                    B.tag(s_act, ("x2ev", ci, s))
                for c8 in range(8):
                    sc.wait_ge(s_dve, B.at(("xmid", ci, c8)))
                    if ci >= 1 or c8 >= 1:
                        prev = (ci, c8 - 1) if c8 > 0 else (ci - 1, 7)
                        sc.wait_ge(s_pe, B.at(("colsq",) + prev))
                    sc.activation(out=sq512[:, 0:cw], in_=xmidb[:, c8 * cw:(c8 + 1) * cw],
                                  func=AF.Square).then_inc(s_act, 1)
                    B.tag(s_act, ("sq", ci, c8))
                sc.wait_ge(s_dve, B.at(("rowvar", ci)))
                sc.activation(out=mrow2[0:1, 512:512 + cw], in_=rows[0:1, 512:512 + cw],
                              func=AF.Ln)
                sc.drain()
                sc.activation(out=mrow2[0:1, 512:512 + cw], in_=mrow2[0:1, 512:512 + cw],
                              func=AF.Exp, scale=0.5).then_inc(s_act, 1)
                B.tag(s_act, ("rstdrow", ci))
                sc.wait_ge(s_dve, B.at(("gnorm", ci)))
                for j in range(8):
                    hj = gb_[:, j * cw:(j + 1) * cw]
                    ins = sc.activation(out=hj, in_=hj, func=AF.Identity,
                                        scale=cv[:, j:j + 1], bias=cv[:, 8 + j:8 + j + 1])
                ins.then_inc(s_act, 1)
                B.tag(s_act, ("gb2", ci))
                for d8 in range(8):
                    sc.wait_ge(s_pe, B.at(("fr", ci, d8)))
                    prevd = (ci, d8 - 1) if d8 > 0 else ((ci - 1, 7) if ci > 0 else None)
                    if prevd is not None:
                        sc.wait_ge(s_dve, B.at(("sig",) + prevd))
                    sc.activation(out=dcol[:, 0:cw], in_=pA0[:, 0:cw], func=AF.Tanh,
                                  scale=0.5).then_inc(s_act, 1)
                    B.tag(s_act, ("frt", ci, d8))

        def f_tensor(te, B):
            te.wait_ge(s_c, 64)
            te.wait_ge(s_dve, B.at("ident"))
            for ci, (cs, cw) in enumerate(CH):
                ns = cw // 128
                xoff = (ci % 2) * 4 * D
                te.wait_ge(s_x[ci % 2], 16 * (ci // 2 + 1))
                for s in range(ns):
                    if ci > 0 or s > 0:
                        prev = (ci, s - 1) if s > 0 else (ci - 1, CH[ci - 1][1] // 128 - 1)
                        te.wait_ge(s_act, B.at(("x2ev",) + prev))
                    if ci >= 1 and s == 0:
                        te.wait_ge(s_dve, B.at(("gnorm", ci - 1)))
                    last = None
                    for j in range(8):
                        bank = pX0 if j < 4 else pX1
                        slot = (j % 4) * 128
                        last = te.transpose(bank[:, slot:slot + 128],
                                            x2c[:, xoff + s * D + j * 128:xoff + s * D + (j + 1) * 128],
                                            ident4[:])
                    last.then_inc(s_pe, 1)
                    B.tag(s_pe, ("xT", ci, s))
                te.wait_ge(s_dve, B.at(("rwkv", ci)))
                te.wait_ge(s_act, B.at(("x2ev", ci, ns - 1)))
                for c8 in range(8):
                    pa = pA0 if c8 % 2 == 0 else pA1
                    if ci >= 1 and c8 == 0:
                        te.wait_ge(s_act, B.at(("frt", ci - 1, 7)))
                    if ci > 0 or c8 > 1:
                        prev = (ci, c8 - 2) if c8 > 1 else (ci - 1, 6 + c8)
                        te.wait_ge(s_dve, B.at(("xmid",) + prev))
                    last = None
                    for dj in range(8):
                        last = te.matmul(pa[:, 0:cw],
                                         wo_sb[:, dj * D + c8 * 128:dj * D + c8 * 128 + 128],
                                         rwkvb[:, dj * cw:(dj + 1) * cw],
                                         start=(dj == 0), stop=(dj == 7))
                    last.then_inc(s_pe, 1)
                    B.tag(s_pe, ("att", ci, c8))
                for c8 in range(8):
                    te.wait_ge(s_dve, B.at(("xmid", ci, c8)))
                    te.wait_ge(s_act, B.at(("sq", ci, c8)))
                    if ci >= 1 and c8 == 0:
                        te.wait_ge(s_dve, B.at(("rowvar", ci - 1)))
                    te.matmul(pSt[0:1, 0:cw], onescol[:], xmidb[:, c8 * cw:(c8 + 1) * cw],
                              start=(c8 == 0), stop=(c8 == 7))
                    last = te.matmul(pSt[0:1, 512:512 + cw], onescol[:], sq512[:, 0:cw],
                                     start=(c8 == 0), stop=(c8 == 7))
                    last.then_inc(s_pe, 1)
                    B.tag(s_pe, ("colsq", ci, c8))
                te.wait_ge(s_act, B.at(("rstdrow", ci)))
                if ci >= 1:
                    te.wait_ge(s_dve, B.at(("gnorm", ci - 1)))
                te.matmul(pM[:, 0:512], ones1[:], mrow2[0:1, 0:512], start=True, stop=True)
                last = te.matmul(pM[:, 512:1024], ones1[:], mrow2[0:1, 512:1024],
                                 start=True, stop=True)
                last.then_inc(s_pe, 1)
                B.tag(s_pe, ("bc", ci))
                te.wait_ge(s_dve, B.at(("mix", ci)))
                for d8 in range(8):
                    prevd = (ci, d8 - 1) if d8 > 0 else ((ci - 1, 7) if ci > 0 else None)
                    if prevd is not None:
                        te.wait_ge(s_act, B.at(("frt",) + prevd))
                    last = None
                    for j in range(8):
                        last = te.matmul(pA0[:, 0:cw],
                                         fr_sb[:, j * D + d8 * 128:j * D + d8 * 128 + 128],
                                         grb[:, j * cw:(j + 1) * cw],
                                         start=(j == 0), stop=(j == 7))
                    last.then_inc(s_pe, 1)
                    B.tag(s_pe, ("fr", ci, d8))

        two_pass(block, None, [("sync", f_sync), ("gpsimd", f_gpsimd),
                               ("vector", f_vector), ("scalar", f_scalar),
                               ("tensor", f_tensor)])


def build_stageC3a(T_loc, EX):
    nc = bass.Bass("TRN2", target_bir_lowering=False)
    io = {
        "x2_in": nc.dram_tensor("x2_in", [T_loc + EX, D], F32, kind="ExternalInput").ap(),
        "y_in": nc.dram_tensor("y_in", [D, T_loc + EX], BF16, kind="ExternalInput").ap(),
        "sr_in": nc.dram_tensor("sr_in", [D, T_loc + EX], BF16, kind="ExternalInput").ap(),
        "WoT": nc.dram_tensor("WoT", [8, 128, D], F32, kind="ExternalInput").ap(),
        "FrT": nc.dram_tensor("FrT", [8, 128, D], F32, kind="ExternalInput").ap(),
        "c4": nc.dram_tensor("c4", [128, 32], F32, kind="ExternalInput").ap(),
        "bmask": nc.dram_tensor("bmask", [128, 1], F32, kind="ExternalInput").ap(),
        "s2m_out": nc.dram_tensor("s2m_out", [D, T_loc + EX], F32, kind="ExternalOutput").ap(),
        "gk_out": nc.dram_tensor("gk_out", [D, T_loc + EX], BF16, kind="ExternalOutput").ap(),
        "sig_out": nc.dram_tensor("sig_out", [D, T_loc + EX], BF16, kind="ExternalOutput").ap(),
    }
    emit_stageC3a(nc, io, T_loc, EX)
    return nc


def stageC3a_inputs(inp, x2_glob, y_glob, sr_core, T, cid):
    b, t0, T_loc = shard_tokens(T, cid)
    lo = t0 - EX
    clo = max(lo, 0)
    x2s = np.zeros((T_loc + EX, D), np.float32)
    x2s[clo - lo:] = x2_glob[b, clo:t0 + T_loc]
    import ml_dtypes
    ys = np.zeros((D, T_loc + EX), ml_dtypes.bfloat16)
    ys[:, clo - lo:] = y_glob[b, :, clo:t0 + T_loc]

    def col8(vec):
        return np.ascontiguousarray(vec.reshape(8, 128).T)

    c4v = np.concatenate([col8(inp["L1_attn_ln2_g"]), col8(inp["L1_attn_ln2_b"]),
                          col8(inp["L1_attn_fm_k"]), col8(inp["L1_attn_fm_r"])], axis=1)
    msk = np.full((128, 1), 0.0 if cid % 4 == 0 else 1.0, np.float32)
    return {"x2_in": x2s, "y_in": ys, "sr_in": sr_core,
            "WoT": np.ascontiguousarray(inp["L1_attn_Wo"].T.reshape(8, 128, D)),
            "FrT": np.ascontiguousarray(inp["L1_attn_Fr"].T.reshape(8, 128, D)),
            "c4": np.ascontiguousarray(c4v), "bmask": msk}


def emit_stageC3b(nc, io, T_loc, sst=None):
    F1 = 4096
    CH = [(s, 512) for s in range(0, T_loc, 512)]
    NCH = len(CH)
    gk_in = io["gk_in"]
    sig_in = io["sig_in"]
    s2m_in = io["s2m_in"]
    FkT = io["FkT"]
    FvT = io["FvT"]
    x3a_out = io["x3a_out"]

    with ExitStack() as st:
        def sb(name, shape, dt=F32):
            return st.enter_context(nc.sbuf_tensor(_tname(name), shape, dt))

        def psum(name, shape, dt=F32):
            return st.enter_context(nc.psum_tensor(_tname(name), shape, dt))

        fk_sb = sb("fk_sb", [128, 8 * F1], BF16)
        fv_sb = sb("fv_sb", [128, 32 * D], BF16)
        gkc = sb("gkc", [128, 4096], BF16)
        sgc = sb("sgc", [128, 4096], BF16)
        smc = sb("smc", [128, 4096])
        kfq = sb("kfq", [128, 32 * 512], BF16)
        sq512 = sb("sq512", [128, 512])
        mrow2 = sb("mrow2", [1, 1024])
        mbc = sb("mbc", [128, 1024])
        rows = mbc[0:1, :]
        x3a = sb("x3a", [128, D])
        x3p = smc
        onescol = sb("onescol", [128, 1])
        ones1 = sb("ones1", [1, 128])
        ident5 = sb("ident5", [128, 128])

        pKf = psum("pKf", [128, 512])
        pKv0 = psum("pKv0", [128, 512])
        pKv1 = psum("pKv1", [128, 512])
        pSt = psum("pSt", [1, 512])
        pM = psum("pM", [128, 1024])
        pX0 = psum("pX0", [128, 512])
        pX1 = psum("pX1", [128, 512])

        s_g = (sst if sst is not None else st).enter_context(nc.semaphore(_semname()))
        s_s = (sst if sst is not None else st).enter_context(nc.semaphore(_semname()))
        s_m = (sst if sst is not None else st).enter_context(nc.semaphore(_semname()))
        s_c = (sst if sst is not None else st).enter_context(nc.semaphore(_semname()))
        s_pe = (sst if sst is not None else st).enter_context(nc.semaphore(_semname()))
        s_act = (sst if sst is not None else st).enter_context(nc.semaphore(_semname()))
        s_dve = (sst if sst is not None else st).enter_context(nc.semaphore(_semname()))
        s_st = (sst if sst is not None else st).enter_context(nc.semaphore(_semname()))
        block = st.enter_context(nc.Block())

        def f_sync(sync, B):
            for ci, (cs, cw) in enumerate(CH):
                if ci >= 1:
                    sync.wait_ge(s_pe, B.at(("kf", ci - 1, 31)))
                sync.dma_start(out=gkc[:, 0:8 * cw]
                               .rearrange("p (j t) -> p j t", t=cw),
                               in_=gk_in[:, cs:cs + cw].rearrange("(j p) t -> p j t", p=128)
                               ).then_inc(s_g, 16)
                if ci >= 1:
                    sync.wait_ge(s_dve, B.at(("x3p", ci - 1, 7)))
                    sync.wait_ge(s_pe, B.at(("x3T", ci - 1, 3)))
                sync.dma_start(out=sgc[:, 0:8 * cw].rearrange("p (j t) -> p j t", t=cw),
                               in_=sig_in[:, cs:cs + cw].rearrange("(j p) t -> p j t", p=128)
                               ).then_inc(s_s, 16)
                sync.dma_start(out=smc[:, 0:8 * cw].rearrange("p (j t) -> p j t", t=cw),
                               in_=s2m_in[:, cs:cs + cw].rearrange("(j p) t -> p j t", p=128)
                               ).then_inc(s_m, 16)
                for s in range(4):
                    cj, sj = (ci, s)
                    sync.wait_ge(s_act, B.at(("x3ev", cj, sj)))
                    sync.dma_start(out=x3a_out[cs + sj * 128:cs + sj * 128 + 128, :],
                                   in_=x3a[:]).then_inc(s_st, 16)
            sync.wait_ge(s_st, 16 * 4 * NCH)

        def f_gpsimd(g, B):
            g.dma_start(out=fk_sb[:].rearrange("p (j f) -> p j f", f=F1),
                        in_=FkT.rearrange("j p f -> p j f")).then_inc(s_c, 16)
            g.dma_start(out=fv_sb[:].rearrange("p (j c) -> p j c", c=D),
                        in_=FvT.rearrange("j p c -> p j c")).then_inc(s_c, 16)
            g.iota(sq512[:, 0:128], [[1, 128]], base=0, channel_multiplier=-1,
                   allow_small_or_imprecise_dtypes=True)
            g.memset(onescol[:], 1.0)
            g.memset(ones1[:], 1.0)
            g.memset(mrow2[:], 0.0)
            g.memset(mbc[:], 0.0)
            g.drain().then_inc(s_dve, 1)
            B.tag(s_dve, "init")

        def f_vector(v, B):
            v.wait_ge(s_dve, B.at("init"))
            v.wait_ge(s_c, 32)
            v.tensor_scalar(out=ident5[:], in0=sq512[:, 0:128], scalar1=0.0,
                            scalar2=None, op0=ALU.is_equal)
            v.drain().then_inc(s_dve, 1)
            B.tag(s_dve, "ident")
            for ci, (cs, cw) in enumerate(CH):
                for fj in range(32):
                    v.wait_ge(s_pe, B.at(("kf", ci, fj)))
                    if ci >= 1 and fj == 0:
                        v.wait_ge(s_pe, B.at(("kv", ci - 1, 7)))
                    kslice = kfq[:, fj * cw:(fj + 1) * cw]
                    v.tensor_scalar(out=kslice, in0=pKf[:, 0:cw], scalar1=0.0,
                                    scalar2=None, op0=ALU.max)
                    v.drain()
                    v.tensor_tensor(out=kslice, in0=kslice, in1=kslice,
                                    op=ALU.mult).then_inc(s_dve, 1)
                    B.tag(s_dve, ("kfq", ci, fj))
                v.wait_ge(s_s, 16 * (ci + 1))
                v.wait_ge(s_m, 16 * (ci + 1))
                for c8 in range(8):
                    v.wait_ge(s_pe, B.at(("kv", ci, c8)))
                    if ci >= 1 and c8 == 0:
                        v.wait_ge(s_pe, B.at(("x3T", ci - 1, 3)))
                        v.wait_ge(s_st, 16 * 4 * ci)
                    pkv = pKv0 if c8 % 2 == 0 else pKv1
                    tmp = x3a[:, (c8 % 2) * 512:(c8 % 2) * 512 + cw]
                    v.tensor_tensor(out=tmp, in0=sgc[:, c8 * cw:(c8 + 1) * cw],
                                    in1=pkv[:, 0:cw], op=ALU.mult)
                    v.drain()
                    v.tensor_tensor(out=x3p[:, c8 * cw:(c8 + 1) * cw],
                                    in0=x3p[:, c8 * cw:(c8 + 1) * cw],
                                    in1=tmp, op=ALU.add).then_inc(s_dve, 1)
                    B.tag(s_dve, ("x3p", ci, c8))
                v.wait_ge(s_pe, B.at(("colsq", ci, 7)))
                v.tensor_copy(rows[0:1, 0:cw], pSt[0:1, 0:cw])
                v.tensor_copy(rows[0:1, 512:512 + cw], pM[0:1, 0:cw])
                v.drain()
                v.tensor_scalar(out=mrow2[0:1, 0:cw], in0=rows[0:1, 0:cw], scalar1=1.0 / D,
                                scalar2=None, op0=ALU.mult)
                v.drain()
                v.tensor_scalar(out=rows[0:1, 512:512 + cw], in0=rows[0:1, 512:512 + cw],
                                scalar1=1.0 / D, scalar2=None, op0=ALU.mult)
                v.tensor_tensor(out=rows[0:1, 0:cw], in0=mrow2[0:1, 0:cw],
                                in1=mrow2[0:1, 0:cw], op=ALU.mult)
                v.drain()
                v.tensor_tensor(out=rows[0:1, 512:512 + cw], in0=rows[0:1, 512:512 + cw],
                                in1=rows[0:1, 0:cw], op=ALU.subtract)
                v.drain()
                v.tensor_scalar(out=rows[0:1, 512:512 + cw], in0=rows[0:1, 512:512 + cw],
                                scalar1=EPS, scalar2=None, op0=ALU.add)
                v.drain()
                v.reciprocal(out=rows[0:1, 512:512 + cw],
                             in_=rows[0:1, 512:512 + cw]).then_inc(s_dve, 1)
                B.tag(s_dve, ("rowvar", ci))
                v.wait_ge(s_pe, B.at(("bc", ci)))
                v.tensor_copy(mbc[:], pM[:])
                v.drain()
                mb3 = mbc[:, 0:512].rearrange("p (o t) -> p o t", o=1)[:, :, 0:cw]
                rb3 = mbc[:, 512:1024].rearrange("p (o t) -> p o t", o=1)[:, :, 0:cw]
                xb3 = x3p[:, 0:8 * cw].rearrange("p (j t) -> p j t", t=cw)
                v.tensor_tensor(out=xb3, in0=xb3, in1=bc(mb3, xb3), op=ALU.subtract)
                v.drain()
                v.tensor_tensor(out=xb3, in0=xb3, in1=bc(rb3, xb3),
                                op=ALU.mult).then_inc(s_dve, 1)
                B.tag(s_dve, ("x3n", ci))

        def f_scalar(sc, B):
            for ci, (cs, cw) in enumerate(CH):
                for c8 in range(8):
                    sc.wait_ge(s_dve, B.at(("x3p", ci, c8)))
                    if ci >= 1 or c8 >= 1:
                        prev = (ci, c8 - 1) if c8 > 0 else (ci - 1, 7)
                        sc.wait_ge(s_pe, B.at(("colsq",) + prev))
                    sc.activation(out=sq512[:, 0:cw], in_=x3p[:, c8 * cw:(c8 + 1) * cw],
                                  func=AF.Square).then_inc(s_act, 1)
                    B.tag(s_act, ("sq", ci, c8))
                sc.wait_ge(s_dve, B.at(("rowvar", ci)))
                sc.activation(out=mrow2[0:1, 512:512 + cw], in_=rows[0:1, 512:512 + cw],
                              func=AF.Ln)
                sc.drain()
                sc.activation(out=mrow2[0:1, 512:512 + cw], in_=mrow2[0:1, 512:512 + cw],
                              func=AF.Exp, scale=0.5).then_inc(s_act, 1)
                B.tag(s_act, ("rstdrow", ci))
                for s in range(4):
                    sc.wait_ge(s_pe, B.at(("x3T", ci, s)))
                    if ci > 0 or s > 0:
                        sc.wait_ge(s_st, 16 * (4 * ci + s))
                    for j in range(8):
                        bank = pX0 if j < 4 else pX1
                        slot = (j % 4) * 128
                        ins = sc.copy(out=x3a[:, j * 128:(j + 1) * 128],
                                      in_=bank[:, slot:slot + 128])
                    ins.then_inc(s_act, 1)
                    B.tag(s_act, ("x3ev", ci, s))

        def f_tensor(te, B):
            te.wait_ge(s_c, 32)
            te.wait_ge(s_dve, B.at("ident"))
            for ci, (cs, cw) in enumerate(CH):
                te.wait_ge(s_g, 16 * (ci + 1))
                for fj in range(32):
                    if ci > 0 or fj > 0:
                        prevf = (ci, fj - 1) if fj > 0 else (ci - 1, 31)
                        te.wait_ge(s_dve, B.at(("kfq",) + prevf))
                    last = None
                    for j in range(8):
                        last = te.matmul(pKf[:, 0:cw],
                                         fk_sb[:, j * F1 + fj * 128:j * F1 + fj * 128 + 128],
                                         gkc[:, j * cw:(j + 1) * cw],
                                         start=(j == 0), stop=(j == 7))
                    last.then_inc(s_pe, 1)
                    B.tag(s_pe, ("kf", ci, fj))
                te.wait_ge(s_dve, B.at(("kfq", ci, 31)))
                for c8 in range(8):
                    pkv = pKv0 if c8 % 2 == 0 else pKv1
                    if ci > 0 or c8 > 1:
                        prev = (ci, c8 - 2) if c8 > 1 else (ci - 1, 6 + c8)
                        te.wait_ge(s_dve, B.at(("x3p",) + prev))
                    last = None
                    for fj in range(32):
                        last = te.matmul(pkv[:, 0:cw],
                                         fv_sb[:, fj * D + c8 * 128:fj * D + c8 * 128 + 128],
                                         kfq[:, fj * cw:(fj + 1) * cw],
                                         start=(fj == 0), stop=(fj == 31))
                    last.then_inc(s_pe, 1)
                    B.tag(s_pe, ("kv", ci, c8))
                for c8 in range(8):
                    te.wait_ge(s_dve, B.at(("x3p", ci, c8)))
                    te.wait_ge(s_act, B.at(("sq", ci, c8)))
                    if ci >= 1 and c8 == 0:
                        te.wait_ge(s_dve, B.at(("x3n", ci - 1)))
                    te.matmul(pSt[0:1, 0:cw], onescol[:], x3p[:, c8 * cw:(c8 + 1) * cw],
                              start=(c8 == 0), stop=(c8 == 7))
                    last = te.matmul(pM[0:1, 0:cw], onescol[:], sq512[:, 0:cw],
                                     start=(c8 == 0), stop=(c8 == 7))
                    last.then_inc(s_pe, 1)
                    B.tag(s_pe, ("colsq", ci, c8))
                te.wait_ge(s_act, B.at(("rstdrow", ci)))
                if ci >= 1:
                    te.wait_ge(s_dve, B.at(("x3n", ci - 1)))
                te.matmul(pM[:, 0:512], ones1[:], mrow2[0:1, 0:512], start=True, stop=True)
                last = te.matmul(pM[:, 512:1024], ones1[:], mrow2[0:1, 512:1024],
                                 start=True, stop=True)
                last.then_inc(s_pe, 1)
                B.tag(s_pe, ("bc", ci))
                te.wait_ge(s_dve, B.at(("x3n", ci)))
                for s in range(4):
                    if ci > 0 or s > 0:
                        prev = (ci, s - 1) if s > 0 else (ci - 1, 3)
                        te.wait_ge(s_act, B.at(("x3ev",) + prev))
                    last = None
                    for j in range(8):
                        bank = pX0 if j < 4 else pX1
                        slot = (j % 4) * 128
                        last = te.transpose(bank[:, slot:slot + 128],
                                            x3p[:, j * cw + s * 128:j * cw + s * 128 + 128],
                                            ident5[:])
                    last.then_inc(s_pe, 1)
                    B.tag(s_pe, ("x3T", ci, s))

        two_pass(block, None, [("sync", f_sync), ("gpsimd", f_gpsimd),
                               ("vector", f_vector), ("scalar", f_scalar),
                               ("tensor", f_tensor)])


def build_stageC3b(T_loc):
    F1 = 4096
    nc = bass.Bass("TRN2", target_bir_lowering=False)
    io = {
        "gk_in": nc.dram_tensor("gk_in", [D, T_loc], BF16, kind="ExternalInput").ap(),
        "sig_in": nc.dram_tensor("sig_in", [D, T_loc], BF16, kind="ExternalInput").ap(),
        "s2m_in": nc.dram_tensor("s2m_in", [D, T_loc], F32, kind="ExternalInput").ap(),
        "FkT": nc.dram_tensor("FkT", [8, 128, F1], F32, kind="ExternalInput").ap(),
        "FvT": nc.dram_tensor("FvT", [32, 128, D], F32, kind="ExternalInput").ap(),
        "x3a_out": nc.dram_tensor("x3a_out", [T_loc, D], F32, kind="ExternalOutput").ap(),
    }
    emit_stageC3b(nc, io, T_loc)
    return nc


def stageC3b_inputs(inp, gk_core, sig_core, s2m_core, T, cid):
    F1 = 4096
    return {"gk_in": gk_core, "sig_in": sig_core, "s2m_in": s2m_core,
            "FkT": np.ascontiguousarray(inp["L1_attn_Fk"].T.reshape(8, 128, F1)),
            "FvT": np.ascontiguousarray(inp["L1_attn_Fv"].T.reshape(32, 128, D))}


def emit_fp6_ladder(v, width, au_ap, sg_ap, out_ap, scr, out_wait=None):
    nb = width // 32
    un = scr["un"][:, 0:width]
    u0 = scr["lad"][:, 0:width]
    fb = scr["lad2"][:, 0:width]
    bm = scr["bm"][:, 0:nb]
    rb = scr["rb6"][:, 0:nb]
    sc = scr["sc6"][:, 0:nb]
    au3 = au_ap.rearrange("p (nb w) -> p nb w", w=32)
    v.tensor_reduce(out=bm, in_=au3, axis=AX.X, op=ALU.max)
    v.drain()
    v.tensor_scalar(out=bm, in0=bm, scalar1=7.5e-30, scalar2=None, op0=ALU.max)
    v.drain()
    v.reciprocal(out=rb, in_=bm)
    v.tensor_scalar(out=sc, in0=bm, scalar1=1.0 / 7.5, scalar2=None, op0=ALU.mult)
    v.drain()
    v.tensor_scalar(out=rb, in0=rb, scalar1=7.5, scalar2=None, op0=ALU.mult)
    v.drain()
    un3 = un.rearrange("p (nb w) -> p nb w", w=32)
    rb3 = rb.rearrange("p (nb w) -> p nb w", w=1)
    v.tensor_tensor(out=un3, in0=au3, in1=bc(rb3, au3), op=ALU.mult)
    v.drain()
    st = scr["stage"][:, 0:width]
    v.tensor_scalar(out=fb, in0=un, scalar1=1.9375, scalar2=-4.0, op0=ALU.is_gt, op1=ALU.mult)
    v.tensor_scalar(out=st, in0=un, scalar1=3.875, scalar2=-2.0, op0=ALU.is_gt, op1=ALU.mult)
    v.drain()
    v.scalar_tensor_tensor(out=fb, in0=fb, scalar=8.0, in1=st, op0=ALU.add, op1=ALU.add)
    v.drain()
    v.tensor_tensor(out=un, in0=un, in1=fb, op=ALU.mult)
    v.drain()
    v.tensor_scalar(out=u0, in0=un, scalar1=0.5, scalar2=None, op0=ALU.add)
    v.reciprocal(out=fb, in_=fb)
    v.drain()
    for K in (8.0, 4.0, 2.0, 1.0):
        v.tensor_scalar(out=st, in0=u0, scalar1=K, scalar2=K,
                        op0=ALU.is_ge, op1=ALU.mult)
        v.drain()
        v.tensor_tensor(out=u0, in0=u0, in1=st, op=ALU.subtract)
        v.drain()
    v.scalar_tensor_tensor(out=u0, in0=un, scalar=0.5, in1=u0,
                           op0=ALU.add, op1=ALU.subtract)
    v.drain()
    v.tensor_tensor(out=u0, in0=u0, in1=fb, op=ALU.mult)
    v.drain()
    u03 = u0.rearrange("p (nb w) -> p nb w", w=32)
    sc3 = sc.rearrange("p (nb w) -> p nb w", w=1)
    v.tensor_tensor(out=u03, in0=u03, in1=bc(sc3, u03), op=ALU.mult)
    v.drain()
    if out_wait is not None:
        out_wait()
    return v.tensor_tensor(out=out_ap, in0=u0, in1=sg_ap, op=ALU.mult)


def emit_stageD(nc, io, T_loc, sst=None):
    CH = [(s, 512) for s in range(0, T_loc, 512)]
    NCH = len(CH)
    x3_in = io["x3_in"]
    w6Tq = io["w6Tq"]
    bias_row = io["bias_row"]
    x4_out = io["x4_out"]

    with ExitStack() as st:
        SST = sst if sst is not None else st

        def sb(name, shape, dt=F32):
            return st.enter_context(nc.sbuf_tensor(_tname(name), shape, dt))

        def psum(name, shape, dt=F32):
            return st.enter_context(nc.psum_tensor(_tname(name), shape, dt))

        x3c = sb("x3c", [128, 2 * 4 * D])
        w6T = sb("w6T", [128, 8 * D], BF16)
        q6a = sb("q6a", [128, 4 * D], BF16)
        q6b = sb("q6b", [128, 8 * 128], BF16)
        auq = sb("auq", [128, 4 * D])
        sgq = sb("sgq", [128, 4 * D])
        unq = sb("unq", [128, 4 * D])
        ladq = sb("ladq", [128, 4 * D])
        lad2q = sb("lad2q", [128, 4 * D])
        stq = sb("stq", [128, 4 * D])
        bmq = sb("bmq", [128, 128])
        rb6q = sb("rb6q", [128, 128])
        sc6q = sb("sc6q", [128, 128])
        biasr = sb("biasr", [1, D])
        biasbc = sb("biasbc", [128, D])
        zpre4 = sb("zpre4", [128, D])
        x4n = sb("x4n", [128, D])
        dump6 = sb("dump6", [128, D])
        stat6 = sb("stat6", [128, 16])
        ident6 = sb("ident6", [128, 128], BF16)
        ones1 = sb("ones1", [1, 128])
        scr = dict(un=unq, lad=ladq, lad2=lad2q, stage=stq, bm=bmq, rb6=rb6q, sc6=sc6q)

        pT0 = psum("pT0", [128, 512], BF16)
        pT1 = psum("pT1", [128, 512], BF16)
        pO0 = psum("pO0", [128, 512])
        pO1 = psum("pO1", [128, 512])
        pB = psum("pB", [128, 1024])

        s_x = [SST.enter_context(nc.semaphore(_semname())) for i in range(2)]
        s_c = SST.enter_context(nc.semaphore(_semname()))
        s_pe = SST.enter_context(nc.semaphore(_semname()))
        s_act = SST.enter_context(nc.semaphore(_semname()))
        s_dve = SST.enter_context(nc.semaphore(_semname()))
        s_st = SST.enter_context(nc.semaphore(_semname()))
        block = st.enter_context(nc.Block())

        def f_sync(sync, B):
            sync.dma_start(out=biasr[:], in_=bias_row).then_inc(s_c, 16)
            sync.dma_start(out=w6T[:].rearrange("p (j c) -> p j c", c=D),
                           in_=w6Tq.rearrange("j p c -> p j c")).then_inc(s_c, 16)
            for ci, (cs, cw) in enumerate(CH):
                if ci >= 2:
                    sync.wait_ge(s_dve, B.at(("x4", ci - 2, 3)))
                sync.dma_start(out=x3c[:, (ci % 2) * 4 * D:(ci % 2 + 1) * 4 * D]
                               .rearrange("p (s c) -> p s c", c=D),
                               in_=x3_in[cs:cs + cw, :].rearrange("(s p) c -> p s c", p=128)
                               ).then_inc(s_x[ci % 2], 16)
                for s in range(4):
                    sync.wait_ge(s_dve, B.at(("x4", ci, s)))
                    sync.dma_start(out=x4_out[cs + s * 128:cs + s * 128 + 128, :],
                                   in_=x4n[:]).then_inc(s_st, 16)
            sync.wait_ge(s_st, 16 * 4 * NCH)

        def f_gpsimd(g, B):
            g.iota(dump6[:, 0:128], [[1, 128]], base=0, channel_multiplier=-1,
                   allow_small_or_imprecise_dtypes=True)
            g.memset(ones1[:], 1.0)
            g.drain().then_inc(s_dve, 1)
            B.tag(s_dve, "init")

        def f_vector(v, B):
            v.wait_ge(s_dve, B.at("init"))
            v.wait_ge(s_c, 16)
            v.tensor_scalar(out=dump6[:, 128:256], in0=dump6[:, 0:128], scalar1=0.0,
                            scalar2=None, op0=ALU.is_equal)
            v.drain()
            v.tensor_copy(ident6[:], dump6[:, 128:256])
            v.drain().then_inc(s_dve, 1)
            B.tag(s_dve, "ident")
            for ci, (cs, cw) in enumerate(CH):
                v.wait_ge(s_act, B.at(("xas", ci)))
                if ci >= 1:
                    v.wait_ge(s_pe, B.at(("qT", ci - 1, 3)))
                emit_fp6_ladder(v, 4 * D, auq[:], sgq[:], q6a[:],
                                scr).then_inc(s_dve, 1)
                B.tag(s_dve, ("q6", ci))
                for s in range(4):
                    v.wait_ge(s_pe, B.at(("out1", ci, s)))
                    if (ci, s) != (0, 0):
                        prevz = (ci, s - 1) if s > 0 else (ci - 1, 3)
                        v.wait_ge(s_act, B.at(("sq6",) + prevz))
                    xa = x3c[:, (ci % 2) * 4 * D + s * D:(ci % 2) * 4 * D + (s + 1) * D]
                    v.tensor_tensor(out=zpre4[:, 0:512], in0=xa[:, 0:512], in1=pO0[:],
                                    op=ALU.add)
                    v.tensor_tensor(out=zpre4[:, 512:1024], in0=xa[:, 512:1024], in1=pO1[:],
                                    op=ALU.add)
                    v.drain()
                    v.tensor_tensor(out=zpre4[:], in0=zpre4[:], in1=biasbc[:], op=ALU.add)
                    v.drain()
                    v.tensor_reduce(out=stat6[:, 0:1], in_=zpre4[:], axis=AX.X, op=ALU.add)
                    v.drain()
                    v.tensor_scalar(out=stat6[:, 6:7], in0=stat6[:, 0:1], scalar1=1.0 / D,
                                    scalar2=None, op0=ALU.mult)
                    v.drain().then_inc(s_dve, 1)
                    B.tag(s_dve, ("zpre", ci, s))
                    v.wait_ge(s_act, B.at(("sq6", ci, s)))
                    v.tensor_scalar(out=stat6[:, 3:4], in0=stat6[:, 2:3], scalar1=1.0 / D,
                                    scalar2=None, op0=ALU.mult)
                    v.tensor_tensor(out=stat6[:, 7:8], in0=stat6[:, 6:7], in1=stat6[:, 6:7],
                                    op=ALU.mult)
                    v.drain()
                    v.tensor_tensor(out=stat6[:, 3:4], in0=stat6[:, 3:4], in1=stat6[:, 7:8],
                                    op=ALU.subtract)
                    v.drain()
                    v.tensor_scalar(out=stat6[:, 3:4], in0=stat6[:, 3:4], scalar1=EPS,
                                    scalar2=None, op0=ALU.add)
                    v.drain()
                    v.reciprocal(out=stat6[:, 3:4], in_=stat6[:, 3:4]).then_inc(s_dve, 1)
                    B.tag(s_dve, ("var", ci, s))
                    v.wait_ge(s_act, B.at(("rstd6", ci, s)))
                    v.wait_ge(s_st, 16 * (4 * ci + s))
                    v.tensor_scalar(out=x4n[:], in0=zpre4[:], scalar1=stat6[:, 6:7],
                                    scalar2=stat6[:, 5:6], op0=ALU.subtract,
                                    op1=ALU.mult).then_inc(s_dve, 1)
                    B.tag(s_dve, ("x4", ci, s))

        def f_scalar(sc, B):
            sc.wait_ge(s_pe, B.at("biasbc"))
            sc.copy(out=biasbc[:], in_=pB[:, 0:1024])
            sc.drain().then_inc(s_act, 1)
            B.tag(s_act, "biasev")
            for ci, (cs, cw) in enumerate(CH):
                sc.wait_ge(s_x[ci % 2], 16 * (ci // 2 + 1))
                if ci >= 1:
                    sc.wait_ge(s_dve, B.at(("q6", ci - 1)))
                xa = x3c[:, (ci % 2) * 4 * D:(ci % 2 + 1) * 4 * D]
                sc.activation(out=auq[:], in_=xa, func=AF.Abs)
                sc.drain()
                sc.activation(out=sgq[:], in_=xa, func=AF.Sign)
                sc.drain().then_inc(s_act, 1)
                B.tag(s_act, ("xas", ci))
                for s in range(4):
                    sc.wait_ge(s_pe, B.at(("qT", ci, s)))
                    if (ci, s) != (0, 0):
                        prev = (ci, s - 1) if s > 0 else (ci - 1, 3)
                        sc.wait_ge(s_pe, B.at(("out1",) + prev))
                    for j in range(8):
                        bank = pT0 if j < 4 else pT1
                        slot = (j % 4) * 128
                        ins = sc.copy(out=q6b[:, j * 128:(j + 1) * 128],
                                      in_=bank[:, slot:slot + 128])
                    ins.then_inc(s_act, 1)
                    B.tag(s_act, ("qev", ci, s))
                    sc.wait_ge(s_dve, B.at(("zpre", ci, s)))
                    sc.activation(out=dump6[:], in_=zpre4[:], func=AF.Square,
                                  accum_out=stat6[:, 2:3]).then_inc(s_act, 1)
                    B.tag(s_act, ("sq6", ci, s))
                    sc.wait_ge(s_dve, B.at(("var", ci, s)))
                    sc.activation(out=stat6[:, 4:5], in_=stat6[:, 3:4], func=AF.Ln)
                    sc.drain()
                    sc.activation(out=stat6[:, 5:6], in_=stat6[:, 4:5], func=AF.Exp,
                                  scale=0.5).then_inc(s_act, 1)
                    B.tag(s_act, ("rstd6", ci, s))

        def f_tensor(te, B):
            te.wait_ge(s_dve, B.at("ident"))
            te.wait_ge(s_c, 32)
            te.matmul(pB[:, 0:512], ones1[:], biasr[0:1, 0:512], start=True, stop=True)
            last = te.matmul(pB[:, 512:1024], ones1[:], biasr[0:1, 512:1024],
                             start=True, stop=True)
            last.then_inc(s_pe, 1)
            B.tag(s_pe, "biasbc")
            for ci, (cs, cw) in enumerate(CH):
                te.wait_ge(s_dve, B.at(("q6", ci)))
                for s in range(4):
                    if (ci, s) != (0, 0):
                        prev = (ci, s - 1) if s > 0 else (ci - 1, 3)
                        te.wait_ge(s_act, B.at(("qev",) + prev))
                    last = None
                    for j in range(8):
                        bank = pT0 if j < 4 else pT1
                        slot = (j % 4) * 128
                        last = te.transpose(bank[:, slot:slot + 128],
                                            q6a[:, s * D + j * 128:s * D + (j + 1) * 128],
                                            ident6[:])
                    last.then_inc(s_pe, 1)
                    B.tag(s_pe, ("qT", ci, s))
                    te.wait_ge(s_act, B.at(("qev", ci, s)))
                    if (ci, s) != (0, 0):
                        prev = (ci, s - 1) if s > 0 else (ci - 1, 3)
                        te.wait_ge(s_dve, B.at(("zpre",) + prev))
                    for j in range(8):
                        te.matmul(pO0[:], q6b[:, j * 128:(j + 1) * 128],
                                  w6T[:, j * D:j * D + 512], start=(j == 0), stop=(j == 7))
                    last = None
                    for j in range(8):
                        last = te.matmul(pO1[:], q6b[:, j * 128:(j + 1) * 128],
                                         w6T[:, j * D + 512:j * D + 1024],
                                         start=(j == 0), stop=(j == 7))
                    last.then_inc(s_pe, 1)
                    B.tag(s_pe, ("out1", ci, s))

        two_pass(block, None, [("sync", f_sync), ("gpsimd", f_gpsimd),
                               ("vector", f_vector), ("scalar", f_scalar),
                               ("tensor", f_tensor)])


def build_stageD(T_loc):
    nc = bass.Bass("TRN2", target_bir_lowering=False)
    io = {
        "x3_in": nc.dram_tensor("x3_in", [T_loc, D], F32, kind="ExternalInput").ap(),
        "w6Tq": nc.dram_tensor("w6Tq", [8, 128, D], BF16, kind="ExternalInput").ap(),
        "bias_row": nc.dram_tensor("bias_row", [1, D], F32, kind="ExternalInput").ap(),
        "x4_out": nc.dram_tensor("x4_out", [T_loc, D], F32, kind="ExternalOutput").ap(),
    }
    emit_stageD(nc, io, T_loc)
    return nc


def emit_c2_pass1(nc, io, T_loc, EX, sst=None):
    ksp, vsp = io["ksp"], io["vsp"]
    c2td = io["c2td"]
    cst_my = io["cst_my"]

    with ExitStack() as st:
        SST = sst if sst is not None else st

        def sb(name, shape, dt=F32):
            return st.enter_context(nc.sbuf_tensor(_tname(name), shape, dt))

        kp = sb("kp", [128, 2 * T_loc])
        vp = sb("vp", [128, 2 * T_loc])
        ek = sb("ek", [128, T_loc])
        ekv = sb("ekv", [128, T_loc])
        As = sb("As", [128, T_loc])
        Bs = sb("Bs", [128, T_loc])
        dmat = sb("dmat", [128, T_loc])
        cvs = sb("cvs", [128, 16])
        w16 = sb("w16", [128, 8])
        st8 = sb("st8", [128, 32])

        s_k = [SST.enter_context(nc.semaphore(_semname())) for i in range(2)]
        s_v = [SST.enter_context(nc.semaphore(_semname())) for i in range(2)]
        s_c = SST.enter_context(nc.semaphore(_semname()))
        s_act = SST.enter_context(nc.semaphore(_semname()))
        s_dve = SST.enter_context(nc.semaphore(_semname()))
        s_st = SST.enter_context(nc.semaphore(_semname()))
        block = st.enter_context(nc.Block())

        def f_sync(sync, B):
            sync.dma_start(out=cvs[:], in_=c2td).then_inc(s_c, 16)
            for j in range(8):
                if j >= 2:
                    sync.wait_ge(s_dve, B.at(("p1", j - 2)))
                sync.dma_start(out=kp[:, (j % 2) * T_loc:(j % 2 + 1) * T_loc],
                               in_=ksp[j * 128:(j + 1) * 128, EX:]).then_inc(s_k[j % 2], 16)
                sync.dma_start(out=vp[:, (j % 2) * T_loc:(j % 2 + 1) * T_loc],
                               in_=vsp[j * 128:(j + 1) * 128, EX:]).then_inc(s_v[j % 2], 16)
            sync.wait_ge(s_dve, B.at(("p1", 7)))
            sync.dma_start(out=cst_my.rearrange("(j p) s -> p j s", p=128),
                           in_=st8[:].rearrange("p (j s) -> p j s", s=4)).then_inc(s_st, 16)
            sync.wait_ge(s_st, 16)

        def f_scalar(sc, B):
            sc.wait_ge(s_c, 16)
            sc.activation(out=w16[:], in_=cvs[:, 0:8], func=AF.Exp)
            sc.drain()
            sc.activation(out=w16[:], in_=w16[:], func=AF.Identity, scale=-1.0)
            sc.drain()
            sc.activation(out=cvs[:, 0:8], in_=w16[:], func=AF.Exp)
            sc.drain().then_inc(s_act, 1)
            B.tag(s_act, "ew1")
            for j in range(8):
                sc.wait_ge(s_k[j % 2], 16 * (j // 2 + 1))
                if j >= 1:
                    sc.wait_ge(s_dve, B.at(("p1", j - 1)))
                sc.activation(out=ek[:], in_=kp[:, (j % 2) * T_loc:(j % 2 + 1) * T_loc],
                              func=AF.Exp)
                sc.drain().then_inc(s_act, 1)
                B.tag(s_act, ("ek1", j))

        def f_vector(v, B):
            v.wait_ge(s_act, B.at("ew1"))
            for j in range(8):
                v.wait_ge(s_act, B.at(("ek1", j)))
                v.wait_ge(s_v[j % 2], 16 * (j // 2 + 1))
                v.memset(dmat[:], 1.0)
                v.drain()
                v.tensor_scalar(out=dmat[:], in0=dmat[:], scalar1=cvs[:, j:j + 1],
                                scalar2=None, op0=ALU.mult)
                v.drain()
                v.tensor_tensor(out=ekv[:], in0=ek[:],
                                in1=vp[:, (j % 2) * T_loc:(j % 2 + 1) * T_loc], op=ALU.mult)
                v.drain()
                v.tensor_tensor_scan(out=As[:], data0=dmat[:], data1=ekv[:], initial=0.0,
                                     op0=ALU.mult, op1=ALU.add)
                v.drain()
                v.tensor_tensor_scan(out=Bs[:], data0=dmat[:], data1=ek[:], initial=0.0,
                                     op0=ALU.mult, op1=ALU.add)
                v.drain()
                v.tensor_copy(st8[:, j * 4 + 0:j * 4 + 1], As[:, T_loc - 1:T_loc])
                v.tensor_copy(st8[:, j * 4 + 1:j * 4 + 2], Bs[:, T_loc - 1:T_loc])
                v.tensor_copy(st8[:, j * 4 + 2:j * 4 + 3], As[:, T_loc - 2:T_loc - 1])
                v.tensor_copy(st8[:, j * 4 + 3:j * 4 + 4], Bs[:, T_loc - 2:T_loc - 1])
                v.drain().then_inc(s_dve, 1)
                B.tag(s_dve, ("p1", j))

        def f_gpsimd(g, B):
            pass

        def f_tensor(te, B):
            pass

        two_pass(block, None, [("sync", f_sync), ("gpsimd", f_gpsimd),
                               ("scalar", f_scalar), ("vector", f_vector),
                               ("tensor", f_tensor)])


def emit_c2_pass2(nc, io, T_loc, EX, sst=None):
    ksp, vsp, ysp = io["ksp"], io["vsp"], io["ysp"]
    c2td, carr = io["c2td"], io["carr"]
    cst_all = io["cst_all"]

    with ExitStack() as st:
        SST = sst if sst is not None else st

        def sb(name, shape, dt=F32):
            return st.enter_context(nc.sbuf_tensor(_tname(name), shape, dt))

        kp = sb("kp", [128, 2 * T_loc])
        vp = sb("vp", [128, 2 * T_loc])
        ek = sb("ek", [128, T_loc])
        ekv = sb("ekv", [128, T_loc])
        As = sb("As", [128, T_loc])
        Bs = sb("Bs", [128, T_loc])
        den = sb("den", [128, T_loc])
        yb = sb("yb", [128, T_loc], BF16)
        dmat = sb("dmat", [128, T_loc])
        cvs = sb("cvs", [128, 16])
        w16 = sb("w16", [128, 8])
        eu16 = sb("eu16", [128, 8])
        carrs = sb("carrs", [128, 40])
        cstl = sb("cstl", [128, 256])
        accs = sb("accs", [128, 32])
        coef = sb("coef", [128, 8])
        coef2 = sb("coef2", [128, 8])
        kcol = sb("kcol", [128, 8])
        vcol = sb("vcol", [128, 8])
        scr8 = sb("scr8", [128, 8])
        ycol = sb("ycol", [128, 8], BF16)
        zer = sb("zer", [128, 8 * (EX - 1)], BF16)

        s_k = [SST.enter_context(nc.semaphore(_semname())) for i in range(2)]
        s_v = [SST.enter_context(nc.semaphore(_semname())) for i in range(2)]
        s_c = SST.enter_context(nc.semaphore(_semname()))
        s_act = SST.enter_context(nc.semaphore(_semname()))
        s_dve = SST.enter_context(nc.semaphore(_semname()))
        s_st = SST.enter_context(nc.semaphore(_semname()))
        block = st.enter_context(nc.Block())

        def f_sync(sync, B):
            sync.dma_start(out=cvs[:], in_=c2td).then_inc(s_c, 16)
            sync.dma_start(out=carrs[:], in_=carr).then_inc(s_c, 16)
            sync.dma_start(out=cstl[:].rearrange("p (r j s) -> p r j s", j=8, s=4),
                           in_=cst_all.rearrange("(r j p) s -> p r j s", p=128, j=8)
                           ).then_inc(s_c, 16)
            sync.dma_start(out=kcol[:].rearrange("p (j o) -> p j o", o=1),
                           in_=ksp[:, EX - 1:EX].rearrange("(j p) o -> p j o", p=128),
                           allow_slow_non_contiguous=True).then_inc(s_c, 16)
            sync.dma_start(out=vcol[:].rearrange("p (j o) -> p j o", o=1),
                           in_=vsp[:, EX - 1:EX].rearrange("(j p) o -> p j o", p=128),
                           allow_slow_non_contiguous=True).then_inc(s_c, 16)
            for j in range(8):
                if j >= 2:
                    sync.wait_ge(s_dve, B.at(("p2", j - 2)))
                sync.dma_start(out=kp[:, (j % 2) * T_loc:(j % 2 + 1) * T_loc],
                               in_=ksp[j * 128:(j + 1) * 128, EX:]).then_inc(s_k[j % 2], 16)
                sync.dma_start(out=vp[:, (j % 2) * T_loc:(j % 2 + 1) * T_loc],
                               in_=vsp[j * 128:(j + 1) * 128, EX:]).then_inc(s_v[j % 2], 16)
                cj = j - 1
                if cj >= 0:
                    sync.wait_ge(s_dve, B.at(("y", cj)))
                    sync.dma_start(out=ysp[cj * 128:(cj + 1) * 128, EX:],
                                   in_=yb[:]).then_inc(s_st, 16)
            sync.wait_ge(s_dve, B.at(("y", 7)))
            sync.dma_start(out=ysp[7 * 128:8 * 128, EX:], in_=yb[:]).then_inc(s_st, 16)
            sync.wait_ge(s_dve, B.at("zer"))
            sync.dma_start(out=ysp[:, 0:EX - 1].rearrange("(j p) t -> p j t", p=128),
                           in_=zer[:].rearrange("p (j t) -> p j t", t=EX - 1)
                           ).then_inc(s_st, 16)
            sync.wait_ge(s_dve, B.at("ycol"))
            sync.dma_start(out=ysp[:, EX - 1:EX].rearrange("(j p) o -> p j o", p=128),
                           in_=ycol[:].rearrange("p (j o) -> p j o", o=1),
                           allow_slow_non_contiguous=True).then_inc(s_st, 16)
            sync.wait_ge(s_st, 16 * 10)

        def f_gpsimd(g, B):
            pass

        def f_scalar(sc, B):
            sc.wait_ge(s_c, 32)
            sc.activation(out=w16[:], in_=cvs[:, 0:8], func=AF.Exp)
            sc.drain()
            sc.activation(out=eu16[:], in_=cvs[:, 8:16], func=AF.Exp)
            sc.drain()
            sc.activation(out=w16[:], in_=w16[:], func=AF.Identity, scale=-1.0)
            sc.drain()
            sc.activation(out=cvs[:, 0:8], in_=w16[:], func=AF.Exp)
            sc.drain().then_inc(s_act, 1)
            B.tag(s_act, "ew2")
            sc.wait_ge(s_c, 48)
            for r in range(8):
                if r >= 1:
                    sc.wait_ge(s_dve, B.at(("acc", r - 1)))
                sc.activation(out=coef[:], in_=w16[:], func=AF.Exp,
                              scale=carrs[:, 8 + r:9 + r])
                sc.drain()
                sc.activation(out=coef2[:], in_=w16[:], func=AF.Exp,
                              scale=carrs[:, 32 + r:33 + r])
                sc.drain().then_inc(s_act, 1)
                B.tag(s_act, ("cexp", r))
            sc.wait_ge(s_c, 80)
            sc.activation(out=kcol[:], in_=kcol[:], func=AF.Exp)
            sc.drain().then_inc(s_act, 1)
            B.tag(s_act, "ekcol")
            for j in range(8):
                sc.wait_ge(s_k[j % 2], 16 * (j // 2 + 1))
                if j >= 1:
                    sc.wait_ge(s_dve, B.at(("p2", j - 1)))
                sc.activation(out=ek[:], in_=kp[:, (j % 2) * T_loc:(j % 2 + 1) * T_loc],
                              func=AF.Exp)
                sc.drain().then_inc(s_act, 1)
                B.tag(s_act, ("ek2", j))

        def f_vector(v, B):
            v.wait_ge(s_act, B.at("ew2"))
            v.memset(zer[:], 0.0)
            v.drain().then_inc(s_dve, 1)
            B.tag(s_dve, "zer")
            v.memset(accs[:], 0.0)
            v.drain()
            cst4 = cstl[:].rearrange("p (r j s) -> p r j s", j=8, s=4)
            for r in range(8):
                v.wait_ge(s_act, B.at(("cexp", r)))
                v.tensor_scalar(out=coef[:], in0=coef[:], scalar1=carrs[:, r:r + 1],
                                scalar2=None, op0=ALU.mult)
                v.tensor_scalar(out=coef2[:], in0=coef2[:], scalar1=carrs[:, 24 + r:25 + r],
                                scalar2=None, op0=ALU.mult)
                v.drain()
                for s_i, acc_off in [(0, 0), (1, 8)]:
                    v.tensor_tensor(out=scr8[:], in0=coef[:], in1=cst4[:, r, :, s_i],
                                    op=ALU.mult)
                    v.drain()
                    v.tensor_tensor(out=accs[:, acc_off:acc_off + 8],
                                    in0=accs[:, acc_off:acc_off + 8], in1=scr8[:], op=ALU.add)
                    v.drain()
                for s_i, acc_off in [(2, 16), (3, 24)]:
                    v.tensor_scalar(out=scr8[:], in0=cst4[:, r, :, s_i],
                                    scalar1=carrs[:, 16 + r:17 + r], scalar2=None, op0=ALU.mult)
                    v.drain()
                    v.tensor_tensor(out=accs[:, acc_off:acc_off + 8],
                                    in0=accs[:, acc_off:acc_off + 8], in1=scr8[:], op=ALU.add)
                    v.drain()
                    v.tensor_tensor(out=scr8[:], in0=coef2[:], in1=cst4[:, r, :, s_i - 2],
                                    op=ALU.mult)
                    v.drain()
                    v.tensor_tensor(out=accs[:, acc_off:acc_off + 8],
                                    in0=accs[:, acc_off:acc_off + 8], in1=scr8[:], op=ALU.add)
                    v.drain()
                v.drain().then_inc(s_dve, 1)
                B.tag(s_dve, ("acc", r))
            v.wait_ge(s_act, B.at("ekcol"))
            v.tensor_tensor(out=vcol[:], in0=kcol[:], in1=vcol[:], op=ALU.mult)
            v.drain()
            v.tensor_tensor(out=kcol[:], in0=kcol[:], in1=eu16[:], op=ALU.mult)
            v.drain()
            v.tensor_tensor(out=vcol[:], in0=vcol[:], in1=eu16[:], op=ALU.mult)
            v.drain()
            v.tensor_tensor(out=vcol[:], in0=vcol[:], in1=accs[:, 16:24], op=ALU.add)
            v.drain()
            v.tensor_tensor(out=kcol[:], in0=kcol[:], in1=accs[:, 24:32], op=ALU.add)
            v.drain()
            v.reciprocal(out=kcol[:], in_=kcol[:])
            v.drain()
            v.tensor_tensor(out=ycol[:], in0=vcol[:], in1=kcol[:], op=ALU.mult)
            v.drain().then_inc(s_dve, 1)
            B.tag(s_dve, "ycol")
            for j in range(8):
                v.wait_ge(s_act, B.at(("ek2", j)))
                v.wait_ge(s_v[j % 2], 16 * (j // 2 + 1))
                v.memset(dmat[:], 1.0)
                v.drain()
                v.tensor_scalar(out=dmat[:], in0=dmat[:], scalar1=cvs[:, j:j + 1],
                                scalar2=None, op0=ALU.mult)
                v.drain()
                v.tensor_tensor(out=ekv[:], in0=ek[:],
                                in1=vp[:, (j % 2) * T_loc:(j % 2 + 1) * T_loc], op=ALU.mult)
                v.drain()
                v.tensor_tensor_scan(out=As[:], data0=dmat[:], data1=ekv[:],
                                     initial=accs[:, 0 + j:1 + j], op0=ALU.mult, op1=ALU.add)
                v.drain()
                v.tensor_tensor_scan(out=Bs[:], data0=dmat[:], data1=ek[:],
                                     initial=accs[:, 8 + j:9 + j], op0=ALU.mult, op1=ALU.add)
                v.drain()
                v.tensor_scalar(out=ek[:], in0=ek[:], scalar1=eu16[:, j:j + 1],
                                scalar2=None, op0=ALU.mult)
                v.drain()
                v.tensor_tensor(out=ekv[:], in0=ek[:],
                                in1=vp[:, (j % 2) * T_loc:(j % 2 + 1) * T_loc], op=ALU.mult)
                v.drain()
                v.tensor_tensor(out=ekv[:, 1:T_loc], in0=ekv[:, 1:T_loc],
                                in1=As[:, 0:T_loc - 1], op=ALU.add)
                v.tensor_tensor(out=ekv[:, 0:1], in0=ekv[:, 0:1], in1=accs[:, j:j + 1],
                                op=ALU.add)
                v.drain()
                v.tensor_tensor(out=den[:, 1:T_loc], in0=ek[:, 1:T_loc],
                                in1=Bs[:, 0:T_loc - 1], op=ALU.add)
                v.tensor_tensor(out=den[:, 0:1], in0=ek[:, 0:1], in1=accs[:, 8 + j:9 + j],
                                op=ALU.add)
                v.drain()
                v.reciprocal(out=den[:], in_=den[:])
                v.drain()
                if j >= 1:
                    v.wait_ge(s_st, 16 * j)
                v.tensor_tensor(out=yb[:], in0=ekv[:], in1=den[:],
                                op=ALU.mult).then_inc(s_dve, 1)
                B.tag(s_dve, ("y", j))
                v.drain().then_inc(s_dve, 1)
                B.tag(s_dve, ("p2", j))

        def f_tensor(te, B):
            pass

        two_pass(block, None, [("sync", f_sync), ("gpsimd", f_gpsimd),
                               ("scalar", f_scalar), ("vector", f_vector),
                               ("tensor", f_tensor)])




def emit_w6_shard(nc, io, sst=None):
    wrow = io["wrow"]
    w6q_my = io["w6q_my"]
    with ExitStack() as st:
        SST = sst if sst is not None else st

        def sb(name, shape, dt=F32):
            return st.enter_context(nc.sbuf_tensor(_tname(name), shape, dt))

        wl = sb("wl", [128, D])
        au = sb("au", [128, D])
        sg = sb("sg", [128, D])
        un = sb("un", [128, D])
        lad = sb("lad", [128, D])
        lad2 = sb("lad2", [128, D])
        stq = sb("stq", [128, D])
        bm = sb("bm", [128, 32])
        rb6 = sb("rb6", [128, 32])
        sc6 = sb("sc6", [128, 32])
        qq = sb("qq", [128, D], BF16)
        scr = dict(un=un, lad=lad, lad2=lad2, stage=stq, bm=bm, rb6=rb6, sc6=sc6)
        s_w = SST.enter_context(nc.semaphore(_semname()))
        s_act = SST.enter_context(nc.semaphore(_semname()))
        s_dve = SST.enter_context(nc.semaphore(_semname()))
        s_st = SST.enter_context(nc.semaphore(_semname()))
        block = st.enter_context(nc.Block())

        def f_sync(sync, B):
            sync.dma_start(out=wl[:], in_=wrow).then_inc(s_w, 16)
            sync.wait_ge(s_dve, 1)
            sync.dma_start(out=w6q_my, in_=qq[:]).then_inc(s_st, 16)
            sync.wait_ge(s_st, 16)

        def f_scalar(sc, B):
            sc.wait_ge(s_w, 16)
            sc.activation(out=au[:], in_=wl[:], func=AF.Abs)
            sc.drain()
            sc.activation(out=sg[:], in_=wl[:], func=AF.Sign)
            sc.drain().then_inc(s_act, 1)

        def f_vector(v, B):
            v.wait_ge(s_act, 1)
            emit_fp6_ladder(v, D, au[:], sg[:], qq[:], scr)
            v.drain().then_inc(s_dve, 1)

        def f_gpsimd(g, B):
            pass

        def f_tensor(te, B):
            pass

        two_pass(block, None, [("sync", f_sync), ("gpsimd", f_gpsimd),
                               ("scalar", f_scalar), ("vector", f_vector),
                               ("tensor", f_tensor)])


def build_fused1(T, T_loc, EX):
    DF = 2816
    nx = T_loc + EX + 32
    TE = T_loc + EX
    nc = bass.Bass("TRN2", target_bir_lowering=False)

    def ein(name, shape, dt=F32):
        return nc.dram_tensor(name, shape, dt, kind="ExternalInput").ap()

    def eout(name, shape, dt=F32):
        return nc.dram_tensor(name, shape, dt, kind="ExternalOutput").ap()

    def intl(name, shape, dt=F32):
        return nc.dram_tensor(name, shape, dt, kind="Internal").ap()

    x_in = ein("x_in", [nx, D])
    c_WT = ein("c_WT", [D, K32])
    c_b32 = ein("c_b32", [K32, 1])
    w_gate = ein("w_gate", [DF, D])
    w_up = ein("w_up", [DF, D])
    w_down = ein("w_down", [D, DF])
    WkT = ein("WkT", [8, 128, D])
    WvT = ein("WvT", [8, 128, D])
    WrT = ein("WrT", [8, 128, D])
    c5 = ein("c5", [128, 40])
    bmask = ein("bmask", [128, 1])
    c2td = ein("c2td", [128, 16])
    x2sp = eout("x2sp", [TE, D])
    ksp = eout("ksp", [D, TE])
    vsp = eout("vsp", [D, TE])
    srsp = eout("srsp", [D, TE], BF16)
    cst_my = eout("cst_my", [1024, 4])
    z1sp = intl("z1sp", [TE, D])
    qxsp = intl("qxsp", [D, TE], BF16)
    flatbuf = intl("flatbuf", [2, 128 * 161])

    wrow = ein("wrow", [128, D])
    w6q_my = eout("w6q_my", [128, D], BF16)
    qwhbm = intl("qwhbm", [68, 128, 1024], BF16)
    wio = (dict(w_gate=w_gate, w_up=w_up, w_down=w_down, qwhbm=qwhbm)
           if (T_loc + EX) // 128 >= 23 else None)
    with ExitStack() as sst:
        emit_w6_shard(nc, dict(wrow=wrow, w6q_my=w6q_my), sst=sst)
        emit_stageA(nc, dict(x_in=x_in, c_WT=c_WT, c_b32=c_b32, z1_out=z1sp,
                             qxb_out=qxsp, flatbuf=flatbuf), T_loc, EX, sst=sst,
                    wio=wio)
        emit_stageB(nc, dict(z1_in=z1sp, qxb_in=qxsp, w_gate=w_gate, w_up=w_up,
                             w_down=w_down, x2_out=x2sp), T_loc, EX, sst=sst,
                    wio=wio)
        emit_stageC1(nc, dict(x2_in=x2sp, WkT=WkT, WvT=WvT, WrT=WrT, c5=c5,
                              bmask=bmask, k_out=ksp, v_out=vsp, sr_out=srsp),
                     T_loc, EX, sst=sst)
        emit_c2_pass1(nc, dict(ksp=ksp, vsp=vsp, c2td=c2td, cst_my=cst_my),
                      T_loc, EX, sst=sst)
    return nc


def build_fused2(T, T_loc, EX):
    F1 = 4096
    TE = T_loc + EX
    nc = bass.Bass("TRN2", target_bir_lowering=False)

    def ein(name, shape, dt=F32):
        return nc.dram_tensor(name, shape, dt, kind="ExternalInput").ap()

    def intl(name, shape, dt=F32):
        return nc.dram_tensor(name, shape, dt, kind="Internal").ap()

    x2sp = ein("x2sp", [TE, D])
    ksp = ein("ksp", [D, TE])
    vsp = ein("vsp", [D, TE])
    srsp = ein("srsp", [D, TE], BF16)
    cst_all = ein("cst_all", [8192, 4])
    c2td = ein("c2td", [128, 16])
    carr = ein("carr", [128, 40])
    WoT = ein("WoT", [8, 128, D])
    FrT = ein("FrT", [8, 128, D])
    c4 = ein("c4", [128, 32])
    bmask = ein("bmask", [128, 1])
    FkT = ein("FkT", [8, 128, F1])
    FvT = ein("FvT", [32, 128, D])
    w6Tq = ein("w6Tq", [8, 128, D], BF16)
    bias_row = ein("bias_row", [1, D])
    x4_out = nc.dram_tensor("x4_out", [T_loc, D], F32, kind="ExternalOutput").ap()
    ysp = intl("ysp", [D, TE], BF16)
    s2msp = intl("s2msp", [D, TE])
    gksp = intl("gksp", [D, TE], BF16)
    sigsp = intl("sigsp", [D, TE], BF16)
    x3sp = intl("x3sp", [T_loc, D])

    with ExitStack() as sst:
        emit_c2_pass2(nc, dict(ksp=ksp, vsp=vsp, ysp=ysp, c2td=c2td, carr=carr,
                               cst_all=cst_all), T_loc, EX, sst=sst)
        emit_stageC3a(nc, dict(x2_in=x2sp, y_in=ysp, sr_in=srsp, WoT=WoT, FrT=FrT,
                               c4=c4, bmask=bmask, s2m_out=s2msp, gk_out=gksp,
                               sig_out=sigsp), T_loc, EX, sst=sst)
        emit_stageC3b(nc, dict(gk_in=gksp[:, EX:], sig_in=sigsp[:, EX:],
                               s2m_in=s2msp[:, EX:], FkT=FkT, FvT=FvT,
                               x3a_out=x3sp), T_loc, sst=sst)
        emit_stageD(nc, dict(x3_in=x3sp, w6Tq=w6Tq, bias_row=bias_row,
                             x4_out=x4_out), T_loc, sst=sst)
    return nc


def fused_inputs(inp, T, cid):
    b, t0, T_loc = shard_tokens(T, cid)
    A = stageA_inputs(inp, T, cid)

    def col8v(vec):
        return np.ascontiguousarray(vec.reshape(8, 128).T)

    c5v = np.concatenate([col8v(inp["L1_attn_ln1_g"]), col8v(inp["L1_attn_ln1_b"]),
                          col8v(inp["L1_attn_tm_k"]), col8v(inp["L1_attn_tm_v"]),
                          col8v(inp["L1_attn_tm_r"])], axis=1)
    c4v = np.concatenate([col8v(inp["L1_attn_ln2_g"]), col8v(inp["L1_attn_ln2_b"]),
                          col8v(inp["L1_attn_fm_k"]), col8v(inp["L1_attn_fm_r"])], axis=1)
    c2v = np.concatenate([col8v(inp["L1_attn_time_decay"]),
                          col8v(inp["L1_attn_time_first"])], axis=1)
    c = cid % 4
    m = np.zeros(8); p = np.zeros(8)
    m2a = np.zeros(8); m2b = np.zeros(8); q = np.zeros(8)
    for r in range(8):
        if r // 4 == cid // 4:
            rp = r % 4
            if rp < c:
                m[r] = 1.0
                p[r] = T_loc * (c - 1 - rp)
            if rp == c - 1:
                m2a[r] = 1.0
            if rp < c - 1:
                m2b[r] = 1.0
                q[r] = T_loc * (c - 1 - rp) - 1
    carrv = np.concatenate([np.tile(v.astype(np.float32), (128, 1))
                            for v in (m, p, m2a, m2b, q)], axis=1)
    msk = np.full((128, 1), 0.0 if cid % 4 == 0 else 1.0, np.float32)
    return {
        "x_in": A["x_in"], "c_WT": A["c_WT"], "c_b32": A["c_b32"],
        "w_gate": inp["L0_ffn_w_gate"], "w_up": inp["L0_ffn_w_up"],
        "w_down": inp["L0_ffn_w_down"],
        "WkT": np.ascontiguousarray(inp["L1_attn_Wk"].T.reshape(8, 128, D)),
        "WvT": np.ascontiguousarray(inp["L1_attn_Wv"].T.reshape(8, 128, D)),
        "WrT": np.ascontiguousarray(inp["L1_attn_Wr"].T.reshape(8, 128, D)),
        "c5": np.ascontiguousarray(c5v), "bmask": msk,
        "WoT": np.ascontiguousarray(inp["L1_attn_Wo"].T.reshape(8, 128, D)),
        "FrT": np.ascontiguousarray(inp["L1_attn_Fr"].T.reshape(8, 128, D)),
        "c4": np.ascontiguousarray(c4v),
        "c2td": np.ascontiguousarray(c2v),
        "carr": np.ascontiguousarray(carrv.astype(np.float32)),
        "FkT": np.ascontiguousarray(inp["L1_attn_Fk"].T.reshape(8, 128, 4096)),
        "FvT": np.ascontiguousarray(inp["L1_attn_Fv"].T.reshape(32, 128, D)),
        "wrow": np.ascontiguousarray(inp["L1_ffn_weight"][cid * 128:(cid + 1) * 128]),
        "bias_row": np.ascontiguousarray(inp["L1_ffn_bias"].reshape(1, D)),
    }


def run_fused(inp, T, n_cores=8, trace=False):
    import ml_dtypes
    T_loc = T // 4
    nc1 = build_fused1(T, T_loc, EX)
    nc2 = build_fused2(T, T_loc, EX)
    im1 = []
    im2 = []
    for cid in range(n_cores):
        m = fused_inputs(inp, T, cid)
        im1.append({k: m[k] for k in ["x_in", "c_WT", "c_b32", "w_gate", "w_up",
                                      "w_down", "WkT", "WvT", "WrT", "c5",
                                      "bmask", "c2td", "wrow"]})
        im2.append({k: m[k] for k in ["c2td", "carr", "WoT", "FrT", "c4", "bmask",
                                      "FkT", "FvT", "bias_row"]})
    res1 = run_bass_kernel_spmd(nc1, im1, core_ids=list(range(n_cores)), trace=trace)
    cst_all = np.concatenate([np.asarray(res1.results[c]["cst_my"])
                              for c in range(n_cores)], axis=0)
    w6q = np.concatenate([np.asarray(res1.results[c]["w6q_my"])
                          for c in range(n_cores)], axis=0)
    w6Tq = np.ascontiguousarray(w6q.T.reshape(8, 128, D))
    for cid in range(n_cores):
        r = res1.results[cid]
        im2[cid].update({"x2sp": np.asarray(r["x2sp"]), "ksp": np.asarray(r["ksp"]),
                         "vsp": np.asarray(r["vsp"]), "srsp": np.asarray(r["srsp"]),
                         "cst_all": cst_all, "w6Tq": w6Tq})
    res2 = run_bass_kernel_spmd(nc2, im2, core_ids=list(range(n_cores)), trace=trace)
    Bb = inp["L0_attn_x"].shape[0]
    out = np.zeros((Bb, T, D), np.float32)
    for cid in range(n_cores):
        b, t0, T_loc = shard_tokens(T, cid)
        out[b, t0:t0 + T_loc] = res2.results[cid]["x4_out"]
    return out, (res1, res2)



_PROG_CACHE = {}


def _prog(name, fn, *args):
    key = (name,) + args
    if key not in _PROG_CACHE:
        _PROG_CACHE[key] = fn(*args)
    return _PROG_CACHE[key]


def kernel(**inp):
    x = inp["L0_attn_x"]
    Bb, T, D_ = x.shape
    T_loc = T // 4
    n_cores = 8
    cores = list(range(n_cores))
    nc1 = _prog("F1", build_fused1, T, T_loc, EX)
    nc2 = _prog("F2", build_fused2, T, T_loc, EX)
    im1 = []
    im2 = []
    for cid in cores:
        m = fused_inputs(inp, T, cid)
        im1.append({k: m[k] for k in ["x_in", "c_WT", "c_b32", "w_gate", "w_up",
                                      "w_down", "WkT", "WvT", "WrT", "c5",
                                      "bmask", "c2td", "wrow"]})
        im2.append({k: m[k] for k in ["c2td", "carr", "WoT", "FrT", "c4", "bmask",
                                      "FkT", "FvT", "bias_row"]})
    res1 = run_bass_kernel_spmd(nc1, im1, core_ids=cores)
    cst_all = np.concatenate([np.asarray(res1.results[c]["cst_my"]) for c in cores],
                             axis=0)
    w6q = np.concatenate([np.asarray(res1.results[c]["w6q_my"]) for c in cores],
                         axis=0)
    w6Tq = np.ascontiguousarray(w6q.T.reshape(8, 128, D))
    for cid in cores:
        r = res1.results[cid]
        im2[cid].update({"x2sp": np.asarray(r["x2sp"]), "ksp": np.asarray(r["ksp"]),
                         "vsp": np.asarray(r["vsp"]), "srsp": np.asarray(r["srsp"]),
                         "cst_all": cst_all, "w6Tq": w6Tq})
    res2 = run_bass_kernel_spmd(nc2, im2, core_ids=cores)
    out = np.zeros((Bb, T, D), np.float32)
    for cid in cores:
        b, t0, T_loc = shard_tokens(T, cid)
        out[b, t0:t0 + T_loc] = res2.results[cid]["x4_out"]
    return out


# version 12
# speedup vs baseline: 1.0354x; 1.0354x over previous
import numpy as np
import concourse.bass as bass
import concourse.mybir as mybir
from concourse.bass_utils import run_bass_kernel_spmd
from contextlib import ExitStack

F32 = mybir.dt.float32
BF16 = mybir.dt.bfloat16
AF = mybir.ActivationFunctionType
ALU = mybir.AluOpType
AX = mybir.AxisListType

D = 1024
K32 = 32
EPS = 1e-5
FP4_MID = [0.25, 0.75, 1.25, 1.75, 2.5, 3.5, 5.0]



_SEMCNT = [0]


def _semname():
    _SEMCNT[0] += 1
    return f"sem{_SEMCNT[0]}"


def _tname(n):
    _SEMCNT[0] += 1
    return f"{n}_{_SEMCNT[0]}"

def bc(ap_small, ap_big):
    a, _ = bass.broadcast_tensor_aps(ap_small, ap_big)
    return a


def _weight_pieces():
    P = []
    for b in range(6):
        for r in range(4 * b, min(4 * b + 4, 22)):
            P.append(("g", r, 0, 1024))
        for r in range(4 * b, min(4 * b + 4, 22)):
            P.append(("u", r, 0, 1024))
    for r in range(8):
        for off, pw in ((0, 1024), (1024, 1024), (2048, 768)):
            P.append(("d", r, off, pw))
    return P


def emit_stageA(nc, io, T_loc, EX, sst=None, wio=None):
    n_ch = (T_loc + EX) // 128
    WP = _weight_pieces() if wio is not None else []

    def wpieces_of(ci):
        if wio is None or ci < 0:
            return []
        lo = (len(WP) * ci) // n_ch
        hi = (len(WP) * (ci + 1)) // n_ch
        return list(range(lo, hi))
    x_in = io["x_in"]
    c_WT = io["c_WT"]
    c_b32 = io["c_b32"]
    z1_out = io["z1_out"]
    qxb_out = io["qxb_out"]
    flatbuf = io["flatbuf"]

    PE_N, ACT_N, DVE_N = 6, 7, 7
    IPE, IACT, IDVE = 0, 0, 2

    def m(ci, k):
        return IPE + ci * PE_N + k

    def a(ci, k):
        return IACT + ci * ACT_N + k

    def d(ci, k):
        return IDVE + ci * DVE_N + k

    with ExitStack() as st:
        def sb(name, shape, dt=F32):
            return st.enter_context(nc.sbuf_tensor(_tname(name), shape, dt))

        def psum(name, shape, dt=F32):
            return st.enter_context(nc.psum_tensor(_tname(name), shape, dt))

        x_a = sb("x_a", [128, 3 * D])
        x_ab = sb("x_ab", [128, 3 * D], BF16)
        x_b = sb("x_b", [128, D], BF16)
        wt_sb = sb("wt_sb", [128, 8 * K32], BF16)
        b32_sb = sb("b32_sb", [K32, 1])
        e_b = sb("e_b", [K32, 128])
        e_pad = sb("e_pad", [128, 2 * 161])
        bt_sb = sb("bt_sb", [128, 2 * 160])
        b1_sb = sb("b1_sb", [128, 128], BF16)
        b2_sb = sb("b2_sb", [K32, 128], BF16)
        rinv = sb("rinv", [128, 1])
        zpre = sb("zpre", [128, D])
        z1n = sb("z1n", [128, 2 * D])
        sqdump = sb("sqdump", [128, D])
        stat = sb("stat", [128, 16])
        au = sb("au", [128, D])
        un = sb("un", [128, D])
        lad = sb("lad", [128, D])
        lad2 = sb("lad2", [128, D])
        sg = sb("sg", [128, D])
        bm = sb("bm", [128, 64])
        rb6 = sb("rb6", [128, 64])
        sc6 = sb("sc6", [128, 64])
        qx_a = sb("qx_a", [128, D], BF16)
        qxb_sb = sb("qxb_sb", [128, D], BF16)
        ident_f = sb("ident_f", [128, 128])
        ident_b = sb("ident_b", [128, 128], BF16)

        pA = psum("pA", [128, 512])
        pB = psum("pB", [128, 512])
        pS = psum("pS", [128, 128])
        pC = psum("pC", [128, 128])
        pAt0 = psum("pAt0", [128, 512])
        pAt1 = psum("pAt1", [128, 512])
        pQ0 = psum("pQ0", [128, 512], BF16)
        pQ1 = psum("pQ1", [128, 512], BF16)

        s_x = [(sst if sst is not None else st).enter_context(nc.semaphore(_semname())) for i in range(3)]
        s_fw = [(sst if sst is not None else st).enter_context(nc.semaphore(_semname())) for i in range(2)]
        s_fr = [(sst if sst is not None else st).enter_context(nc.semaphore(_semname())) for i in range(2)]
        s_pe = (sst if sst is not None else st).enter_context(nc.semaphore(_semname()))
        s_act = (sst if sst is not None else st).enter_context(nc.semaphore(_semname()))
        s_dve = (sst if sst is not None else st).enter_context(nc.semaphore(_semname()))
        s_stz = [(sst if sst is not None else st).enter_context(nc.semaphore(_semname())) for i in range(2)]
        s_stq = (sst if sst is not None else st).enter_context(nc.semaphore(_semname()))
        s_c = (sst if sst is not None else st).enter_context(nc.semaphore(_semname()))
        if wio is not None:
            wloadw = sb("wloadw", [128, 2 * 1024])
            auw = sb("auw", [128, 4 * 1024])
            sgw = sb("sgw", [128, 4 * 1024])
            qww = sb("qww", [128, 4 * 1024], BF16)
            s_wl = [st.enter_context(nc.semaphore(_semname())) for _ in range(2)]
            s_wo = [st.enter_context(nc.semaphore(_semname())) for _ in range(2)]
            s_wab = st.enter_context(nc.semaphore(_semname()))
            s_wq = st.enter_context(nc.semaphore(_semname()))
            w_src = {"g": wio["w_gate"], "u": wio["w_up"], "d": wio["w_down"]}
            qwhbm = wio["qwhbm"]

        block = st.enter_context(nc.Block())

        @block.sync
        def _(sync):
            for ci in range(n_ch + 3):
                if ci < n_ch:
                    if ci >= 3:
                        sync.wait_ge(s_pe, m(ci - 3, 1))
                        sync.wait_ge(s_dve, d(ci - 3, 4))
                    sync.dma_start(out=x_a[:, (ci % 3) * D:(ci % 3 + 1) * D],
                                   in_=x_in[ci * 128:(ci + 1) * 128, :]).then_inc(s_x[ci % 3], 16)
                elif ci == n_ch:
                    sync.wait_ge(s_pe, m(n_ch - 3, 1))
                    sync.wait_ge(s_dve, d(n_ch - 3, 4))
                    sync.dma_start(out=x_a[0:32, (n_ch % 3) * D:(n_ch % 3) * D + D],
                                   in_=x_in[n_ch * 128:n_ch * 128 + 32, :]).then_inc(s_x[n_ch % 3], 16)
                cj = ci - 3
                if 0 <= cj < n_ch:
                    sync.wait_ge(s_dve, d(cj, 6))
                    sync.dma_start(out=z1_out[cj * 128:(cj + 1) * 128, :],
                                   in_=z1n[:, (cj % 2) * D:(cj % 2 + 1) * D]).then_inc(s_stz[cj % 2], 16)

            sync.wait_ge(s_stz[0], 16 * ((n_ch + 1) // 2))
            sync.wait_ge(s_stz[1], 16 * (n_ch // 2))
            sync.wait_ge(s_stq, 16 * n_ch)

        @block.gpsimd
        def _(g):
            g.dma_start(out=wt_sb[:].rearrange("p (j k) -> p j k", k=K32),
                        in_=c_WT.rearrange("(j p) k -> p j k", p=128)).then_inc(s_c, 16)
            g.dma_start(out=b32_sb[:], in_=c_b32).then_inc(s_c, 16)
            g.memset(e_pad[:], 0.0)
            g.iota(sqdump[:, 0:128], [[1, 128]], base=0, channel_multiplier=-1,
                   allow_small_or_imprecise_dtypes=True)
            g.drain().then_inc(s_dve, 1)
            for ci in range(n_ch):
                g.wait_ge(s_dve, d(ci, 2))
                g.dma_start(out=flatbuf[ci % 2, :],
                            in_=e_pad[:, (ci % 2) * 161:(ci % 2) * 161 + 161]).then_inc(s_fw[ci % 2], 16)
                g.wait_ge(s_fw[ci % 2], 16 * (ci // 2 + 1))
                src = bass.AP(flatbuf.tensor, flatbuf.offset + (ci % 2) * (128 * 161),
                              [[160, 128], [1, 160]])
                if ci >= 2:
                    g.wait_ge(s_pe, m(ci - 2, 4))
                g.dma_start(out=bt_sb[:, (ci % 2) * 160:(ci % 2) * 160 + 160],
                            in_=src).then_inc(s_fr[ci % 2], 16)
                def w_out(q):
                    kq, rq, offq, pwq = WP[q]
                    g.wait_ge(s_wq, q + 1)
                    g.dma_start(
                        out=qwhbm[q, :, 0:pwq],
                        in_=qww[:, (q % 4) * 1024:(q % 4) * 1024 + pwq],
                    ).then_inc(s_wo[q % 2], 16)

                for p in wpieces_of(ci):
                    kind, r, off, pw = WP[p]
                    if p >= 2:
                        g.wait_ge(s_wab, p - 1)
                    g.dma_start(
                        out=wloadw[:, (p % 2) * 1024:(p % 2) * 1024 + pw],
                        in_=w_src[kind][r * 128:(r + 1) * 128, off:off + pw],
                    ).then_inc(s_wl[p % 2], 16)
                for q in wpieces_of(ci - 2):
                    w_out(q)
                g.wait_ge(s_act, a(ci, 7))
                g.dma_start(
                    out=qxb_out[:, ci * 128:(ci + 1) * 128].rearrange("(j p) t -> p j t", p=128),
                    in_=qxb_sb[:].rearrange("p (j t) -> p j t", t=128),
                ).then_inc(s_stq, 16)
            if wio is not None:
                for q in wpieces_of(n_ch - 2) + wpieces_of(n_ch - 1):
                    w_out(q)
            g.wait_ge(s_stq, 16 * n_ch)
            if wio is not None:
                g.wait_ge(s_wo[0], 16 * ((len(WP) + 1) // 2))
                g.wait_ge(s_wo[1], 16 * (len(WP) // 2))

        @block.vector
        def _(v):
            v.wait_ge(s_c, 32)
            v.wait_ge(s_dve, 1)
            v.tensor_scalar(out=ident_f[:], in0=sqdump[:, 0:128], scalar1=0.0,
                            scalar2=None, op0=ALU.is_equal)
            v.drain()
            v.tensor_copy(ident_b[:], ident_f[:])
            v.drain().then_inc(s_dve, 1)
            def emit_wlad(p):
                kind, r, off, pw = WP[p]
                v.wait_ge(s_wab, p + 1)
                if p >= 4:
                    v.wait_ge(s_wo[p % 2], 16 * ((p - 4) // 2 + 1))
                emit_fp4_ladder(
                    v, pw, auw[:, (p % 4) * 1024:(p % 4) * 1024 + pw],
                    sgw[:, (p % 4) * 1024:(p % 4) * 1024 + pw],
                    qww[:, (p % 4) * 1024:(p % 4) * 1024 + pw],
                    {"un": un, "lad": lad, "lad2": lad2,
                     "bm": bm, "rb6": rb6, "sc6": sc6},
                ).then_inc(s_wq, 1)

            for ci in range(n_ch):
                wl = wpieces_of(ci - 1) if ci >= 1 else []
                if ci == n_ch - 1:
                    wl = wl + wpieces_of(ci)
                wlq = [[], [], wl]
                if ci == 0:
                    v.wait_ge(s_x[0], 16)
                    v.tensor_copy(x_ab[:, 0:D], x_a[:, 0:D])
                nxt = ci + 1
                v.wait_ge(s_x[nxt % 3], 16 * (nxt // 3 + 1))
                if nxt >= 3:
                    v.wait_ge(s_pe, m(nxt - 3, 5))
                if nxt < n_ch:
                    v.tensor_copy(x_ab[:, (nxt % 3) * D:(nxt % 3 + 1) * D],
                                  x_a[:, (nxt % 3) * D:(nxt % 3 + 1) * D])
                else:
                    v.tensor_copy(x_ab[0:32, (nxt % 3) * D:(nxt % 3) * D + D],
                                  x_a[0:32, (nxt % 3) * D:(nxt % 3) * D + D])
                v.drain().then_inc(s_dve, 1)
                v.wait_ge(s_pe, m(ci, 3))
                if ci >= 2:
                    v.wait_ge(s_fw[ci % 2], 16 * (ci // 2))
                ep = e_pad[:, (ci % 2) * 161:(ci % 2) * 161 + 161]
                v.tensor_copy(ep[:, 0:K32], pS[:, 0:K32])
                v.drain()
                v.tensor_reduce(out=stat[:, 0:1], in_=ep[:, 0:K32], axis=AX.X, op=ALU.add)
                v.drain()
                v.reciprocal(out=rinv[:], in_=stat[:, 0:1]).then_inc(s_dve, 1)
                v.wait_ge(s_pe, m(ci, 4))
                v.tensor_copy(b1_sb[:], pC[:])
                v.tensor_copy(b2_sb[:], pS[0:K32, :]).then_inc(s_dve, 1)
                for p in wlq[0]:
                    emit_wlad(p)
                v.wait_ge(s_pe, m(ci, 5))
                if ci >= 1:
                    v.wait_ge(s_act, a(ci - 1, 3))
                xa = x_a[:, (ci % 3) * D:(ci % 3 + 1) * D]
                v.drain()
                v.scalar_tensor_tensor(out=zpre[:, 0:512], in0=pAt0[:], scalar=rinv[:],
                                       in1=xa[:, 0:512], op0=ALU.mult, op1=ALU.add,
                                       accum_out=stat[:, 0:1])
                v.scalar_tensor_tensor(out=zpre[:, 512:1024], in0=pAt1[:], scalar=rinv[:],
                                       in1=xa[:, 512:1024], op0=ALU.mult, op1=ALU.add,
                                       accum_out=stat[:, 1:2])
                v.drain()
                v.tensor_tensor(out=stat[:, 6:7], in0=stat[:, 0:1], in1=stat[:, 1:2], op=ALU.add)
                v.drain()
                v.tensor_scalar(out=stat[:, 6:7], in0=stat[:, 6:7], scalar1=1.0 / D,
                                scalar2=None, op0=ALU.mult)
                v.drain().then_inc(s_dve, 1)
                v.wait_ge(s_act, a(ci, 3))
                v.tensor_scalar(out=stat[:, 3:4], in0=stat[:, 2:3], scalar1=1.0 / D,
                                scalar2=None, op0=ALU.mult)
                v.tensor_tensor(out=stat[:, 7:8], in0=stat[:, 6:7], in1=stat[:, 6:7], op=ALU.mult)
                v.drain()
                v.tensor_tensor(out=stat[:, 3:4], in0=stat[:, 3:4], in1=stat[:, 7:8], op=ALU.subtract)
                v.drain()
                v.tensor_scalar(out=stat[:, 3:4], in0=stat[:, 3:4], scalar1=EPS,
                                scalar2=None, op0=ALU.add)
                v.drain()
                v.reciprocal(out=stat[:, 3:4], in_=stat[:, 3:4]).then_inc(s_dve, 1)
                for p in wlq[1]:
                    emit_wlad(p)
                v.wait_ge(s_act, a(ci, 4))
                if ci >= 2:
                    v.wait_ge(s_stz[ci % 2], 16 * (ci // 2))
                zn = z1n[:, (ci % 2) * D:(ci % 2 + 1) * D]
                v.tensor_scalar(out=zn, in0=zpre[:], scalar1=stat[:, 6:7],
                                scalar2=stat[:, 5:6], op0=ALU.subtract, op1=ALU.mult).then_inc(s_dve, 1)
                v.wait_ge(s_act, a(ci, 5))
                if ci >= 1:
                    v.wait_ge(s_pe, m(ci - 1, 6))
                emit_fp4_ladder(
                    v, D, au[:], sg[:], qx_a[:],
                    {"un": un, "lad": lad, "lad2": lad2,
                     "bm": bm, "rb6": rb6, "sc6": sc6},
                    out_wait=lambda ci=ci: v.wait_ge(s_act, a(ci, 6)),
                ).then_inc(s_dve, 1)
                for p in wlq[2]:
                    emit_wlad(p)

        @block.scalar
        def _(sc):
            for ci in range(n_ch):
                sc.wait_ge(s_pe, m(ci, 1))
                sc.copy(out=x_b[:, 0:512], in_=pA[:])
                sc.copy(out=x_b[:, 512:1024], in_=pB[:]).then_inc(s_act, 1)
                sc.wait_ge(s_pe, m(ci, 2))
                if ci >= 1:
                    sc.wait_ge(s_pe, m(ci - 1, 3))
                sc.activation(out=e_b[:], in_=pS[0:K32, :], func=AF.Exp,
                              bias=b32_sb[:], scale=1.0).then_inc(s_act, 1)
                sc.wait_ge(s_dve, d(ci, 4))
                sc.activation(out=sqdump[:], in_=zpre[:], func=AF.Square,
                              accum_out=stat[:, 2:3]).then_inc(s_act, 1)
                sc.wait_ge(s_dve, d(ci, 5))
                sc.activation(out=stat[:, 4:5], in_=stat[:, 3:4], func=AF.Ln)
                sc.drain()
                sc.activation(out=stat[:, 5:6], in_=stat[:, 4:5], func=AF.Exp,
                              scale=0.5).then_inc(s_act, 1)
                sc.wait_ge(s_dve, d(ci, 6))
                if ci >= 1:
                    sc.wait_ge(s_dve, d(ci - 1, 7))
                zn = z1n[:, (ci % 2) * D:(ci % 2 + 1) * D]
                sc.activation(out=au[:], in_=zn, func=AF.Abs).then_inc(s_act, 1)
                sc.activation(out=sg[:], in_=zn, func=AF.Sign).then_inc(s_act, 1)
                sc.wait_ge(s_pe, m(ci, 6))
                if ci >= 1:
                    sc.wait_ge(s_stq, 16 * ci)
                sc.copy(out=qxb_sb[:, 0:512], in_=pQ0[:])
                sc.copy(out=qxb_sb[:, 512:1024], in_=pQ1[:]).then_inc(s_act, 1)
                for p in wpieces_of(ci):
                    kind, r, off, pw = WP[p]
                    sc.wait_ge(s_wl[p % 2], 16 * (p // 2 + 1))
                    if p >= 4:
                        sc.wait_ge(s_wq, p - 3)
                    wlv = wloadw[:, (p % 2) * 1024:(p % 2) * 1024 + pw]
                    sc.activation(out=auw[:, (p % 4) * 1024:(p % 4) * 1024 + pw],
                                  in_=wlv, func=AF.Abs)
                    sc.activation(out=sgw[:, (p % 4) * 1024:(p % 4) * 1024 + pw],
                                  in_=wlv, func=AF.Sign).then_inc(s_wab, 1)

        @block.tensor
        def _(te):
            te.wait_ge(s_c, 32)
            te.wait_ge(s_dve, 2)
            for ci in range(n_ch):
                te.wait_ge(s_x[ci % 3], 16 * (ci // 3 + 1))
                if ci >= 1:
                    te.wait_ge(s_act, a(ci - 1, 1))
                xa = x_a[:, (ci % 3) * D:(ci % 3 + 1) * D]
                for j in range(8):
                    dst = (pA if j < 4 else pB)[:, (j % 4) * 128:(j % 4) * 128 + 128]
                    last = te.transpose(dst, xa[:, j * 128:(j + 1) * 128], ident_f[:])
                last.then_inc(s_pe, 1)
                te.wait_ge(s_act, a(ci, 1))
                if ci >= 1:
                    te.wait_ge(s_dve, d(ci - 1, 3))
                    te.wait_ge(s_act, a(ci - 1, 2))
                for j in range(8):
                    last = te.matmul(pS[0:K32, :], wt_sb[:, j * K32:(j + 1) * K32],
                                     x_b[:, j * 128:(j + 1) * 128],
                                     start=(j == 0), stop=(j == 7))
                last.then_inc(s_pe, 1)
                te.wait_ge(s_act, a(ci, 2))
                te.transpose(pS[:, 0:K32], e_b[:], ident_f[0:K32, 0:K32]).then_inc(s_pe, 1)
                te.wait_ge(s_fr[ci % 2], 16 * (ci // 2 + 1))
                te.wait_ge(s_dve, d(ci, 2))
                bt = bt_sb[:, (ci % 2) * 160:(ci % 2) * 160 + 160]
                te.transpose(pC[:], bt[:, 0:128], ident_f[:])
                te.transpose(pS[0:K32, :], bt[:, 128:160], ident_f[:]).then_inc(s_pe, 1)
                te.wait_ge(s_dve, d(ci, 3))
                if ci >= 1:
                    te.wait_ge(s_dve, d(ci - 1, 4))
                xm = x_ab[:, (ci % 3) * D:(ci % 3 + 1) * D]
                xh = x_ab[0:32, ((ci + 1) % 3) * D:((ci + 1) % 3) * D + D]
                for h in range(2):
                    dst = (pAt0 if h == 0 else pAt1)[:]
                    te.matmul(dst, b1_sb[:], xm[:, h * 512:(h + 1) * 512], start=True, stop=False)
                    last = te.matmul(dst, b2_sb[:], xh[:, h * 512:(h + 1) * 512], start=False, stop=True)
                last.then_inc(s_pe, 1)
                te.wait_ge(s_dve, d(ci, 7))
                if ci >= 1:
                    te.wait_ge(s_act, a(ci - 1, 7))
                for j in range(8):
                    dst = (pQ0 if j < 4 else pQ1)[:, (j % 4) * 128:(j % 4) * 128 + 128]
                    last = te.transpose(dst, qx_a[:, j * 128:(j + 1) * 128], ident_b[:])
                last.then_inc(s_pe, 1)


def build_stageA(T_loc, EX):
    nx = T_loc + EX + 32
    nc = bass.Bass("TRN2", target_bir_lowering=False)
    io = {
        "x_in": nc.dram_tensor("x_in", [nx, D], F32, kind="ExternalInput").ap(),
        "c_WT": nc.dram_tensor("c_WT", [D, K32], F32, kind="ExternalInput").ap(),
        "c_b32": nc.dram_tensor("c_b32", [K32, 1], F32, kind="ExternalInput").ap(),
        "z1_out": nc.dram_tensor("z1_out", [T_loc + EX, D], F32, kind="ExternalOutput").ap(),
        "qxb_out": nc.dram_tensor("qxb_out", [D, T_loc + EX], BF16, kind="ExternalOutput").ap(),
        "flatbuf": nc.dram_tensor("flatbuf", [2, 128 * 161], F32, kind="Internal").ap(),
    }
    emit_stageA(nc, io, T_loc, EX)
    return nc



EX = 128


def shard_tokens(T, cid):
    T_loc = T // 4
    b = cid // 4
    pos = cid % 4
    t0 = pos * T_loc
    return b, t0, T_loc


def stageA_inputs(inp, T, cid):
    b, t0, T_loc = shard_tokens(T, cid)
    x = inp["L0_attn_x"][b]
    lo = t0 - EX
    hi = t0 + T_loc + 32
    xs = np.zeros((T_loc + EX + 32, D), np.float32)
    clo = max(lo, 0)
    chi = min(hi, T)
    xs[clo - lo: chi - lo] = x[clo:chi]
    WT = np.ascontiguousarray(inp["L0_attn_W"].T)
    b32 = np.ascontiguousarray(inp["L0_attn_b"].reshape(K32, 1))
    return {"x_in": xs, "c_WT": WT, "c_b32": b32}


def run_stageA(inp, T, n_cores=8, trace=False):
    T_loc = T // 4
    nc = build_stageA(T_loc, EX)
    in_maps = [stageA_inputs(inp, T, cid) for cid in range(n_cores)]
    res = run_bass_kernel_spmd(nc, in_maps, core_ids=list(range(n_cores)), trace=trace)
    import ml_dtypes
    B = inp["L0_attn_x"].shape[0]
    z1 = np.zeros((B, T, D), np.float32)
    qxb = np.zeros((B, D, T), ml_dtypes.bfloat16)
    for cid in range(n_cores):
        b, t0, T_loc = shard_tokens(T, cid)
        r = res.results[cid]
        z1[b, t0:t0 + T_loc] = r["z1_out"][EX:]
        qxb[b, :, t0:t0 + T_loc] = np.asarray(r["qxb_out"])[:, EX:]
    return z1, qxb, res


class Ev:

    def __init__(self):
        self.n = 0
        self.tags = {}

    def inc(self, tag=None):
        self.n += 1
        if tag is not None:
            assert tag not in self.tags, tag
            self.tags[tag] = self.n
        return self.n

    def at(self, tag):
        return self.tags[tag]




def emit_fp4_scales(v, width, au_ap, scr):
    nb = width // 16
    bm = scr["bm"][:, 0:nb]
    rb6 = scr["rb6"][:, 0:nb]
    sc6 = scr["sc6"][:, 0:nb]
    au3 = au_ap.rearrange("p (nb w) -> p nb w", w=16)
    v.tensor_reduce(out=bm, in_=au3, axis=AX.X, op=ALU.max)
    v.drain()
    v.tensor_scalar(out=bm, in0=bm, scalar1=6e-12, scalar2=None, op0=ALU.max)
    v.drain()
    v.reciprocal(out=rb6, in_=bm)
    v.tensor_scalar(out=sc6, in0=bm, scalar1=1.0 / 6.0, scalar2=None, op0=ALU.mult)
    v.drain()
    v.tensor_scalar(out=rb6, in0=rb6, scalar1=6.0, scalar2=None, op0=ALU.mult)
    v.drain()


def emit_fp4_grid(e, width, c0, au_ap, sg_ap, out_ap, scr, un_ap, lad_ap, lad2_ap,
                  use_stt=True):
    nb0, nb = c0 // 16, width // 16
    un = un_ap
    lad = lad_ap
    lad2 = lad2_ap
    au3 = au_ap.rearrange("p (nb w) -> p nb w", w=16)
    rb63 = scr["rb6"][:, nb0:nb0 + nb].rearrange("p (nb w) -> p nb w", w=1)
    sc63 = scr["sc6"][:, nb0:nb0 + nb].rearrange("p (nb w) -> p nb w", w=1)
    un3 = un.rearrange("p (nb w) -> p nb w", w=16)
    e.tensor_tensor(out=un3, in0=au3, in1=bc(rb63, au3), op=ALU.mult)
    if use_stt:
        e.memset(lad2, 0.0)
        e.drain()
        e.scalar_tensor_tensor(out=lad, in0=un, scalar=FP4_MID[0], in1=lad2,
                               op0=ALU.is_gt, op1=ALU.add)
        e.drain()
        for thr in FP4_MID[1:4]:
            e.scalar_tensor_tensor(out=lad, in0=un, scalar=thr, in1=lad,
                                   op0=ALU.is_gt, op1=ALU.add)
            e.drain()
        e.scalar_tensor_tensor(out=lad, in0=lad, scalar=0.5, in1=lad2,
                               op0=ALU.mult, op1=ALU.add)
        e.drain()
        for thr in (FP4_MID[4], FP4_MID[5], FP4_MID[6], FP4_MID[6]):
            e.scalar_tensor_tensor(out=lad, in0=un, scalar=thr, in1=lad,
                                   op0=ALU.is_gt, op1=ALU.add)
            e.drain()
    else:
        e.drain()
        e.tensor_scalar(out=lad, in0=un, scalar1=FP4_MID[0], scalar2=0.5,
                        op0=ALU.is_gt, op1=ALU.mult)
        e.drain()
        steps = [(FP4_MID[1], 0.5), (FP4_MID[2], 0.5), (FP4_MID[3], 0.5),
                 (FP4_MID[4], None), (FP4_MID[5], None), (FP4_MID[6], 2.0)]
        for thr, w in steps:
            if w is None:
                e.tensor_scalar(out=lad2, in0=un, scalar1=thr, scalar2=None,
                                op0=ALU.is_gt)
            else:
                e.tensor_scalar(out=lad2, in0=un, scalar1=thr, scalar2=w,
                                op0=ALU.is_gt, op1=ALU.mult)
            e.drain()
            e.tensor_tensor(out=lad, in0=lad, in1=lad2, op=ALU.add)
            e.drain()
    lad3 = lad.rearrange("p (nb w) -> p nb w", w=16)
    e.tensor_tensor(out=lad3, in0=lad3, in1=bc(sc63, lad3), op=ALU.mult)
    e.drain()
    e.tensor_tensor(out=out_ap, in0=lad, in1=sg_ap, op=ALU.mult)
    e.drain()


def emit_fp4_ladder(v, width, au_ap, sg_ap, out_ap, scr, out_wait=None):
    nb = width // 16
    un = scr["un"][:, 0:width]
    lad = scr["lad"][:, 0:width]
    lad2 = scr["lad2"][:, 0:width]
    bm = scr["bm"][:, 0:nb]
    rb6 = scr["rb6"][:, 0:nb]
    sc6 = scr["sc6"][:, 0:nb]
    au3 = au_ap.rearrange("p (nb w) -> p nb w", w=16)
    v.tensor_reduce(out=bm, in_=au3, axis=AX.X, op=ALU.max)
    v.drain()
    v.tensor_scalar(out=bm, in0=bm, scalar1=6e-12, scalar2=None, op0=ALU.max)
    v.drain()
    v.reciprocal(out=rb6, in_=bm)
    v.tensor_scalar(out=sc6, in0=bm, scalar1=1.0 / 6.0, scalar2=None, op0=ALU.mult)
    v.drain()
    v.tensor_scalar(out=rb6, in0=rb6, scalar1=6.0, scalar2=None, op0=ALU.mult)
    v.drain()
    un3 = un.rearrange("p (nb w) -> p nb w", w=16)
    rb63 = rb6.rearrange("p (nb w) -> p nb w", w=1)
    v.tensor_tensor(out=un3, in0=au3, in1=bc(rb63, au3), op=ALU.mult)
    v.drain()
    v.tensor_scalar(out=lad, in0=un, scalar1=FP4_MID[0], scalar2=None, op0=ALU.is_gt)
    v.tensor_scalar(out=lad2, in0=un, scalar1=FP4_MID[4], scalar2=None, op0=ALU.is_gt)
    v.drain()
    for ta, tb in ((FP4_MID[1], FP4_MID[5]), (FP4_MID[2], FP4_MID[6]),
                   (FP4_MID[3], FP4_MID[6])):
        v.scalar_tensor_tensor(out=lad, in0=un, scalar=ta, in1=lad,
                               op0=ALU.is_gt, op1=ALU.add)
        v.scalar_tensor_tensor(out=lad2, in0=un, scalar=tb, in1=lad2,
                               op0=ALU.is_gt, op1=ALU.add)
        v.drain()
    v.scalar_tensor_tensor(out=lad, in0=lad, scalar=0.5, in1=lad2,
                           op0=ALU.mult, op1=ALU.add)
    v.drain()
    lad3 = lad.rearrange("p (nb w) -> p nb w", w=16)
    sc63 = sc6.rearrange("p (nb w) -> p nb w", w=1)
    v.tensor_tensor(out=lad3, in0=lad3, in1=bc(sc63, lad3), op=ALU.mult)
    v.drain()
    if out_wait is not None:
        out_wait()
    return v.tensor_tensor(out=out_ap, in0=lad, in1=sg_ap, op=ALU.mult)

class _MockIns:
    def __init__(self, book):
        self._book = book

    def then_inc(self, sem, n=1):
        self._book._bump(sem, n)
        return self


class _MockEng:
    def __init__(self, book):
        self._book = book

    def __getattr__(self, name):
        book = self._book

        def f(*a, **k):
            return _MockIns(book)
        return f


class Book:

    def __init__(self):
        self.counts = {}
        self.tags = {}
        self.pass2 = False

    def _bump(self, sem, n):
        if not self.pass2:
            self.counts[id(sem)] = self.counts.get(id(sem), 0) + n

    def tag(self, sem, name):
        if not self.pass2:
            assert name not in self.tags, name
            self.tags[name] = self.counts.get(id(sem), 0)

    def at(self, name):
        if not self.pass2:
            return 0
        return self.tags[name]

    def total(self, sem):
        return self.counts.get(id(sem), 0)


def two_pass(block, engines, builders):
    B = Book()
    for name, fn in builders:
        fn(_MockEng(B), B)
    B.pass2 = True
    for name, fn in builders:
        dec = getattr(block, name)
        dec(lambda eng, fn=fn: fn(eng, B))
    return B


def emit_stageB(nc, io, T_loc, EX, sst=None, wio=None):
    DF = 2816
    NJT = DF // 128
    SL = [512, 512, 512, 512, 512, 256]
    SOFF = [0, 512, 1024, 1536, 2048, 2560]
    n_ch = (T_loc + EX) // 128
    z1_in = io["z1_in"]
    qxb_in = io["qxb_in"]
    w_gate = io["w_gate"]
    w_up = io["w_up"]
    w_down = io["w_down"]
    x2_out = io["x2_out"]

    jobs = ([("g", r, 1024) for r in range(NJT)]
            + [("u", r, 1024) for r in range(NJT)]
            + [("d", r, DF) for r in range(8)])
    WP = _weight_pieces() if wio is not None else []

    def pieces(wd):
        return [(i * 1024, min(1024, wd - i * 1024)) for i in range((wd + 1023) // 1024)]

    def nrounds(wd):
        return (wd // 128 + 7) // 8

    with ExitStack() as st:
        def sb(name, shape, dt=F32):
            return st.enter_context(nc.sbuf_tensor(_tname(name), shape, dt))

        def psum(name, shape, dt=F32):
            return st.enter_context(nc.psum_tensor(_tname(name), shape, dt))

        wgT = sb("wgT", [128, 8 * DF], BF16)
        wuT = sb("wuT", [128, 8 * DF], BF16)
        wdT = sb("wdT", [128, NJT * D], BF16)
        if wio is None:
            wload = sb("wload", [128, DF])
            qw = sb("qw", [128, DF], BF16)
        else:
            qw = sb("qw", [128, 2 * 1024], BF16)
            s_w2 = [(sst if sst is not None else st).enter_context(
                nc.semaphore(_semname())) for _ in range(2)]
            qwhbm = wio["qwhbm"]
        qxc = sb("qxc", [128, 2 * D], BF16)
        z1c = sb("z1c", [128, D])
        tt_sb = sb("tt_sb", [128, 512])
        u2_sb = sb("u2_sb", [128, 512])
        h_sb = sb("h_sb", [128, 1024])
        auq = sb("auq", [128, 1024])
        sgq = sb("sgq", [128, 1024])
        unq = sb("unq", [128, 1024])
        ladq = sb("ladq", [128, 1024])
        lad2q = sb("lad2q", [128, 1024])
        bmq = sb("bmq", [128, 64])
        rb6q = sb("rb6q", [128, 64])
        sc6q = sb("sc6q", [128, 64])
        hq_a = sb("hq_a", [128, DF], BF16)
        hq_b = sb("hq_b", [128, NJT * 128], BF16)
        zpre2 = sb("zpre2", [128, D])
        x2n = sb("x2n", [128, D])
        stat2 = sb("stat2", [128, 16])
        ident_b2 = sb("ident_b2", [128, 128], BF16)
        scr = dict(un=unq, lad=ladq, lad2=lad2q, bm=bmq, rb6=rb6q, sc6=sc6q)

        pG_ = [psum("pG", [128, 512]), psum("pGb", [128, 512])]
        pU_ = [psum("pU", [128, 512]), psum("pUb", [128, 512])]
        pD0 = psum("pD0", [128, 512])
        pD1 = psum("pD1", [128, 512])
        pT0 = psum("pT0", [128, 512], BF16)
        pT1 = psum("pT1", [128, 512], BF16)

        s_w = (sst if sst is not None else st).enter_context(nc.semaphore(_semname()))
        s_q = [(sst if sst is not None else st).enter_context(nc.semaphore(_semname())) for i in range(2)]
        s_z = (sst if sst is not None else st).enter_context(nc.semaphore(_semname()))
        s_pe = (sst if sst is not None else st).enter_context(nc.semaphore(_semname()))
        s_act = (sst if sst is not None else st).enter_context(nc.semaphore(_semname()))
        s_dve = (sst if sst is not None else st).enter_context(nc.semaphore(_semname()))
        s_st = (sst if sst is not None else st).enter_context(nc.semaphore(_semname()))
        s_pool = (sst if sst is not None else st).enter_context(nc.semaphore(_semname()))
        block = st.enter_context(nc.Block())

        def wslot(wi):
            return 0

        def dsplit(w):
            wd = (int(w * 0.64) // 16) * 16
            return wd, w - wd

        def wdest(kind, r, j):
            if kind == "g":
                return wgT[:, j * DF + r * 128: j * DF + r * 128 + 128]
            if kind == "u":
                return wuT[:, j * DF + r * 128: j * DF + r * 128 + 128]
            return wdT[:, j * D + r * 128: j * D + r * 128 + 128]

        def f_sync(sync, B):
            if wio is None:
                for wi, (kind, r, wd) in enumerate(jobs):
                    if wi >= 1:
                        sync.wait_ge(s_dve, B.at(("wq", wi - 1)))
                    srcw = {"g": w_gate, "u": w_up, "d": w_down}[kind][r * 128:(r + 1) * 128, :]
                    sync.dma_start(out=wload[:, wslot(wi):wslot(wi) + wd],
                                   in_=srcw).then_inc(s_w, 16)
            else:
                for p, (kind, r, off, pw) in enumerate(WP):
                    if p >= 2:
                        sync.wait_ge(s_pe, B.at(("wtr", p - 2)))
                    sync.dma_start(out=qw[:, (p % 2) * 1024:(p % 2) * 1024 + pw],
                                   in_=qwhbm[p, :, 0:pw]).then_inc(s_w2[p % 2], 16)
            for ci in range(n_ch):
                if ci >= 1:
                    sync.wait_ge(s_dve, B.at(("zpre", ci - 1)))
                sync.dma_start(out=z1c[:],
                               in_=z1_in[ci * 128:(ci + 1) * 128, :]).then_inc(s_z, 16)
                cj = ci - 1
                if cj >= 0:
                    sync.wait_ge(s_dve, B.at(("x2n", cj)))
                    sync.dma_start(out=x2_out[cj * 128:(cj + 1) * 128, :],
                                   in_=x2n[:]).then_inc(s_st, 16)
            cj = n_ch - 1
            sync.wait_ge(s_dve, B.at(("x2n", cj)))
            sync.dma_start(out=x2_out[cj * 128:(cj + 1) * 128, :],
                           in_=x2n[:]).then_inc(s_st, 16)
            sync.wait_ge(s_st, 16 * n_ch)

        def f_gpsimd(g, B):
            g.iota(unq[:, 0:128], [[1, 128]], base=0, channel_multiplier=-1,
                   allow_small_or_imprecise_dtypes=True)
            g.drain().then_inc(s_dve, 1)
            B.tag(s_dve, "iota")
            for ci in range(n_ch):
                if ci >= 2:
                    g.wait_ge(s_pe, B.at(("gu", ci - 2, len(SL) - 1)))
                g.dma_start(
                    out=qxc[:, (ci % 2) * D:(ci % 2 + 1) * D].rearrange("p (j t) -> p j t", t=128),
                    in_=qxb_in[:, ci * 128:(ci + 1) * 128].rearrange("(j p) t -> p j t", p=128),
                ).then_inc(s_q[ci % 2], 16)

        def f_vector(v, B):
            v.wait_ge(s_dve, B.at("iota"))
            v.tensor_scalar(out=unq[:, 128:256], in0=unq[:, 0:128], scalar1=0.0,
                            scalar2=None, op0=ALU.is_equal)
            v.drain()
            v.tensor_copy(ident_b2[:], unq[:, 128:256])
            v.drain().then_inc(s_dve, 1)
            B.tag(s_dve, "ident")
            if wio is None:
                for wi, (kind, r, wd) in enumerate(jobs):
                    v.wait_ge(s_w, 16 * (wi + 1))
                    if wi >= 1:
                        v.wait_ge(s_pe, B.at(("wt", wi - 1)))
                    for pi, (off, pw) in enumerate(pieces(wd)):
                        v.wait_ge(s_act, B.at(("wabs", wi, pi)))
                        emit_fp4_ladder(v, pw, auq[:, 0:pw], sgq[:, 0:pw],
                                        qw[:, off:off + pw], scr).then_inc(s_dve, 1)
                        B.tag(s_dve, ("wqp", wi, pi))
                    B.tag(s_dve, ("wq", wi))
            for ci in range(n_ch):
                for si, sw in enumerate(SL):
                    v.wait_ge(s_pe, B.at(("gu", ci, si)))
                    v.wait_ge(s_act, B.at(("tanh", ci, si)))
                    hof = (si % 2) * 512
                    v.scalar_tensor_tensor(out=u2_sb[:, 0:sw], in0=tt_sb[:, 0:sw],
                                           scalar=1.0, in1=u2_sb[:, 0:sw],
                                           op0=ALU.add, op1=ALU.mult)
                    v.drain()
                    v.tensor_tensor(out=h_sb[:, hof:hof + sw], in0=u2_sb[:, 0:sw],
                                    in1=pG_[si % 2][:, 0:sw], op=ALU.mult).then_inc(s_dve, 1)
                    B.tag(s_dve, ("h", ci, si))
                    if si % 2 == 1:
                        g_ = si // 2
                        gw = SL[si - 1] + sw
                        v.wait_ge(s_act, B.at(("hasg", ci, g_)))
                        if ci >= 1:
                            v.wait_ge(s_pe, B.at(("ht", ci - 1, g_)))
                        emit_fp4_ladder(v, gw, auq[:, 0:gw], sgq[:, 0:gw],
                                        hq_a[:, SOFF[si - 1]:SOFF[si - 1] + gw], scr).then_inc(s_dve, 1)
                        B.tag(s_dve, ("hqg", ci, g_))
                v.wait_ge(s_pe, B.at(("down", ci)))
                v.wait_ge(s_z, 16 * (ci + 1))
                if ci >= 1:
                    v.wait_ge(s_act, B.at(("sq", ci - 1)))
                z1v = z1c[:]
                v.scalar_tensor_tensor(out=zpre2[:, 0:512], in0=pD0[:], scalar=1.0,
                                       in1=z1v[:, 0:512], op0=ALU.mult, op1=ALU.add,
                                       accum_out=stat2[:, 0:1])
                v.scalar_tensor_tensor(out=zpre2[:, 512:1024], in0=pD1[:], scalar=1.0,
                                       in1=z1v[:, 512:1024], op0=ALU.mult, op1=ALU.add,
                                       accum_out=stat2[:, 1:2])
                v.drain()
                v.tensor_tensor(out=stat2[:, 6:7], in0=stat2[:, 0:1], in1=stat2[:, 1:2], op=ALU.add)
                v.drain()
                v.tensor_scalar(out=stat2[:, 6:7], in0=stat2[:, 6:7], scalar1=1.0 / D,
                                scalar2=None, op0=ALU.mult)
                v.drain().then_inc(s_dve, 1)
                B.tag(s_dve, ("zpre", ci))
                v.wait_ge(s_act, B.at(("sq", ci)))
                v.tensor_scalar(out=stat2[:, 3:4], in0=stat2[:, 2:3], scalar1=1.0 / D,
                                scalar2=None, op0=ALU.mult)
                v.tensor_tensor(out=stat2[:, 7:8], in0=stat2[:, 6:7], in1=stat2[:, 6:7], op=ALU.mult)
                v.drain()
                v.tensor_tensor(out=stat2[:, 3:4], in0=stat2[:, 3:4], in1=stat2[:, 7:8], op=ALU.subtract)
                v.drain()
                v.tensor_scalar(out=stat2[:, 3:4], in0=stat2[:, 3:4], scalar1=EPS,
                                scalar2=None, op0=ALU.add)
                v.drain()
                v.reciprocal(out=stat2[:, 3:4], in_=stat2[:, 3:4]).then_inc(s_dve, 1)
                B.tag(s_dve, ("var", ci))
                v.wait_ge(s_act, B.at(("rstd", ci)))
                if ci >= 1:
                    v.wait_ge(s_st, 16 * ci)
                v.tensor_scalar(out=x2n[:], in0=zpre2[:],
                                scalar1=stat2[:, 6:7], scalar2=stat2[:, 5:6],
                                op0=ALU.subtract, op1=ALU.mult).then_inc(s_dve, 1)
                B.tag(s_dve, ("x2n", ci))

        def f_scalar(sc, B):
            if wio is None:
                for wi, (kind, r, wd) in enumerate(jobs):
                    sc.wait_ge(s_w, 16 * (wi + 1))
                    for pi, (off, pw) in enumerate(pieces(wd)):
                        if pi >= 1:
                            sc.wait_ge(s_dve, B.at(("wqp", wi, pi - 1)))
                        elif wi >= 1:
                            sc.wait_ge(s_dve, B.at(("wq", wi - 1)))
                        src_ap = wload[:, wslot(wi) + off:wslot(wi) + off + pw]
                        sc.activation(out=auq[:, 0:pw], in_=src_ap, func=AF.Abs)
                        sc.activation(out=sgq[:, 0:pw], in_=src_ap, func=AF.Sign).then_inc(s_act, 1)
                        B.tag(s_act, ("wabs", wi, pi))
                    for rd in range(nrounds(wd)):
                        sc.wait_ge(s_pe, B.at(("wtr", wi, rd)))
                        ji = list(range(rd * 8, min(rd * 8 + 8, wd // 128)))
                        for j in ji:
                            bank = pT0 if (j % 8) < 4 else pT1
                            slot = (j % 4) * 128
                            ins = sc.copy(out=wdest(kind, r, j), in_=bank[:, slot:slot + 128])
                            if j == ji[-1]:
                                ins.then_inc(s_act, 1)
                        B.tag(s_act, ("wev", wi, rd))
            else:
                for p, (kind, r, off, pw) in enumerate(WP):
                    sc.wait_ge(s_pe, B.at(("wtr", p)))
                    ji = list(range(pw // 128))
                    for jl in ji:
                        bank = pT0 if jl < 4 else pT1
                        slot = (jl % 4) * 128
                        ins = sc.copy(out=wdest(kind, r, off // 128 + jl),
                                      in_=bank[:, slot:slot + 128])
                        if jl == ji[-1]:
                            ins.then_inc(s_act, 1)
                    B.tag(s_act, ("wev", p))
            for ci in range(n_ch):
                for si, sw in enumerate(SL):
                    sc.wait_ge(s_pe, B.at(("gu", ci, si)))
                    t = ci * len(SL) + si
                    if t >= 1:
                        p1 = ((t - 1) // len(SL), (t - 1) % len(SL))
                        sc.wait_ge(s_dve, B.at(("h",) + p1))
                    sc.activation(out=tt_sb[:, 0:sw], in_=pG_[si % 2][:, 0:sw],
                                  func=AF.Tanh, scale=0.5)
                    sc.activation(out=u2_sb[:, 0:sw], in_=pU_[si % 2][:, 0:sw],
                                  func=AF.Identity, scale=0.5).then_inc(s_act, 1)
                    B.tag(s_act, ("tanh", ci, si))
                    if si % 2 == 1:
                        g_ = si // 2
                        gw = SL[si - 1] + sw
                        sc.wait_ge(s_dve, B.at(("h", ci, si)))
                        if (ci, g_) != (0, 0):
                            prev = (ci, g_ - 1) if g_ > 0 else (ci - 1, 2)
                            sc.wait_ge(s_dve, B.at(("hqg",) + prev))
                        elif wio is None:
                            sc.wait_ge(s_dve, B.at(("wq", len(jobs) - 1)))
                        sc.activation(out=auq[:, 0:gw], in_=h_sb[:, 0:gw], func=AF.Abs)
                        sc.activation(out=sgq[:, 0:gw], in_=h_sb[:, 0:gw], func=AF.Sign).then_inc(s_act, 1)
                        B.tag(s_act, ("hasg", ci, g_))
                for rd in range(3):
                    sc.wait_ge(s_pe, B.at(("ht", ci, rd)))
                    if ci >= 1 and rd == 0:
                        sc.wait_ge(s_pe, B.at(("down", ci - 1)))
                    ji = list(range(rd * 8, min(rd * 8 + 8, NJT)))
                    for j in ji:
                        bank = pT0 if (j % 8) < 4 else pT1
                        slot = (j % 4) * 128
                        ins = sc.copy(out=hq_b[:, j * 128:(j + 1) * 128], in_=bank[:, slot:slot + 128])
                        if j == ji[-1]:
                            ins.then_inc(s_act, 1)
                    B.tag(s_act, ("hev", ci, rd))
                sc.wait_ge(s_dve, B.at(("zpre", ci)))
                sc.activation(out=lad2q[:, 0:D], in_=zpre2[:], func=AF.Square,
                              accum_out=stat2[:, 2:3]).then_inc(s_act, 1)
                B.tag(s_act, ("sq", ci))
                sc.wait_ge(s_dve, B.at(("var", ci)))
                sc.activation(out=stat2[:, 4:5], in_=stat2[:, 3:4], func=AF.Ln)
                sc.drain()
                sc.activation(out=stat2[:, 5:6], in_=stat2[:, 4:5], func=AF.Exp,
                              scale=0.5).then_inc(s_act, 1)
                B.tag(s_act, ("rstd", ci))

        def f_tensor(te, B):
            te.wait_ge(s_dve, B.at("ident"))
            prev_ev = None
            if wio is None:
                for wi, (kind, r, wd) in enumerate(jobs):
                    njt = wd // 128
                    te.wait_ge(s_dve, B.at(("wq", wi)))
                    for rd in range(nrounds(wd)):
                        if prev_ev is not None:
                            te.wait_ge(s_act, B.at(prev_ev))
                        last = None
                        for j in range(rd * 8, min(rd * 8 + 8, njt)):
                            bank = pT0 if (j % 8) < 4 else pT1
                            slot = (j % 4) * 128
                            last = te.transpose(bank[:, slot:slot + 128],
                                                qw[:, j * 128:(j + 1) * 128], ident_b2[:])
                        last.then_inc(s_pe, 1)
                        B.tag(s_pe, ("wtr", wi, rd))
                        prev_ev = ("wev", wi, rd)
                    B.tag(s_pe, ("wt", wi))
            else:
                for p, (kind, r, off, pw) in enumerate(WP):
                    te.wait_ge(s_w2[p % 2], 16 * (p // 2 + 1))
                    if prev_ev is not None:
                        te.wait_ge(s_act, B.at(prev_ev))
                    last = None
                    for jl in range(pw // 128):
                        bank = pT0 if jl < 4 else pT1
                        slot = (jl % 4) * 128
                        last = te.transpose(
                            bank[:, slot:slot + 128],
                            qw[:, (p % 2) * 1024 + jl * 128:(p % 2) * 1024 + (jl + 1) * 128],
                            ident_b2[:])
                    last.then_inc(s_pe, 1)
                    B.tag(s_pe, ("wtr", p))
                    prev_ev = ("wev", p)
            for ci in range(n_ch):
                for si, sw in enumerate(SL):
                    te.wait_ge(s_q[ci % 2], 16 * (ci // 2 + 1))
                    t = ci * len(SL) + si
                    if t >= 2:
                        p2 = ((t - 2) // len(SL), (t - 2) % len(SL))
                        te.wait_ge(s_dve, B.at(("h",) + p2))
                    for j in range(8):
                        lhs = qxc[:, (ci % 2) * D + j * 128:(ci % 2) * D + (j + 1) * 128]
                        te.matmul(pG_[si % 2][:, 0:sw], lhs,
                                  wgT[:, j * DF + SOFF[si]:j * DF + SOFF[si] + sw],
                                  start=(j == 0), stop=(j == 7))
                    last = None
                    for j in range(8):
                        lhs = qxc[:, (ci % 2) * D + j * 128:(ci % 2) * D + (j + 1) * 128]
                        last = te.matmul(pU_[si % 2][:, 0:sw], lhs,
                                         wuT[:, j * DF + SOFF[si]:j * DF + SOFF[si] + sw],
                                         start=(j == 0), stop=(j == 7))
                    last.then_inc(s_pe, 1)
                    B.tag(s_pe, ("gu", ci, si))
                for rd in range(3):
                    te.wait_ge(s_dve, B.at(("hqg", ci, rd)))
                    te.wait_ge(s_act, B.at(prev_ev))
                    last = None
                    for j in range(rd * 8, min(rd * 8 + 8, NJT)):
                        bank = pT0 if (j % 8) < 4 else pT1
                        slot = (j % 4) * 128
                        last = te.transpose(bank[:, slot:slot + 128],
                                            hq_a[:, j * 128:(j + 1) * 128], ident_b2[:])
                    last.then_inc(s_pe, 1)
                    B.tag(s_pe, ("ht", ci, rd))
                    prev_ev = ("hev", ci, rd)
                te.wait_ge(s_act, B.at(("hev", ci, 2)))
                if ci >= 1:
                    te.wait_ge(s_dve, B.at(("zpre", ci - 1)))
                last = None
                for half in range(2):
                    for j in range(NJT):
                        lhs = hq_b[:, j * 128:(j + 1) * 128]
                        last = te.matmul((pD0 if half == 0 else pD1)[:], lhs,
                                         wdT[:, j * D + half * 512:j * D + half * 512 + 512],
                                         start=(j == 0), stop=(j == NJT - 1))
                last.then_inc(s_pe, 1)
                B.tag(s_pe, ("down", ci))

        two_pass(block, None, [("sync", f_sync), ("gpsimd", f_gpsimd),
                               ("vector", f_vector), ("scalar", f_scalar),
                               ("tensor", f_tensor)])


def build_stageB(T_loc, EX):
    DF = 2816
    nc = bass.Bass("TRN2", target_bir_lowering=False)
    io = {
        "z1_in": nc.dram_tensor("z1_in", [T_loc + EX, D], F32, kind="ExternalInput").ap(),
        "qxb_in": nc.dram_tensor("qxb_in", [D, T_loc + EX], BF16, kind="ExternalInput").ap(),
        "w_gate": nc.dram_tensor("w_gate", [DF, D], F32, kind="ExternalInput").ap(),
        "w_up": nc.dram_tensor("w_up", [DF, D], F32, kind="ExternalInput").ap(),
        "w_down": nc.dram_tensor("w_down", [D, DF], F32, kind="ExternalInput").ap(),
        "x2_out": nc.dram_tensor("x2_out", [T_loc + EX, D], F32, kind="ExternalOutput").ap(),
    }
    emit_stageB(nc, io, T_loc, EX)
    return nc


def stageB_inputs(inp, z1_glob, qxb_glob, T, cid):
    b, t0, T_loc = shard_tokens(T, cid)
    lo = t0 - EX
    z1s = np.zeros((T_loc + EX, D), np.float32)
    clo = max(lo, 0)
    z1s[clo - lo:] = z1_glob[b, clo:t0 + T_loc]
    qs = np.zeros((D, T_loc + EX), qxb_glob.dtype)
    qs[:, clo - lo:] = qxb_glob[b, :, clo:t0 + T_loc]
    return {"z1_in": z1s, "qxb_in": qs,
            "w_gate": inp["L0_ffn_w_gate"], "w_up": inp["L0_ffn_w_up"],
            "w_down": inp["L0_ffn_w_down"]}


def chunk_list(T_loc, EX, W=512):
    out = []
    if EX:
        out.append((0, EX))
    for s in range(EX, T_loc + EX, W):
        out.append((s, min(W, T_loc + EX - s)))
    return out


def emit_stageC1(nc, io, T_loc, EX, sst=None):
    CH = chunk_list(T_loc, EX)
    NCH = len(CH)
    x2_in = io["x2_in"]
    WkT = io["WkT"]
    WvT = io["WvT"]
    WrT = io["WrT"]
    c5 = io["c5"]
    bmask = io["bmask"]
    k_out = io["k_out"]
    v_out = io["v_out"]
    sr_out = io["sr_out"]

    with ExitStack() as st:
        def sb(name, shape, dt=F32):
            return st.enter_context(nc.sbuf_tensor(_tname(name), shape, dt))

        def psum(name, shape, dt=F32):
            return st.enter_context(nc.psum_tensor(_tname(name), shape, dt))

        x2c = sb("x2c", [128, 2 * 4 * D])
        hn = sb("hn", [128, 4 * D])
        hb = sb("hb", [128, 8 * 512])
        dcol = sb("dcol", [128, 8 * 512])
        xkb = sb("xkb", [128, 8 * 512], BF16)
        xvb = sb("xvb", [128, 8 * 512], BF16)
        xrb = sb("xrb", [128, 8 * 512], BF16)
        wk_sb = sb("wk_sb", [128, 8 * D], BF16)
        wv_sb = sb("wv_sb", [128, 8 * D], BF16)
        wr_sb = sb("wr_sb", [128, 8 * D], BF16)
        cv = sb("cv", [128, 40])
        maskc = sb("maskc", [128, 1])
        mcol = sb("mcol", [128, 8])
        stat3 = sb("stat3", [128, 16])
        dump3 = sb("dump3", [128, D])
        hprev = sb("hprev", [128, 8])
        st_k = sb("st_k", [128, 8 * 512])
        st_v = sb("st_v", [128, 8 * 512])
        st_s = sb("st_s", [128, 8 * 512], BF16)
        tt3 = sb("tt3", [128, 512])
        ident3 = sb("ident3", [128, 128])

        pX0 = psum("pX0", [128, 512])
        pX1 = psum("pX1", [128, 512])
        pK = psum("pK", [128, 512])
        pV = psum("pV", [128, 512])
        pR = psum("pR", [128, 512])

        s_x = [(sst if sst is not None else st).enter_context(nc.semaphore(_semname())) for i in range(2)]
        s_c = (sst if sst is not None else st).enter_context(nc.semaphore(_semname()))
        s_pe = (sst if sst is not None else st).enter_context(nc.semaphore(_semname()))
        s_act = (sst if sst is not None else st).enter_context(nc.semaphore(_semname()))
        s_dve = (sst if sst is not None else st).enter_context(nc.semaphore(_semname()))
        s_stk = (sst if sst is not None else st).enter_context(nc.semaphore(_semname()))
        s_stv = (sst if sst is not None else st).enter_context(nc.semaphore(_semname()))
        s_str = (sst if sst is not None else st).enter_context(nc.semaphore(_semname()))
        block = st.enter_context(nc.Block())

        def emit_store(sync, B, cj):
            cjs, cjw = CH[cj]
            sync.wait_ge(s_act, B.at(("ev", cj, 7)))
            sync.wait_ge(s_dve, B.at(("sr", cj, 7)))
            sync.dma_start(out=k_out[:, cjs:cjs + cjw].rearrange("(j p) t -> p j t", p=128),
                           in_=st_k[:, 0:8 * cjw].rearrange("p (j t) -> p j t", t=cjw)
                           ).then_inc(s_stk, 16)
            sync.dma_start(out=v_out[:, cjs:cjs + cjw].rearrange("(j p) t -> p j t", p=128),
                           in_=st_v[:, 0:8 * cjw].rearrange("p (j t) -> p j t", t=cjw)
                           ).then_inc(s_stv, 16)
            sync.dma_start(out=sr_out[:, cjs:cjs + cjw].rearrange("(j p) t -> p j t", p=128),
                           in_=st_s[:, 0:8 * cjw].rearrange("p (j t) -> p j t", t=cjw)
                           ).then_inc(s_str, 16)

        def f_sync(sync, B):
            for ci, (cs, cw) in enumerate(CH):
                if ci >= 2:
                    sync.wait_ge(s_dve, B.at(("hn", ci - 2)))
                sync.dma_start(out=x2c[:, (ci % 2) * 4 * D:(ci % 2) * 4 * D + (cw // 128) * D]
                               .rearrange("p (s c) -> p s c", c=D),
                               in_=x2_in[cs:cs + cw, :].rearrange("(s p) c -> p s c", p=128)
                               ).then_inc(s_x[ci % 2], 16)
                if ci >= 1:
                    emit_store(sync, B, ci - 1)
            emit_store(sync, B, NCH - 1)
            sync.wait_ge(s_stk, 16 * NCH)
            sync.wait_ge(s_stv, 16 * NCH)
            sync.wait_ge(s_str, 16 * NCH)

        def f_gpsimd(g, B):
            g.dma_start(out=wk_sb[:].rearrange("p (j c) -> p j c", c=D), in_=WkT.rearrange("j p c -> p j c")).then_inc(s_c, 16)
            g.dma_start(out=wv_sb[:].rearrange("p (j c) -> p j c", c=D), in_=WvT.rearrange("j p c -> p j c")).then_inc(s_c, 16)
            g.dma_start(out=wr_sb[:].rearrange("p (j c) -> p j c", c=D), in_=WrT.rearrange("j p c -> p j c")).then_inc(s_c, 16)
            g.dma_start(out=cv[:], in_=c5).then_inc(s_c, 16)
            g.dma_start(out=maskc[:], in_=bmask).then_inc(s_c, 16)
            g.iota(dump3[:, 0:128], [[1, 128]], base=0, channel_multiplier=-1,
                   allow_small_or_imprecise_dtypes=True)
            g.memset(hprev[:], 0.0)
            g.drain().then_inc(s_dve, 1)
            B.tag(s_dve, "init")

        def f_vector(v, B):
            v.wait_ge(s_dve, B.at("init"))
            v.wait_ge(s_c, 16 * 5)
            v.tensor_scalar(out=ident3[:], in0=dump3[:, 0:128], scalar1=0.0,
                            scalar2=None, op0=ALU.is_equal)
            v.drain().then_inc(s_dve, 1)
            B.tag(s_dve, "ident")
            for ci, (cs, cw) in enumerate(CH):
                ns = cw // 128
                xoff = (ci % 2) * 4 * D
                v.wait_ge(s_x[ci % 2], 16 * (ci // 2 + 1))
                for s in range(ns):
                    xa = x2c[:, xoff + s * D:xoff + (s + 1) * D]
                    v.tensor_reduce(out=mcol[:, s:s + 1], in_=xa, axis=AX.X, op=ALU.add)
                v.drain()
                v.tensor_scalar(out=mcol[:, 0:ns], in0=mcol[:, 0:ns], scalar1=1.0 / D,
                                scalar2=None, op0=ALU.mult)
                v.drain().then_inc(s_dve, 1)
                B.tag(s_dve, ("mean", ci))
                v.wait_ge(s_act, B.at(("sq", ci)))
                v.tensor_scalar(out=stat3[:, 0:ns], in0=stat3[:, 8:8 + ns], scalar1=1.0 / D,
                                scalar2=None, op0=ALU.mult)
                v.tensor_tensor(out=stat3[:, 4:4 + ns], in0=mcol[:, 0:ns], in1=mcol[:, 0:ns],
                                op=ALU.mult)
                v.drain()
                v.tensor_tensor(out=stat3[:, 0:ns], in0=stat3[:, 0:ns], in1=stat3[:, 4:4 + ns],
                                op=ALU.subtract)
                v.drain()
                v.tensor_scalar(out=stat3[:, 0:ns], in0=stat3[:, 0:ns], scalar1=EPS,
                                scalar2=None, op0=ALU.add)
                v.drain()
                v.reciprocal(out=stat3[:, 0:ns], in_=stat3[:, 0:ns])
                v.drain().then_inc(s_dve, 1)
                B.tag(s_dve, ("var", ci))
                v.wait_ge(s_act, B.at(("rstd", ci)))
                if ci >= 1:
                    v.wait_ge(s_pe, B.at(("xT", ci - 1, CH[ci - 1][1] // 128 - 1)))
                for s in range(ns):
                    xa = x2c[:, xoff + s * D:xoff + (s + 1) * D]
                    v.tensor_scalar(out=hn[:, s * D:(s + 1) * D], in0=xa,
                                    scalar1=mcol[:, s:s + 1], scalar2=stat3[:, 4 + s:5 + s],
                                    op0=ALU.subtract, op1=ALU.mult)
                v.drain().then_inc(s_dve, 1)
                B.tag(s_dve, ("hn", ci))
                v.wait_ge(s_act, B.at(("gb", ci, ns - 1)))
                if ci >= 1:
                    v.wait_ge(s_pe, B.at(("kvr", ci - 1, 7)))
                for j in range(8):
                    hj = hb[:, j * cw:(j + 1) * cw]
                    dj = dcol[:, j * cw:(j + 1) * cw]
                    v.tensor_tensor(out=dj[:, 1:cw], in0=hj[:, 1:cw], in1=hj[:, 0:cw - 1],
                                    op=ALU.subtract)
                    v.tensor_tensor(out=dj[:, 0:1], in0=hj[:, 0:1], in1=hprev[:, j:j + 1],
                                    op=ALU.subtract)
                v.drain()
                for mi, outb in [(2, xkb), (3, xvb), (4, xrb)]:
                    for j in range(8):
                        hj = hb[:, j * cw:(j + 1) * cw]
                        dj = dcol[:, j * cw:(j + 1) * cw]
                        ob = outb[:, j * cw:(j + 1) * cw]
                        tm = cv[:, mi * 8 + j:mi * 8 + j + 1]
                        v.scalar_tensor_tensor(out=ob[:, 1:cw], in0=dj[:, 1:cw], scalar=tm,
                                               in1=hj[:, 0:cw - 1], op0=ALU.mult, op1=ALU.add)
                        v.scalar_tensor_tensor(out=ob[:, 0:1], in0=dj[:, 0:1], scalar=tm,
                                               in1=hprev[:, j:j + 1], op0=ALU.mult, op1=ALU.add)
                v.drain()
                for j in range(8):
                    v.tensor_copy(hprev[:, j:j + 1], hb[:, j * cw + cw - 1:j * cw + cw])
                v.drain()
                if ci == 0 and EX:
                    v.tensor_scalar(out=hprev[:], in0=hprev[:], scalar1=maskc[:],
                                    scalar2=None, op0=ALU.mult)
                    v.drain()
                v.engine_nop().then_inc(s_dve, 1)
                B.tag(s_dve, ("mix", ci))
                for dj in range(8):
                    v.wait_ge(s_act, B.at(("ev", ci, dj)))
                    if ci >= 1 and dj == 0:
                        v.wait_ge(s_str, 16 * ci)
                    v.tensor_scalar(out=st_s[:, dj * cw:(dj + 1) * cw], in0=tt3[:, 0:cw],
                                    scalar1=0.5, scalar2=0.5, op0=ALU.mult, op1=ALU.add)
                    v.drain().then_inc(s_dve, 1)
                    B.tag(s_dve, ("sr", ci, dj))

        def f_scalar(sc, B):
            for ci, (cs, cw) in enumerate(CH):
                ns = cw // 128
                xoff = (ci % 2) * 4 * D
                sc.wait_ge(s_x[ci % 2], 16 * (ci // 2 + 1))
                sc.wait_ge(s_dve, B.at(("mean", ci)))
                for s in range(ns):
                    xa = x2c[:, xoff + s * D:xoff + (s + 1) * D]
                    sc.activation(out=dump3[:], in_=xa, func=AF.Square,
                                  accum_out=stat3[:, 8 + s:9 + s])
                    sc.drain()
                sc.drain().then_inc(s_act, 1)
                B.tag(s_act, ("sq", ci))
                sc.wait_ge(s_dve, B.at(("var", ci)))
                sc.activation(out=stat3[:, 4:4 + ns], in_=stat3[:, 0:ns], func=AF.Ln)
                sc.drain()
                sc.activation(out=stat3[:, 4:4 + ns], in_=stat3[:, 4:4 + ns], func=AF.Exp,
                              scale=0.5)
                sc.drain().then_inc(s_act, 1)
                B.tag(s_act, ("rstd", ci))
                for s in range(ns):
                    sc.wait_ge(s_pe, B.at(("xT", ci, s)))
                    for j in range(8):
                        bank = pX0 if j < 4 else pX1
                        slot = (j % 4) * 128
                        sc.activation(out=hb[:, j * cw + s * 128:j * cw + s * 128 + 128],
                                      in_=bank[:, slot:slot + 128], func=AF.Identity,
                                      scale=cv[:, j:j + 1], bias=cv[:, 8 + j:8 + j + 1])
                    sc.drain().then_inc(s_act, 1)
                    B.tag(s_act, ("gb", ci, s))
                for dj in range(8):
                    sc.wait_ge(s_pe, B.at(("kvr", ci, dj)))
                    prevd = (ci, dj - 1) if dj > 0 else ((ci - 1, 7) if ci > 0 else None)
                    if prevd is not None:
                        sc.wait_ge(s_dve, B.at(("sr",) + prevd))
                    if ci >= 1 and dj == 0:
                        sc.wait_ge(s_stk, 16 * ci)
                        sc.wait_ge(s_stv, 16 * ci)
                    sc.copy(out=st_k[:, dj * cw:(dj + 1) * cw], in_=pK[:, 0:cw])
                    sc.copy(out=st_v[:, dj * cw:(dj + 1) * cw], in_=pV[:, 0:cw])
                    sc.drain()
                    sc.activation(out=tt3[:, 0:cw], in_=pR[:, 0:cw], func=AF.Tanh, scale=0.5)
                    sc.drain().then_inc(s_act, 1)
                    B.tag(s_act, ("ev", ci, dj))

        def f_tensor(te, B):
            te.wait_ge(s_c, 16 * 5)
            te.wait_ge(s_dve, B.at("ident"))
            for ci, (cs, cw) in enumerate(CH):
                ns = cw // 128
                te.wait_ge(s_dve, B.at(("hn", ci)))
                for s in range(ns):
                    if ci > 0 or s > 0:
                        prev = (ci, s - 1) if s > 0 else (ci - 1, CH[ci - 1][1] // 128 - 1)
                        te.wait_ge(s_act, B.at(("gb",) + prev))
                    last = None
                    for j in range(8):
                        bank = pX0 if j < 4 else pX1
                        slot = (j % 4) * 128
                        last = te.transpose(bank[:, slot:slot + 128],
                                            hn[:, s * D + j * 128:s * D + (j + 1) * 128],
                                            ident3[:])
                    last.then_inc(s_pe, 1)
                    B.tag(s_pe, ("xT", ci, s))
                te.wait_ge(s_dve, B.at(("mix", ci)))
                for dj in range(8):
                    if True:
                        prev = (ci, dj - 1) if dj > 0 else ((ci - 1, 7) if ci > 0 else None)
                        if prev is not None:
                            te.wait_ge(s_act, B.at(("ev",) + prev))
                            te.wait_ge(s_dve, B.at(("sr",) + prev))
                    for j in range(8):
                        te.matmul(pK[:, 0:cw], wk_sb[:, j * D + dj * 128:j * D + dj * 128 + 128],
                                  xkb[:, j * cw:j * cw + cw], start=(j == 0), stop=(j == 7))
                    for j in range(8):
                        te.matmul(pV[:, 0:cw], wv_sb[:, j * D + dj * 128:j * D + dj * 128 + 128],
                                  xvb[:, j * cw:j * cw + cw], start=(j == 0), stop=(j == 7))
                    last = None
                    for j in range(8):
                        last = te.matmul(pR[:, 0:cw], wr_sb[:, j * D + dj * 128:j * D + dj * 128 + 128],
                                         xrb[:, j * cw:j * cw + cw], start=(j == 0), stop=(j == 7))
                    last.then_inc(s_pe, 1)
                    B.tag(s_pe, ("kvr", ci, dj))

        two_pass(block, None, [("sync", f_sync), ("gpsimd", f_gpsimd),
                               ("vector", f_vector), ("scalar", f_scalar),
                               ("tensor", f_tensor)])


def build_stageC1(T_loc, EX):
    nc = bass.Bass("TRN2", target_bir_lowering=False)
    io = {
        "x2_in": nc.dram_tensor("x2_in", [T_loc + EX, D], F32, kind="ExternalInput").ap(),
        "WkT": nc.dram_tensor("WkT", [8, 128, D], F32, kind="ExternalInput").ap(),
        "WvT": nc.dram_tensor("WvT", [8, 128, D], F32, kind="ExternalInput").ap(),
        "WrT": nc.dram_tensor("WrT", [8, 128, D], F32, kind="ExternalInput").ap(),
        "c5": nc.dram_tensor("c5", [128, 40], F32, kind="ExternalInput").ap(),
        "bmask": nc.dram_tensor("bmask", [128, 1], F32, kind="ExternalInput").ap(),
        "k_out": nc.dram_tensor("k_out", [D, T_loc + EX], F32, kind="ExternalOutput").ap(),
        "v_out": nc.dram_tensor("v_out", [D, T_loc + EX], F32, kind="ExternalOutput").ap(),
        "sr_out": nc.dram_tensor("sr_out", [D, T_loc + EX], BF16, kind="ExternalOutput").ap(),
    }
    emit_stageC1(nc, io, T_loc, EX)
    return nc


def stageC1_inputs(inp, x2_glob, T, cid):
    b, t0, T_loc = shard_tokens(T, cid)
    lo = t0 - EX
    clo = max(lo, 0)
    x2s = np.zeros((T_loc + EX, D), np.float32)
    x2s[clo - lo:] = x2_glob[b, clo:t0 + T_loc]

    def tileT(w):
        return np.ascontiguousarray(w.T.reshape(8, 128, D))

    def col8(vec):
        return np.ascontiguousarray(vec.reshape(8, 128).T)

    c5v = np.concatenate([col8(inp["L1_attn_ln1_g"]), col8(inp["L1_attn_ln1_b"]),
                          col8(inp["L1_attn_tm_k"]), col8(inp["L1_attn_tm_v"]),
                          col8(inp["L1_attn_tm_r"])], axis=1)
    msk = np.full((128, 1), 0.0 if cid % 4 == 0 else 1.0, np.float32)
    return {"x2_in": x2s, "WkT": tileT(inp["L1_attn_Wk"]), "WvT": tileT(inp["L1_attn_Wv"]),
            "WrT": tileT(inp["L1_attn_Wr"]), "c5": np.ascontiguousarray(c5v), "bmask": msk}


def build_stageC2(T, PIECE=4096, sst=None):
    NP = T // PIECE

    nc = bass.Bass("TRN2", target_bir_lowering=False)
    k_in = nc.dram_tensor("k_in", [2, 128, T], F32, kind="ExternalInput").ap()
    v_in = nc.dram_tensor("v_in", [2, 128, T], F32, kind="ExternalInput").ap()
    td = nc.dram_tensor("td", [128, 1], F32, kind="ExternalInput").ap()
    tf = nc.dram_tensor("tf", [128, 1], F32, kind="ExternalInput").ap()
    y_out = nc.dram_tensor("y_out", [2, 128, T], BF16, kind="ExternalOutput").ap()

    with ExitStack() as st:
        def sb(name, shape, dt=F32):
            return st.enter_context(nc.sbuf_tensor(_tname(name), shape, dt))

        kp = sb("kp", [128, 2 * PIECE])
        vp = sb("vp", [128, 2 * PIECE])
        ek = sb("ek", [128, PIECE])
        ekv = sb("ekv", [128, PIECE])
        euk = sb("euk", [128, PIECE])
        As = sb("As", [128, PIECE])
        Bs = sb("Bs", [128, PIECE])
        num = sb("num", [128, PIECE])
        den = sb("den", [128, PIECE])
        yb = sb("yb", [128, PIECE], BF16)
        dmat = sb("dmat", [128, PIECE])
        cvec = sb("cvec", [128, 8])
        s_k = [(sst if sst is not None else st).enter_context(nc.semaphore(_semname())) for i in range(2)]
        s_v = [(sst if sst is not None else st).enter_context(nc.semaphore(_semname())) for i in range(2)]
        s_dve = (sst if sst is not None else st).enter_context(nc.semaphore(_semname()))
        s_act = (sst if sst is not None else st).enter_context(nc.semaphore(_semname()))
        s_st = (sst if sst is not None else st).enter_context(nc.semaphore(_semname()))
        s_c = (sst if sst is not None else st).enter_context(nc.semaphore(_semname()))
        block = st.enter_context(nc.Block())

        def f_sync(sync, B):
            sync.dma_start(out=cvec[:, 0:1], in_=td).then_inc(s_c, 16)
            sync.dma_start(out=cvec[:, 1:2], in_=tf).then_inc(s_c, 16)
            for pi in range(2 * NP):
                b, pp = divmod(pi, NP)
                if pi >= 2:
                    sync.wait_ge(s_dve, B.at(("used", pi - 2)))
                sync.dma_start(out=kp[:, (pi % 2) * PIECE:(pi % 2 + 1) * PIECE],
                               in_=k_in[b, :, pp * PIECE:(pp + 1) * PIECE]).then_inc(s_k[pi % 2], 16)
                sync.dma_start(out=vp[:, (pi % 2) * PIECE:(pi % 2 + 1) * PIECE],
                               in_=v_in[b, :, pp * PIECE:(pp + 1) * PIECE]).then_inc(s_v[pi % 2], 16)
                cj = pi - 1
                if cj >= 0:
                    bj, ppj = divmod(cj, NP)
                    sync.wait_ge(s_dve, B.at(("y", cj)))
                    sync.dma_start(out=y_out[bj, :, ppj * PIECE:(ppj + 1) * PIECE],
                                   in_=yb[:]).then_inc(s_st, 16)
            cj = 2 * NP - 1
            bj, ppj = divmod(cj, NP)
            sync.wait_ge(s_dve, B.at(("y", cj)))
            sync.dma_start(out=y_out[bj, :, ppj * PIECE:(ppj + 1) * PIECE],
                           in_=yb[:]).then_inc(s_st, 16)
            sync.wait_ge(s_st, 16 * 2 * NP)

        def f_scalar(sc, B):
            sc.wait_ge(s_c, 32)
            sc.activation(out=cvec[:, 2:3], in_=cvec[:, 0:1], func=AF.Exp)
            sc.drain()
            sc.activation(out=cvec[:, 2:3], in_=cvec[:, 2:3], func=AF.Exp, scale=-1.0)
            sc.drain()
            sc.activation(out=cvec[:, 3:4], in_=cvec[:, 1:2], func=AF.Exp)
            sc.drain().then_inc(s_act, 1)
            B.tag(s_act, "consts")
            for pi in range(2 * NP):
                sc.wait_ge(s_k[pi % 2], 16 * (pi // 2 + 1))
                if pi >= 1:
                    sc.wait_ge(s_dve, B.at(("ekdone", pi - 1)))
                sc.activation(out=ek[:], in_=kp[:, (pi % 2) * PIECE:(pi % 2 + 1) * PIECE],
                              func=AF.Exp)
                sc.drain().then_inc(s_act, 1)
                B.tag(s_act, ("ek", pi))

        def f_gpsimd(g, B):
            g.memset(dmat[:], 1.0)
            g.memset(cvec[:, 4:7], 0.0)
            g.drain().then_inc(s_dve, 1)
            B.tag(s_dve, "dinit")

        def f_vector(v, B):
            v.wait_ge(s_dve, B.at("dinit"))
            v.wait_ge(s_act, B.at("consts"))
            v.tensor_scalar(out=dmat[:], in0=dmat[:], scalar1=cvec[:, 2:3],
                            scalar2=None, op0=ALU.mult)
            v.drain().then_inc(s_dve, 1)
            B.tag(s_dve, "dmat")
            for pi in range(2 * NP):
                b, pp = divmod(pi, NP)
                slot = (pi % 2) * PIECE
                v.wait_ge(s_act, B.at(("ek", pi)))
                v.wait_ge(s_v[pi % 2], 16 * (pi // 2 + 1))
                v.tensor_scalar(out=euk[:], in0=ek[:], scalar1=cvec[:, 3:4],
                                scalar2=None, op0=ALU.mult)
                v.tensor_tensor(out=ekv[:], in0=ek[:], in1=vp[:, slot:slot + PIECE],
                                op=ALU.mult)
                v.drain()
                if pp == 0:
                    v.tensor_scalar(out=cvec[:, 4:6], in0=cvec[:, 4:6], scalar1=0.0,
                                    scalar2=None, op0=ALU.mult)
                    v.drain()
                v.tensor_tensor_scan(out=As[:], data0=dmat[:], data1=ekv[:],
                                     initial=cvec[:, 4:5], op0=ALU.mult, op1=ALU.add)
                v.drain()
                v.tensor_tensor_scan(out=Bs[:], data0=dmat[:], data1=ek[:],
                                     initial=cvec[:, 5:6], op0=ALU.mult, op1=ALU.add)
                v.drain().then_inc(s_dve, 1)
                B.tag(s_dve, ("ekdone", pi))
                v.tensor_tensor(out=num[:], in0=euk[:], in1=vp[:, slot:slot + PIECE],
                                op=ALU.mult)
                v.drain()
                v.tensor_tensor(out=num[:, 1:PIECE], in0=num[:, 1:PIECE],
                                in1=As[:, 0:PIECE - 1], op=ALU.add)
                v.tensor_tensor(out=num[:, 0:1], in0=num[:, 0:1], in1=cvec[:, 4:5], op=ALU.add)
                v.drain()
                v.tensor_tensor(out=den[:, 1:PIECE], in0=euk[:, 1:PIECE],
                                in1=Bs[:, 0:PIECE - 1], op=ALU.add)
                v.tensor_tensor(out=den[:, 0:1], in0=euk[:, 0:1], in1=cvec[:, 5:6], op=ALU.add)
                v.drain()
                v.tensor_copy(cvec[:, 4:5], As[:, PIECE - 1:PIECE])
                v.tensor_copy(cvec[:, 5:6], Bs[:, PIECE - 1:PIECE])
                v.drain()
                v.reciprocal(out=den[:], in_=den[:])
                v.drain()
                if pi >= 1:
                    v.wait_ge(s_st, 16 * pi)
                v.tensor_tensor(out=yb[:], in0=num[:], in1=den[:], op=ALU.mult)
                v.drain().then_inc(s_dve, 1)
                B.tag(s_dve, ("y", pi))
                v.engine_nop().then_inc(s_dve, 1)
                B.tag(s_dve, ("used", pi))

        two_pass(block, None, [("sync", f_sync), ("gpsimd", f_gpsimd),
                               ("scalar", f_scalar), ("vector", f_vector)])
    return nc


def stageC2_inputs(k_glob, v_glob, inp, cid):
    ch0 = cid * 128
    return {"k_in": np.ascontiguousarray(k_glob[:, ch0:ch0 + 128, :]),
            "v_in": np.ascontiguousarray(v_glob[:, ch0:ch0 + 128, :]),
            "td": np.ascontiguousarray(inp["L1_attn_time_decay"][ch0:ch0 + 128].reshape(128, 1)),
            "tf": np.ascontiguousarray(inp["L1_attn_time_first"][ch0:ch0 + 128].reshape(128, 1))}


def emit_stageC3a(nc, io, T_loc, EX, sst=None):
    CH = chunk_list(T_loc, EX)
    NCH = len(CH)
    x2_in = io["x2_in"]
    y_in = io["y_in"]
    sr_in = io["sr_in"]
    WoT = io["WoT"]
    FrT = io["FrT"]
    c4 = io["c4"]
    bmask = io["bmask"]
    s2m_out = io["s2m_out"]
    gk_out = io["gk_out"]
    sig_out = io["sig_out"]

    with ExitStack() as st:
        def sb(name, shape, dt=F32):
            return st.enter_context(nc.sbuf_tensor(_tname(name), shape, dt))

        def psum(name, shape, dt=F32):
            return st.enter_context(nc.psum_tensor(_tname(name), shape, dt))

        x2c = sb("x2c", [128, 2 * 4 * D])
        yc = sb("yc", [128, 8 * 512], BF16)
        src_ = sb("src_", [128, 8 * 512], BF16)
        rwkvb = sb("rwkvb", [128, 8 * 512], BF16)
        x2b = sb("x2b", [128, 8 * 512])
        xmidb = sb("xmidb", [128, 8 * 512])
        s2m = sb("s2m", [128, 8 * 512])
        gb_ = sb("gb_", [128, 8 * 512])
        dcol = sb("dcol", [128, 8 * 512])
        gkb = sb("gkb", [128, 8 * 512], BF16)
        grb = sb("grb", [128, 8 * 512], BF16)
        sigb = sb("sigb", [128, 8 * 512], BF16)
        wo_sb = sb("wo_sb", [128, 8 * D], BF16)
        fr_sb = sb("fr_sb", [128, 8 * D], BF16)
        cv = sb("cv", [128, 32])
        maskc = sb("maskc", [128, 1])
        gprev = sb("gprev", [128, 8])
        sq512 = sb("sq512", [128, 512])
        rows = sb("rows", [1, 2 * 512])
        mrow2 = sb("mrow2", [1, 1024])
        mbc = sb("mbc", [128, 1024])
        onescol = sb("onescol", [128, 1])
        ones1 = sb("ones1", [1, 128])
        ident4 = sb("ident4", [128, 128])

        pX0 = psum("pX0", [128, 512])
        pX1 = psum("pX1", [128, 512])
        pA0 = psum("pA0", [128, 512])
        pA1 = psum("pA1", [128, 512])
        pSt = psum("pSt", [1, 1024])
        pM = psum("pM", [128, 1024])

        s_x = [(sst if sst is not None else st).enter_context(nc.semaphore(_semname())) for i in range(2)]
        s_y = (sst if sst is not None else st).enter_context(nc.semaphore(_semname()))
        s_r = (sst if sst is not None else st).enter_context(nc.semaphore(_semname()))
        s_c = (sst if sst is not None else st).enter_context(nc.semaphore(_semname()))
        s_pe = (sst if sst is not None else st).enter_context(nc.semaphore(_semname()))
        s_act = (sst if sst is not None else st).enter_context(nc.semaphore(_semname()))
        s_dve = (sst if sst is not None else st).enter_context(nc.semaphore(_semname()))
        s_st1 = (sst if sst is not None else st).enter_context(nc.semaphore(_semname()))
        s_st2 = (sst if sst is not None else st).enter_context(nc.semaphore(_semname()))
        s_st3 = (sst if sst is not None else st).enter_context(nc.semaphore(_semname()))
        block = st.enter_context(nc.Block())

        def emit_store(sync, B, cj):
            cjs, cjw = CH[cj]
            sync.wait_ge(s_dve, B.at(("s2m", cj)))
            sync.dma_start(out=s2m_out[:, cjs:cjs + cjw].rearrange("(j p) t -> p j t", p=128),
                           in_=s2m[:, 0:8 * cjw].rearrange("p (j t) -> p j t", t=cjw)
                           ).then_inc(s_st1, 16)
            sync.wait_ge(s_dve, B.at(("mix", cj)))
            sync.dma_start(out=gk_out[:, cjs:cjs + cjw].rearrange("(j p) t -> p j t", p=128),
                           in_=gkb[:, 0:8 * cjw].rearrange("p (j t) -> p j t", t=cjw)
                           ).then_inc(s_st2, 16)
            sync.wait_ge(s_dve, B.at(("sig", cj, 7)))
            sync.dma_start(out=sig_out[:, cjs:cjs + cjw].rearrange("(j p) t -> p j t", p=128),
                           in_=sigb[:, 0:8 * cjw].rearrange("p (j t) -> p j t", t=cjw)
                           ).then_inc(s_st3, 16)

        def f_sync(sync, B):
            for ci, (cs, cw) in enumerate(CH):
                if ci >= 2:
                    sync.wait_ge(s_dve, B.at(("xmid", ci - 2, 7)))
                sync.dma_start(out=x2c[:, (ci % 2) * 4 * D:(ci % 2) * 4 * D + (cw // 128) * D]
                               .rearrange("p (s c) -> p s c", c=D),
                               in_=x2_in[cs:cs + cw, :].rearrange("(s p) c -> p s c", p=128)
                               ).then_inc(s_x[ci % 2], 16)
                if ci >= 1:
                    sync.wait_ge(s_dve, B.at(("rwkv", ci - 1)))
                sync.dma_start(out=yc[:, 0:8 * cw]
                               .rearrange("p (j t) -> p j t", t=cw),
                               in_=y_in[:, cs:cs + cw].rearrange("(j p) t -> p j t", p=128)
                               ).then_inc(s_y, 16)
                sync.dma_start(out=src_[:, 0:8 * cw]
                               .rearrange("p (j t) -> p j t", t=cw),
                               in_=sr_in[:, cs:cs + cw].rearrange("(j p) t -> p j t", p=128)
                               ).then_inc(s_r, 16)
                if ci >= 1:
                    emit_store(sync, B, ci - 1)
            emit_store(sync, B, NCH - 1)
            sync.wait_ge(s_st1, 16 * NCH)
            sync.wait_ge(s_st2, 16 * NCH)
            sync.wait_ge(s_st3, 16 * NCH)

        def f_gpsimd(g, B):
            g.dma_start(out=wo_sb[:].rearrange("p (j c) -> p j c", c=D),
                        in_=WoT.rearrange("j p c -> p j c")).then_inc(s_c, 16)
            g.dma_start(out=fr_sb[:].rearrange("p (j c) -> p j c", c=D),
                        in_=FrT.rearrange("j p c -> p j c")).then_inc(s_c, 16)
            g.dma_start(out=cv[:], in_=c4).then_inc(s_c, 16)
            g.dma_start(out=maskc[:], in_=bmask).then_inc(s_c, 16)
            g.iota(sq512[:, 0:128], [[1, 128]], base=0, channel_multiplier=-1,
                   allow_small_or_imprecise_dtypes=True)
            g.memset(gprev[:], 0.0)
            g.memset(onescol[:], 1.0)
            g.memset(ones1[:], 1.0)
            g.memset(mrow2[:], 0.0)
            g.memset(rows[:], 0.0)
            g.drain().then_inc(s_dve, 1)
            B.tag(s_dve, "init")

        def f_vector(v, B):
            v.wait_ge(s_dve, B.at("init"))
            v.wait_ge(s_c, 64)
            v.tensor_scalar(out=ident4[:], in0=sq512[:, 0:128], scalar1=0.0,
                            scalar2=None, op0=ALU.is_equal)
            v.drain().then_inc(s_dve, 1)
            B.tag(s_dve, "ident")
            for ci, (cs, cw) in enumerate(CH):
                ns = cw // 128
                v.wait_ge(s_y, 16 * (ci + 1))
                v.wait_ge(s_r, 16 * (ci + 1))
                if ci >= 1:
                    v.wait_ge(s_pe, B.at(("att", ci - 1, 7)))
                yv = yc[:, 0:8 * cw]
                sv = src_[:, 0:8 * cw]
                v.tensor_tensor(out=rwkvb[:, 0:8 * cw], in0=yv, in1=sv,
                                op=ALU.mult).then_inc(s_dve, 1)
                B.tag(s_dve, ("rwkv", ci))
                for c8 in range(8):
                    v.wait_ge(s_pe, B.at(("att", ci, c8)))
                    v.wait_ge(s_act, B.at(("x2ev", ci, ns - 1)))
                    if ci >= 1 and c8 == 0:
                        v.wait_ge(s_pe, B.at(("colsq", ci - 1, 7)))
                    pa = pA0 if c8 % 2 == 0 else pA1
                    v.tensor_tensor(out=xmidb[:, c8 * cw:(c8 + 1) * cw],
                                    in0=x2b[:, c8 * cw:(c8 + 1) * cw], in1=pa[:, 0:cw],
                                    op=ALU.add)
                    v.drain()
                    if ci >= 1 and c8 == 0:
                        v.wait_ge(s_st1, 16 * ci)
                    v.tensor_tensor(out=s2m[:, c8 * cw:(c8 + 1) * cw],
                                    in0=x2b[:, c8 * cw:(c8 + 1) * cw],
                                    in1=xmidb[:, c8 * cw:(c8 + 1) * cw],
                                    op=ALU.add).then_inc(s_dve, 1)
                    B.tag(s_dve, ("xmid", ci, c8))
                B.tag(s_dve, ("s2m", ci))
                v.wait_ge(s_pe, B.at(("colsq", ci, 7)))
                v.tensor_copy(rows[0:1, 0:cw], pSt[0:1, 0:cw])
                v.tensor_copy(rows[0:1, 512:512 + cw], pSt[0:1, 512:512 + cw])
                v.drain()
                v.tensor_scalar(out=mrow2[0:1, 0:cw], in0=rows[0:1, 0:cw], scalar1=1.0 / D,
                                scalar2=None, op0=ALU.mult)
                v.drain()
                v.tensor_scalar(out=rows[0:1, 512:512 + cw], in0=rows[0:1, 512:512 + cw],
                                scalar1=1.0 / D, scalar2=None, op0=ALU.mult)
                v.tensor_tensor(out=rows[0:1, 0:cw], in0=mrow2[0:1, 0:cw],
                                in1=mrow2[0:1, 0:cw], op=ALU.mult)
                v.drain()
                v.tensor_tensor(out=rows[0:1, 512:512 + cw], in0=rows[0:1, 512:512 + cw],
                                in1=rows[0:1, 0:cw], op=ALU.subtract)
                v.drain()
                v.tensor_scalar(out=rows[0:1, 512:512 + cw], in0=rows[0:1, 512:512 + cw],
                                scalar1=EPS, scalar2=None, op0=ALU.add)
                v.drain()
                v.reciprocal(out=rows[0:1, 512:512 + cw],
                             in_=rows[0:1, 512:512 + cw]).then_inc(s_dve, 1)
                B.tag(s_dve, ("rowvar", ci))
                v.wait_ge(s_pe, B.at(("bc", ci)))
                v.tensor_copy(mbc[:], pM[:])
                v.drain()
                mb3 = mbc[:, 0:512].rearrange("p (o t) -> p o t", o=1)[:, :, 0:cw]
                rb3 = mbc[:, 512:1024].rearrange("p (o t) -> p o t", o=1)[:, :, 0:cw]
                xb3 = xmidb[:, 0:8 * cw].rearrange("p (j t) -> p j t", t=cw)
                gb3 = gb_[:, 0:8 * cw].rearrange("p (j t) -> p j t", t=cw)
                v.tensor_tensor(out=gb3, in0=xb3, in1=bc(mb3, xb3), op=ALU.subtract)
                v.drain()
                v.tensor_tensor(out=gb3, in0=gb3, in1=bc(rb3, gb3),
                                op=ALU.mult).then_inc(s_dve, 1)
                B.tag(s_dve, ("gnorm", ci))
                v.wait_ge(s_act, B.at(("gb2", ci)))
                if ci >= 1:
                    v.wait_ge(s_pe, B.at(("fr", ci - 1, 7)))
                    v.wait_ge(s_st2, 16 * ci)
                for j in range(8):
                    hj = gb_[:, j * cw:(j + 1) * cw]
                    dj = dcol[:, j * cw:(j + 1) * cw]
                    v.tensor_tensor(out=dj[:, 1:cw], in0=hj[:, 1:cw], in1=hj[:, 0:cw - 1],
                                    op=ALU.subtract)
                    v.tensor_tensor(out=dj[:, 0:1], in0=hj[:, 0:1], in1=gprev[:, j:j + 1],
                                    op=ALU.subtract)
                v.drain()
                for mi, outb in [(2, gkb), (3, grb)]:
                    for j in range(8):
                        hj = gb_[:, j * cw:(j + 1) * cw]
                        dj = dcol[:, j * cw:(j + 1) * cw]
                        ob = outb[:, j * cw:(j + 1) * cw]
                        tm = cv[:, mi * 8 + j:mi * 8 + j + 1]
                        v.scalar_tensor_tensor(out=ob[:, 1:cw], in0=dj[:, 1:cw], scalar=tm,
                                               in1=hj[:, 0:cw - 1], op0=ALU.mult, op1=ALU.add)
                        v.scalar_tensor_tensor(out=ob[:, 0:1], in0=dj[:, 0:1], scalar=tm,
                                               in1=gprev[:, j:j + 1], op0=ALU.mult, op1=ALU.add)
                v.drain()
                for j in range(8):
                    v.tensor_copy(gprev[:, j:j + 1], gb_[:, j * cw + cw - 1:j * cw + cw])
                v.drain()
                if ci == 0 and EX:
                    v.tensor_scalar(out=gprev[:], in0=gprev[:], scalar1=maskc[:],
                                    scalar2=None, op0=ALU.mult)
                    v.drain()
                v.engine_nop().then_inc(s_dve, 1)
                B.tag(s_dve, ("mix", ci))
                for d8 in range(8):
                    v.wait_ge(s_act, B.at(("frt", ci, d8)))
                    if ci >= 1 and d8 == 0:
                        v.wait_ge(s_st3, 16 * ci)
                    v.tensor_scalar(out=sigb[:, d8 * cw:(d8 + 1) * cw], in0=dcol[:, 0:cw],
                                    scalar1=0.5, scalar2=0.5, op0=ALU.mult,
                                    op1=ALU.add).then_inc(s_dve, 1)
                    B.tag(s_dve, ("sig", ci, d8))

        def f_scalar(sc, B):
            for ci, (cs, cw) in enumerate(CH):
                ns = cw // 128
                for s in range(ns):
                    sc.wait_ge(s_pe, B.at(("xT", ci, s)))
                    for j in range(8):
                        bank = pX0 if j < 4 else pX1
                        slot = (j % 4) * 128
                        ins = sc.copy(out=x2b[:, j * cw + s * 128:j * cw + s * 128 + 128],
                                      in_=bank[:, slot:slot + 128])
                        if j == 7:
                            ins.then_inc(s_act, 1)
                    B.tag(s_act, ("x2ev", ci, s))
                for c8 in range(8):
                    sc.wait_ge(s_dve, B.at(("xmid", ci, c8)))
                    if ci >= 1 or c8 >= 1:
                        prev = (ci, c8 - 1) if c8 > 0 else (ci - 1, 7)
                        sc.wait_ge(s_pe, B.at(("colsq",) + prev))
                    sc.activation(out=sq512[:, 0:cw], in_=xmidb[:, c8 * cw:(c8 + 1) * cw],
                                  func=AF.Square).then_inc(s_act, 1)
                    B.tag(s_act, ("sq", ci, c8))
                sc.wait_ge(s_dve, B.at(("rowvar", ci)))
                sc.activation(out=mrow2[0:1, 512:512 + cw], in_=rows[0:1, 512:512 + cw],
                              func=AF.Ln)
                sc.drain()
                sc.activation(out=mrow2[0:1, 512:512 + cw], in_=mrow2[0:1, 512:512 + cw],
                              func=AF.Exp, scale=0.5).then_inc(s_act, 1)
                B.tag(s_act, ("rstdrow", ci))
                sc.wait_ge(s_dve, B.at(("gnorm", ci)))
                for j in range(8):
                    hj = gb_[:, j * cw:(j + 1) * cw]
                    ins = sc.activation(out=hj, in_=hj, func=AF.Identity,
                                        scale=cv[:, j:j + 1], bias=cv[:, 8 + j:8 + j + 1])
                ins.then_inc(s_act, 1)
                B.tag(s_act, ("gb2", ci))
                for d8 in range(8):
                    sc.wait_ge(s_pe, B.at(("fr", ci, d8)))
                    prevd = (ci, d8 - 1) if d8 > 0 else ((ci - 1, 7) if ci > 0 else None)
                    if prevd is not None:
                        sc.wait_ge(s_dve, B.at(("sig",) + prevd))
                    sc.activation(out=dcol[:, 0:cw], in_=pA0[:, 0:cw], func=AF.Tanh,
                                  scale=0.5).then_inc(s_act, 1)
                    B.tag(s_act, ("frt", ci, d8))

        def f_tensor(te, B):
            te.wait_ge(s_c, 64)
            te.wait_ge(s_dve, B.at("ident"))
            for ci, (cs, cw) in enumerate(CH):
                ns = cw // 128
                xoff = (ci % 2) * 4 * D
                te.wait_ge(s_x[ci % 2], 16 * (ci // 2 + 1))
                for s in range(ns):
                    if ci > 0 or s > 0:
                        prev = (ci, s - 1) if s > 0 else (ci - 1, CH[ci - 1][1] // 128 - 1)
                        te.wait_ge(s_act, B.at(("x2ev",) + prev))
                    if ci >= 1 and s == 0:
                        te.wait_ge(s_dve, B.at(("gnorm", ci - 1)))
                    last = None
                    for j in range(8):
                        bank = pX0 if j < 4 else pX1
                        slot = (j % 4) * 128
                        last = te.transpose(bank[:, slot:slot + 128],
                                            x2c[:, xoff + s * D + j * 128:xoff + s * D + (j + 1) * 128],
                                            ident4[:])
                    last.then_inc(s_pe, 1)
                    B.tag(s_pe, ("xT", ci, s))
                te.wait_ge(s_dve, B.at(("rwkv", ci)))
                te.wait_ge(s_act, B.at(("x2ev", ci, ns - 1)))
                for c8 in range(8):
                    pa = pA0 if c8 % 2 == 0 else pA1
                    if ci >= 1 and c8 == 0:
                        te.wait_ge(s_act, B.at(("frt", ci - 1, 7)))
                    if ci > 0 or c8 > 1:
                        prev = (ci, c8 - 2) if c8 > 1 else (ci - 1, 6 + c8)
                        te.wait_ge(s_dve, B.at(("xmid",) + prev))
                    last = None
                    for dj in range(8):
                        last = te.matmul(pa[:, 0:cw],
                                         wo_sb[:, dj * D + c8 * 128:dj * D + c8 * 128 + 128],
                                         rwkvb[:, dj * cw:(dj + 1) * cw],
                                         start=(dj == 0), stop=(dj == 7))
                    last.then_inc(s_pe, 1)
                    B.tag(s_pe, ("att", ci, c8))
                for c8 in range(8):
                    te.wait_ge(s_dve, B.at(("xmid", ci, c8)))
                    te.wait_ge(s_act, B.at(("sq", ci, c8)))
                    if ci >= 1 and c8 == 0:
                        te.wait_ge(s_dve, B.at(("rowvar", ci - 1)))
                    te.matmul(pSt[0:1, 0:cw], onescol[:], xmidb[:, c8 * cw:(c8 + 1) * cw],
                              start=(c8 == 0), stop=(c8 == 7))
                    last = te.matmul(pSt[0:1, 512:512 + cw], onescol[:], sq512[:, 0:cw],
                                     start=(c8 == 0), stop=(c8 == 7))
                    last.then_inc(s_pe, 1)
                    B.tag(s_pe, ("colsq", ci, c8))
                te.wait_ge(s_act, B.at(("rstdrow", ci)))
                if ci >= 1:
                    te.wait_ge(s_dve, B.at(("gnorm", ci - 1)))
                te.matmul(pM[:, 0:512], ones1[:], mrow2[0:1, 0:512], start=True, stop=True)
                last = te.matmul(pM[:, 512:1024], ones1[:], mrow2[0:1, 512:1024],
                                 start=True, stop=True)
                last.then_inc(s_pe, 1)
                B.tag(s_pe, ("bc", ci))
                te.wait_ge(s_dve, B.at(("mix", ci)))
                for d8 in range(8):
                    prevd = (ci, d8 - 1) if d8 > 0 else ((ci - 1, 7) if ci > 0 else None)
                    if prevd is not None:
                        te.wait_ge(s_act, B.at(("frt",) + prevd))
                    last = None
                    for j in range(8):
                        last = te.matmul(pA0[:, 0:cw],
                                         fr_sb[:, j * D + d8 * 128:j * D + d8 * 128 + 128],
                                         grb[:, j * cw:(j + 1) * cw],
                                         start=(j == 0), stop=(j == 7))
                    last.then_inc(s_pe, 1)
                    B.tag(s_pe, ("fr", ci, d8))

        two_pass(block, None, [("sync", f_sync), ("gpsimd", f_gpsimd),
                               ("vector", f_vector), ("scalar", f_scalar),
                               ("tensor", f_tensor)])


def build_stageC3a(T_loc, EX):
    nc = bass.Bass("TRN2", target_bir_lowering=False)
    io = {
        "x2_in": nc.dram_tensor("x2_in", [T_loc + EX, D], F32, kind="ExternalInput").ap(),
        "y_in": nc.dram_tensor("y_in", [D, T_loc + EX], BF16, kind="ExternalInput").ap(),
        "sr_in": nc.dram_tensor("sr_in", [D, T_loc + EX], BF16, kind="ExternalInput").ap(),
        "WoT": nc.dram_tensor("WoT", [8, 128, D], F32, kind="ExternalInput").ap(),
        "FrT": nc.dram_tensor("FrT", [8, 128, D], F32, kind="ExternalInput").ap(),
        "c4": nc.dram_tensor("c4", [128, 32], F32, kind="ExternalInput").ap(),
        "bmask": nc.dram_tensor("bmask", [128, 1], F32, kind="ExternalInput").ap(),
        "s2m_out": nc.dram_tensor("s2m_out", [D, T_loc + EX], F32, kind="ExternalOutput").ap(),
        "gk_out": nc.dram_tensor("gk_out", [D, T_loc + EX], BF16, kind="ExternalOutput").ap(),
        "sig_out": nc.dram_tensor("sig_out", [D, T_loc + EX], BF16, kind="ExternalOutput").ap(),
    }
    emit_stageC3a(nc, io, T_loc, EX)
    return nc


def stageC3a_inputs(inp, x2_glob, y_glob, sr_core, T, cid):
    b, t0, T_loc = shard_tokens(T, cid)
    lo = t0 - EX
    clo = max(lo, 0)
    x2s = np.zeros((T_loc + EX, D), np.float32)
    x2s[clo - lo:] = x2_glob[b, clo:t0 + T_loc]
    import ml_dtypes
    ys = np.zeros((D, T_loc + EX), ml_dtypes.bfloat16)
    ys[:, clo - lo:] = y_glob[b, :, clo:t0 + T_loc]

    def col8(vec):
        return np.ascontiguousarray(vec.reshape(8, 128).T)

    c4v = np.concatenate([col8(inp["L1_attn_ln2_g"]), col8(inp["L1_attn_ln2_b"]),
                          col8(inp["L1_attn_fm_k"]), col8(inp["L1_attn_fm_r"])], axis=1)
    msk = np.full((128, 1), 0.0 if cid % 4 == 0 else 1.0, np.float32)
    return {"x2_in": x2s, "y_in": ys, "sr_in": sr_core,
            "WoT": np.ascontiguousarray(inp["L1_attn_Wo"].T.reshape(8, 128, D)),
            "FrT": np.ascontiguousarray(inp["L1_attn_Fr"].T.reshape(8, 128, D)),
            "c4": np.ascontiguousarray(c4v), "bmask": msk}


def emit_stageC3b(nc, io, T_loc, sst=None):
    F1 = 4096
    CH = [(s, 512) for s in range(0, T_loc, 512)]
    NCH = len(CH)
    gk_in = io["gk_in"]
    sig_in = io["sig_in"]
    s2m_in = io["s2m_in"]
    FkT = io["FkT"]
    FvT = io["FvT"]
    x3a_out = io["x3a_out"]

    with ExitStack() as st:
        def sb(name, shape, dt=F32):
            return st.enter_context(nc.sbuf_tensor(_tname(name), shape, dt))

        def psum(name, shape, dt=F32):
            return st.enter_context(nc.psum_tensor(_tname(name), shape, dt))

        fk_sb = sb("fk_sb", [128, 8 * F1], BF16)
        fv_sb = sb("fv_sb", [128, 32 * D], BF16)
        gkc = sb("gkc", [128, 4096], BF16)
        sgc = sb("sgc", [128, 4096], BF16)
        smc = sb("smc", [128, 4096])
        kfq = sb("kfq", [128, 32 * 512], BF16)
        sq512 = sb("sq512", [128, 512])
        mrow2 = sb("mrow2", [1, 1024])
        mbc = sb("mbc", [128, 1024])
        rows = mbc[0:1, :]
        x3a = sb("x3a", [128, D])
        x3p = smc
        onescol = sb("onescol", [128, 1])
        ones1 = sb("ones1", [1, 128])
        ident5 = sb("ident5", [128, 128])

        pKf = psum("pKf", [128, 512])
        pKv0 = psum("pKv0", [128, 512])
        pKv1 = psum("pKv1", [128, 512])
        pSt = psum("pSt", [1, 512])
        pM = psum("pM", [128, 1024])
        pX0 = psum("pX0", [128, 512])
        pX1 = psum("pX1", [128, 512])

        s_g = (sst if sst is not None else st).enter_context(nc.semaphore(_semname()))
        s_s = (sst if sst is not None else st).enter_context(nc.semaphore(_semname()))
        s_m = (sst if sst is not None else st).enter_context(nc.semaphore(_semname()))
        s_c = (sst if sst is not None else st).enter_context(nc.semaphore(_semname()))
        s_pe = (sst if sst is not None else st).enter_context(nc.semaphore(_semname()))
        s_act = (sst if sst is not None else st).enter_context(nc.semaphore(_semname()))
        s_dve = (sst if sst is not None else st).enter_context(nc.semaphore(_semname()))
        s_st = (sst if sst is not None else st).enter_context(nc.semaphore(_semname()))
        block = st.enter_context(nc.Block())

        def f_sync(sync, B):
            for ci, (cs, cw) in enumerate(CH):
                if ci >= 1:
                    sync.wait_ge(s_pe, B.at(("kf", ci - 1, 31)))
                sync.dma_start(out=gkc[:, 0:8 * cw]
                               .rearrange("p (j t) -> p j t", t=cw),
                               in_=gk_in[:, cs:cs + cw].rearrange("(j p) t -> p j t", p=128)
                               ).then_inc(s_g, 16)
                if ci >= 1:
                    sync.wait_ge(s_dve, B.at(("x3p", ci - 1, 7)))
                    sync.wait_ge(s_pe, B.at(("x3T", ci - 1, 3)))
                sync.dma_start(out=sgc[:, 0:8 * cw].rearrange("p (j t) -> p j t", t=cw),
                               in_=sig_in[:, cs:cs + cw].rearrange("(j p) t -> p j t", p=128)
                               ).then_inc(s_s, 16)
                sync.dma_start(out=smc[:, 0:8 * cw].rearrange("p (j t) -> p j t", t=cw),
                               in_=s2m_in[:, cs:cs + cw].rearrange("(j p) t -> p j t", p=128)
                               ).then_inc(s_m, 16)
                for s in range(4):
                    cj, sj = (ci, s)
                    sync.wait_ge(s_act, B.at(("x3ev", cj, sj)))
                    sync.dma_start(out=x3a_out[cs + sj * 128:cs + sj * 128 + 128, :],
                                   in_=x3a[:]).then_inc(s_st, 16)
            sync.wait_ge(s_st, 16 * 4 * NCH)

        def f_gpsimd(g, B):
            g.dma_start(out=fk_sb[:].rearrange("p (j f) -> p j f", f=F1),
                        in_=FkT.rearrange("j p f -> p j f")).then_inc(s_c, 16)
            g.dma_start(out=fv_sb[:].rearrange("p (j c) -> p j c", c=D),
                        in_=FvT.rearrange("j p c -> p j c")).then_inc(s_c, 16)
            g.iota(sq512[:, 0:128], [[1, 128]], base=0, channel_multiplier=-1,
                   allow_small_or_imprecise_dtypes=True)
            g.memset(onescol[:], 1.0)
            g.memset(ones1[:], 1.0)
            g.memset(mrow2[:], 0.0)
            g.memset(mbc[:], 0.0)
            g.drain().then_inc(s_dve, 1)
            B.tag(s_dve, "init")

        def f_vector(v, B):
            v.wait_ge(s_dve, B.at("init"))
            v.wait_ge(s_c, 32)
            v.tensor_scalar(out=ident5[:], in0=sq512[:, 0:128], scalar1=0.0,
                            scalar2=None, op0=ALU.is_equal)
            v.drain().then_inc(s_dve, 1)
            B.tag(s_dve, "ident")
            for ci, (cs, cw) in enumerate(CH):
                for fj in range(32):
                    v.wait_ge(s_pe, B.at(("kf", ci, fj)))
                    if ci >= 1 and fj == 0:
                        v.wait_ge(s_pe, B.at(("kv", ci - 1, 7)))
                    pkf = pKf if fj % 2 == 0 else pKv1
                    kslice = kfq[:, fj * cw:(fj + 1) * cw]
                    v.tensor_scalar(out=kslice, in0=pkf[:, 0:cw], scalar1=0.0,
                                    scalar2=None, op0=ALU.max)
                    v.drain()
                    v.tensor_tensor(out=kslice, in0=kslice, in1=kslice,
                                    op=ALU.mult).then_inc(s_dve, 1)
                    B.tag(s_dve, ("kfq", ci, fj))
                v.wait_ge(s_s, 16 * (ci + 1))
                v.wait_ge(s_m, 16 * (ci + 1))
                for c8 in range(8):
                    v.wait_ge(s_pe, B.at(("kv", ci, c8)))
                    if ci >= 1 and c8 == 0:
                        v.wait_ge(s_pe, B.at(("x3T", ci - 1, 3)))
                        v.wait_ge(s_st, 16 * 4 * ci)
                    pkv = pKv0 if c8 % 2 == 0 else pKv1
                    tmp = x3a[:, (c8 % 2) * 512:(c8 % 2) * 512 + cw]
                    v.tensor_tensor(out=tmp, in0=sgc[:, c8 * cw:(c8 + 1) * cw],
                                    in1=pkv[:, 0:cw], op=ALU.mult)
                    v.drain()
                    v.tensor_tensor(out=x3p[:, c8 * cw:(c8 + 1) * cw],
                                    in0=x3p[:, c8 * cw:(c8 + 1) * cw],
                                    in1=tmp, op=ALU.add).then_inc(s_dve, 1)
                    B.tag(s_dve, ("x3p", ci, c8))
                v.wait_ge(s_pe, B.at(("colsq", ci, 7)))
                v.tensor_copy(rows[0:1, 0:cw], pSt[0:1, 0:cw])
                v.tensor_copy(rows[0:1, 512:512 + cw], pM[0:1, 0:cw])
                v.drain()
                v.tensor_scalar(out=mrow2[0:1, 0:cw], in0=rows[0:1, 0:cw], scalar1=1.0 / D,
                                scalar2=None, op0=ALU.mult)
                v.drain()
                v.tensor_scalar(out=rows[0:1, 512:512 + cw], in0=rows[0:1, 512:512 + cw],
                                scalar1=1.0 / D, scalar2=None, op0=ALU.mult)
                v.tensor_tensor(out=rows[0:1, 0:cw], in0=mrow2[0:1, 0:cw],
                                in1=mrow2[0:1, 0:cw], op=ALU.mult)
                v.drain()
                v.tensor_tensor(out=rows[0:1, 512:512 + cw], in0=rows[0:1, 512:512 + cw],
                                in1=rows[0:1, 0:cw], op=ALU.subtract)
                v.drain()
                v.tensor_scalar(out=rows[0:1, 512:512 + cw], in0=rows[0:1, 512:512 + cw],
                                scalar1=EPS, scalar2=None, op0=ALU.add)
                v.drain()
                v.reciprocal(out=rows[0:1, 512:512 + cw],
                             in_=rows[0:1, 512:512 + cw]).then_inc(s_dve, 1)
                B.tag(s_dve, ("rowvar", ci))
                v.wait_ge(s_pe, B.at(("bc", ci)))
                v.tensor_copy(mbc[:], pM[:])
                v.drain()
                mb3 = mbc[:, 0:512].rearrange("p (o t) -> p o t", o=1)[:, :, 0:cw]
                rb3 = mbc[:, 512:1024].rearrange("p (o t) -> p o t", o=1)[:, :, 0:cw]
                xb3 = x3p[:, 0:8 * cw].rearrange("p (j t) -> p j t", t=cw)
                v.tensor_tensor(out=xb3, in0=xb3, in1=bc(mb3, xb3), op=ALU.subtract)
                v.drain()
                v.tensor_tensor(out=xb3, in0=xb3, in1=bc(rb3, xb3),
                                op=ALU.mult).then_inc(s_dve, 1)
                B.tag(s_dve, ("x3n", ci))

        def f_scalar(sc, B):
            for ci, (cs, cw) in enumerate(CH):
                for c8 in range(8):
                    sc.wait_ge(s_dve, B.at(("x3p", ci, c8)))
                    if ci >= 1 or c8 >= 1:
                        prev = (ci, c8 - 1) if c8 > 0 else (ci - 1, 7)
                        sc.wait_ge(s_pe, B.at(("colsq",) + prev))
                    sc.activation(out=sq512[:, 0:cw], in_=x3p[:, c8 * cw:(c8 + 1) * cw],
                                  func=AF.Square).then_inc(s_act, 1)
                    B.tag(s_act, ("sq", ci, c8))
                sc.wait_ge(s_dve, B.at(("rowvar", ci)))
                sc.activation(out=mrow2[0:1, 512:512 + cw], in_=rows[0:1, 512:512 + cw],
                              func=AF.Ln)
                sc.drain()
                sc.activation(out=mrow2[0:1, 512:512 + cw], in_=mrow2[0:1, 512:512 + cw],
                              func=AF.Exp, scale=0.5).then_inc(s_act, 1)
                B.tag(s_act, ("rstdrow", ci))
                for s in range(4):
                    sc.wait_ge(s_pe, B.at(("x3T", ci, s)))
                    if ci > 0 or s > 0:
                        sc.wait_ge(s_st, 16 * (4 * ci + s))
                    for j in range(8):
                        bank = pX0 if j < 4 else pX1
                        slot = (j % 4) * 128
                        ins = sc.copy(out=x3a[:, j * 128:(j + 1) * 128],
                                      in_=bank[:, slot:slot + 128])
                    ins.then_inc(s_act, 1)
                    B.tag(s_act, ("x3ev", ci, s))

        def f_tensor(te, B):
            te.wait_ge(s_c, 32)
            te.wait_ge(s_dve, B.at("ident"))
            for ci, (cs, cw) in enumerate(CH):
                te.wait_ge(s_g, 16 * (ci + 1))
                for fj in range(32):
                    if ci > 0 or fj > 1:
                        prevf = (ci, fj - 2) if fj > 1 else (ci - 1, 30 + fj)
                        te.wait_ge(s_dve, B.at(("kfq",) + prevf))
                    pkf = pKf if fj % 2 == 0 else pKv1
                    last = None
                    for j in range(8):
                        last = te.matmul(pkf[:, 0:cw],
                                         fk_sb[:, j * F1 + fj * 128:j * F1 + fj * 128 + 128],
                                         gkc[:, j * cw:(j + 1) * cw],
                                         start=(j == 0), stop=(j == 7))
                    last.then_inc(s_pe, 1)
                    B.tag(s_pe, ("kf", ci, fj))
                te.wait_ge(s_dve, B.at(("kfq", ci, 31)))
                for c8 in range(8):
                    pkv = pKv0 if c8 % 2 == 0 else pKv1
                    if ci > 0 or c8 > 1:
                        prev = (ci, c8 - 2) if c8 > 1 else (ci - 1, 6 + c8)
                        te.wait_ge(s_dve, B.at(("x3p",) + prev))
                    last = None
                    for fj in range(32):
                        last = te.matmul(pkv[:, 0:cw],
                                         fv_sb[:, fj * D + c8 * 128:fj * D + c8 * 128 + 128],
                                         kfq[:, fj * cw:(fj + 1) * cw],
                                         start=(fj == 0), stop=(fj == 31))
                    last.then_inc(s_pe, 1)
                    B.tag(s_pe, ("kv", ci, c8))
                for c8 in range(8):
                    te.wait_ge(s_dve, B.at(("x3p", ci, c8)))
                    te.wait_ge(s_act, B.at(("sq", ci, c8)))
                    if ci >= 1 and c8 == 0:
                        te.wait_ge(s_dve, B.at(("x3n", ci - 1)))
                    te.matmul(pSt[0:1, 0:cw], onescol[:], x3p[:, c8 * cw:(c8 + 1) * cw],
                              start=(c8 == 0), stop=(c8 == 7))
                    last = te.matmul(pM[0:1, 0:cw], onescol[:], sq512[:, 0:cw],
                                     start=(c8 == 0), stop=(c8 == 7))
                    last.then_inc(s_pe, 1)
                    B.tag(s_pe, ("colsq", ci, c8))
                te.wait_ge(s_act, B.at(("rstdrow", ci)))
                if ci >= 1:
                    te.wait_ge(s_dve, B.at(("x3n", ci - 1)))
                te.matmul(pM[:, 0:512], ones1[:], mrow2[0:1, 0:512], start=True, stop=True)
                last = te.matmul(pM[:, 512:1024], ones1[:], mrow2[0:1, 512:1024],
                                 start=True, stop=True)
                last.then_inc(s_pe, 1)
                B.tag(s_pe, ("bc", ci))
                te.wait_ge(s_dve, B.at(("x3n", ci)))
                for s in range(4):
                    if ci > 0 or s > 0:
                        prev = (ci, s - 1) if s > 0 else (ci - 1, 3)
                        te.wait_ge(s_act, B.at(("x3ev",) + prev))
                    last = None
                    for j in range(8):
                        bank = pX0 if j < 4 else pX1
                        slot = (j % 4) * 128
                        last = te.transpose(bank[:, slot:slot + 128],
                                            x3p[:, j * cw + s * 128:j * cw + s * 128 + 128],
                                            ident5[:])
                    last.then_inc(s_pe, 1)
                    B.tag(s_pe, ("x3T", ci, s))

        two_pass(block, None, [("sync", f_sync), ("gpsimd", f_gpsimd),
                               ("vector", f_vector), ("scalar", f_scalar),
                               ("tensor", f_tensor)])


def build_stageC3b(T_loc):
    F1 = 4096
    nc = bass.Bass("TRN2", target_bir_lowering=False)
    io = {
        "gk_in": nc.dram_tensor("gk_in", [D, T_loc], BF16, kind="ExternalInput").ap(),
        "sig_in": nc.dram_tensor("sig_in", [D, T_loc], BF16, kind="ExternalInput").ap(),
        "s2m_in": nc.dram_tensor("s2m_in", [D, T_loc], F32, kind="ExternalInput").ap(),
        "FkT": nc.dram_tensor("FkT", [8, 128, F1], F32, kind="ExternalInput").ap(),
        "FvT": nc.dram_tensor("FvT", [32, 128, D], F32, kind="ExternalInput").ap(),
        "x3a_out": nc.dram_tensor("x3a_out", [T_loc, D], F32, kind="ExternalOutput").ap(),
    }
    emit_stageC3b(nc, io, T_loc)
    return nc


def stageC3b_inputs(inp, gk_core, sig_core, s2m_core, T, cid):
    F1 = 4096
    return {"gk_in": gk_core, "sig_in": sig_core, "s2m_in": s2m_core,
            "FkT": np.ascontiguousarray(inp["L1_attn_Fk"].T.reshape(8, 128, F1)),
            "FvT": np.ascontiguousarray(inp["L1_attn_Fv"].T.reshape(32, 128, D))}


def emit_fp6_ladder(v, width, au_ap, sg_ap, out_ap, scr, out_wait=None):
    nb = width // 32
    un = scr["un"][:, 0:width]
    u0 = scr["lad"][:, 0:width]
    fb = scr["lad2"][:, 0:width]
    bm = scr["bm"][:, 0:nb]
    rb = scr["rb6"][:, 0:nb]
    sc = scr["sc6"][:, 0:nb]
    au3 = au_ap.rearrange("p (nb w) -> p nb w", w=32)
    v.tensor_reduce(out=bm, in_=au3, axis=AX.X, op=ALU.max)
    v.drain()
    v.tensor_scalar(out=bm, in0=bm, scalar1=7.5e-30, scalar2=None, op0=ALU.max)
    v.drain()
    v.reciprocal(out=rb, in_=bm)
    v.tensor_scalar(out=sc, in0=bm, scalar1=1.0 / 7.5, scalar2=None, op0=ALU.mult)
    v.drain()
    v.tensor_scalar(out=rb, in0=rb, scalar1=7.5, scalar2=None, op0=ALU.mult)
    v.drain()
    un3 = un.rearrange("p (nb w) -> p nb w", w=32)
    rb3 = rb.rearrange("p (nb w) -> p nb w", w=1)
    v.tensor_tensor(out=un3, in0=au3, in1=bc(rb3, au3), op=ALU.mult)
    v.drain()
    st = scr["stage"][:, 0:width]
    v.tensor_scalar(out=fb, in0=un, scalar1=1.9375, scalar2=-4.0, op0=ALU.is_gt, op1=ALU.mult)
    v.tensor_scalar(out=st, in0=un, scalar1=3.875, scalar2=-2.0, op0=ALU.is_gt, op1=ALU.mult)
    v.drain()
    v.scalar_tensor_tensor(out=fb, in0=fb, scalar=8.0, in1=st, op0=ALU.add, op1=ALU.add)
    v.drain()
    v.tensor_tensor(out=un, in0=un, in1=fb, op=ALU.mult)
    v.drain()
    v.tensor_scalar(out=u0, in0=un, scalar1=0.5, scalar2=None, op0=ALU.add)
    v.reciprocal(out=fb, in_=fb)
    v.drain()
    for K in (8.0, 4.0, 2.0, 1.0):
        v.tensor_scalar(out=st, in0=u0, scalar1=K, scalar2=K,
                        op0=ALU.is_ge, op1=ALU.mult)
        v.drain()
        v.tensor_tensor(out=u0, in0=u0, in1=st, op=ALU.subtract)
        v.drain()
    v.scalar_tensor_tensor(out=u0, in0=un, scalar=0.5, in1=u0,
                           op0=ALU.add, op1=ALU.subtract)
    v.drain()
    v.tensor_tensor(out=u0, in0=u0, in1=fb, op=ALU.mult)
    v.drain()
    u03 = u0.rearrange("p (nb w) -> p nb w", w=32)
    sc3 = sc.rearrange("p (nb w) -> p nb w", w=1)
    v.tensor_tensor(out=u03, in0=u03, in1=bc(sc3, u03), op=ALU.mult)
    v.drain()
    if out_wait is not None:
        out_wait()
    return v.tensor_tensor(out=out_ap, in0=u0, in1=sg_ap, op=ALU.mult)


def emit_stageD(nc, io, T_loc, sst=None):
    CH = [(s, 512) for s in range(0, T_loc, 512)]
    NCH = len(CH)
    x3_in = io["x3_in"]
    w6Tq = io["w6Tq"]
    bias_row = io["bias_row"]
    x4_out = io["x4_out"]

    with ExitStack() as st:
        SST = sst if sst is not None else st

        def sb(name, shape, dt=F32):
            return st.enter_context(nc.sbuf_tensor(_tname(name), shape, dt))

        def psum(name, shape, dt=F32):
            return st.enter_context(nc.psum_tensor(_tname(name), shape, dt))

        x3c = sb("x3c", [128, 2 * 4 * D])
        w6T = sb("w6T", [128, 8 * D], BF16)
        q6a = sb("q6a", [128, 4 * D], BF16)
        q6b = sb("q6b", [128, 8 * 128], BF16)
        auq = sb("auq", [128, 4 * D])
        sgq = sb("sgq", [128, 4 * D])
        unq = sb("unq", [128, 4 * D])
        ladq = sb("ladq", [128, 4 * D])
        lad2q = sb("lad2q", [128, 4 * D])
        stq = sb("stq", [128, 4 * D])
        bmq = sb("bmq", [128, 128])
        rb6q = sb("rb6q", [128, 128])
        sc6q = sb("sc6q", [128, 128])
        biasr = sb("biasr", [1, D])
        biasbc = sb("biasbc", [128, D])
        zpre4 = sb("zpre4", [128, D])
        x4n = sb("x4n", [128, D])
        dump6 = sb("dump6", [128, D])
        stat6 = sb("stat6", [128, 16])
        ident6 = sb("ident6", [128, 128], BF16)
        ones1 = sb("ones1", [1, 128])
        scr = dict(un=unq, lad=ladq, lad2=lad2q, stage=stq, bm=bmq, rb6=rb6q, sc6=sc6q)

        pT0 = psum("pT0", [128, 512], BF16)
        pT1 = psum("pT1", [128, 512], BF16)
        pO0 = psum("pO0", [128, 512])
        pO1 = psum("pO1", [128, 512])
        pB = psum("pB", [128, 1024])

        s_x = [SST.enter_context(nc.semaphore(_semname())) for i in range(2)]
        s_c = SST.enter_context(nc.semaphore(_semname()))
        s_pe = SST.enter_context(nc.semaphore(_semname()))
        s_act = SST.enter_context(nc.semaphore(_semname()))
        s_dve = SST.enter_context(nc.semaphore(_semname()))
        s_st = SST.enter_context(nc.semaphore(_semname()))
        block = st.enter_context(nc.Block())

        def f_sync(sync, B):
            sync.dma_start(out=biasr[:], in_=bias_row).then_inc(s_c, 16)
            sync.dma_start(out=w6T[:].rearrange("p (j c) -> p j c", c=D),
                           in_=w6Tq.rearrange("j p c -> p j c")).then_inc(s_c, 16)
            for ci, (cs, cw) in enumerate(CH):
                if ci >= 2:
                    sync.wait_ge(s_dve, B.at(("x4", ci - 2, 3)))
                sync.dma_start(out=x3c[:, (ci % 2) * 4 * D:(ci % 2 + 1) * 4 * D]
                               .rearrange("p (s c) -> p s c", c=D),
                               in_=x3_in[cs:cs + cw, :].rearrange("(s p) c -> p s c", p=128)
                               ).then_inc(s_x[ci % 2], 16)
                for s in range(4):
                    sync.wait_ge(s_dve, B.at(("x4", ci, s)))
                    sync.dma_start(out=x4_out[cs + s * 128:cs + s * 128 + 128, :],
                                   in_=x4n[:]).then_inc(s_st, 16)
            sync.wait_ge(s_st, 16 * 4 * NCH)

        def f_gpsimd(g, B):
            g.iota(dump6[:, 0:128], [[1, 128]], base=0, channel_multiplier=-1,
                   allow_small_or_imprecise_dtypes=True)
            g.memset(ones1[:], 1.0)
            g.drain().then_inc(s_dve, 1)
            B.tag(s_dve, "init")

        def f_vector(v, B):
            v.wait_ge(s_dve, B.at("init"))
            v.wait_ge(s_c, 16)
            v.tensor_scalar(out=dump6[:, 128:256], in0=dump6[:, 0:128], scalar1=0.0,
                            scalar2=None, op0=ALU.is_equal)
            v.drain()
            v.tensor_copy(ident6[:], dump6[:, 128:256])
            v.drain().then_inc(s_dve, 1)
            B.tag(s_dve, "ident")
            for ci, (cs, cw) in enumerate(CH):
                v.wait_ge(s_act, B.at(("xas", ci)))
                if ci >= 1:
                    v.wait_ge(s_pe, B.at(("qT", ci - 1, 3)))
                emit_fp6_ladder(v, 4 * D, auq[:], sgq[:], q6a[:],
                                scr).then_inc(s_dve, 1)
                B.tag(s_dve, ("q6", ci))
                for s in range(4):
                    v.wait_ge(s_pe, B.at(("out1", ci, s)))
                    if (ci, s) != (0, 0):
                        prevz = (ci, s - 1) if s > 0 else (ci - 1, 3)
                        v.wait_ge(s_act, B.at(("sq6",) + prevz))
                    xa = x3c[:, (ci % 2) * 4 * D + s * D:(ci % 2) * 4 * D + (s + 1) * D]
                    v.tensor_tensor(out=zpre4[:, 0:512], in0=xa[:, 0:512], in1=pO0[:],
                                    op=ALU.add)
                    v.tensor_tensor(out=zpre4[:, 512:1024], in0=xa[:, 512:1024], in1=pO1[:],
                                    op=ALU.add)
                    v.drain()
                    v.tensor_tensor(out=zpre4[:], in0=zpre4[:], in1=biasbc[:], op=ALU.add)
                    v.drain()
                    v.tensor_reduce(out=stat6[:, 0:1], in_=zpre4[:], axis=AX.X, op=ALU.add)
                    v.drain()
                    v.tensor_scalar(out=stat6[:, 6:7], in0=stat6[:, 0:1], scalar1=1.0 / D,
                                    scalar2=None, op0=ALU.mult)
                    v.drain().then_inc(s_dve, 1)
                    B.tag(s_dve, ("zpre", ci, s))
                    v.wait_ge(s_act, B.at(("sq6", ci, s)))
                    v.tensor_scalar(out=stat6[:, 3:4], in0=stat6[:, 2:3], scalar1=1.0 / D,
                                    scalar2=None, op0=ALU.mult)
                    v.tensor_tensor(out=stat6[:, 7:8], in0=stat6[:, 6:7], in1=stat6[:, 6:7],
                                    op=ALU.mult)
                    v.drain()
                    v.tensor_tensor(out=stat6[:, 3:4], in0=stat6[:, 3:4], in1=stat6[:, 7:8],
                                    op=ALU.subtract)
                    v.drain()
                    v.tensor_scalar(out=stat6[:, 3:4], in0=stat6[:, 3:4], scalar1=EPS,
                                    scalar2=None, op0=ALU.add)
                    v.drain()
                    v.reciprocal(out=stat6[:, 3:4], in_=stat6[:, 3:4]).then_inc(s_dve, 1)
                    B.tag(s_dve, ("var", ci, s))
                    v.wait_ge(s_act, B.at(("rstd6", ci, s)))
                    v.wait_ge(s_st, 16 * (4 * ci + s))
                    v.tensor_scalar(out=x4n[:], in0=zpre4[:], scalar1=stat6[:, 6:7],
                                    scalar2=stat6[:, 5:6], op0=ALU.subtract,
                                    op1=ALU.mult).then_inc(s_dve, 1)
                    B.tag(s_dve, ("x4", ci, s))

        def f_scalar(sc, B):
            sc.wait_ge(s_pe, B.at("biasbc"))
            sc.copy(out=biasbc[:], in_=pB[:, 0:1024])
            sc.drain().then_inc(s_act, 1)
            B.tag(s_act, "biasev")
            for ci, (cs, cw) in enumerate(CH):
                sc.wait_ge(s_x[ci % 2], 16 * (ci // 2 + 1))
                if ci >= 1:
                    sc.wait_ge(s_dve, B.at(("q6", ci - 1)))
                xa = x3c[:, (ci % 2) * 4 * D:(ci % 2 + 1) * 4 * D]
                sc.activation(out=auq[:], in_=xa, func=AF.Abs)
                sc.drain()
                sc.activation(out=sgq[:], in_=xa, func=AF.Sign)
                sc.drain().then_inc(s_act, 1)
                B.tag(s_act, ("xas", ci))
                for s in range(4):
                    sc.wait_ge(s_pe, B.at(("qT", ci, s)))
                    if (ci, s) != (0, 0):
                        prev = (ci, s - 1) if s > 0 else (ci - 1, 3)
                        sc.wait_ge(s_pe, B.at(("out1",) + prev))
                    for j in range(8):
                        bank = pT0 if j < 4 else pT1
                        slot = (j % 4) * 128
                        ins = sc.copy(out=q6b[:, j * 128:(j + 1) * 128],
                                      in_=bank[:, slot:slot + 128])
                    ins.then_inc(s_act, 1)
                    B.tag(s_act, ("qev", ci, s))
                    sc.wait_ge(s_dve, B.at(("zpre", ci, s)))
                    sc.activation(out=dump6[:], in_=zpre4[:], func=AF.Square,
                                  accum_out=stat6[:, 2:3]).then_inc(s_act, 1)
                    B.tag(s_act, ("sq6", ci, s))
                    sc.wait_ge(s_dve, B.at(("var", ci, s)))
                    sc.activation(out=stat6[:, 4:5], in_=stat6[:, 3:4], func=AF.Ln)
                    sc.drain()
                    sc.activation(out=stat6[:, 5:6], in_=stat6[:, 4:5], func=AF.Exp,
                                  scale=0.5).then_inc(s_act, 1)
                    B.tag(s_act, ("rstd6", ci, s))

        def f_tensor(te, B):
            te.wait_ge(s_dve, B.at("ident"))
            te.wait_ge(s_c, 32)
            te.matmul(pB[:, 0:512], ones1[:], biasr[0:1, 0:512], start=True, stop=True)
            last = te.matmul(pB[:, 512:1024], ones1[:], biasr[0:1, 512:1024],
                             start=True, stop=True)
            last.then_inc(s_pe, 1)
            B.tag(s_pe, "biasbc")
            for ci, (cs, cw) in enumerate(CH):
                te.wait_ge(s_dve, B.at(("q6", ci)))
                for s in range(4):
                    if (ci, s) != (0, 0):
                        prev = (ci, s - 1) if s > 0 else (ci - 1, 3)
                        te.wait_ge(s_act, B.at(("qev",) + prev))
                    last = None
                    for j in range(8):
                        bank = pT0 if j < 4 else pT1
                        slot = (j % 4) * 128
                        last = te.transpose(bank[:, slot:slot + 128],
                                            q6a[:, s * D + j * 128:s * D + (j + 1) * 128],
                                            ident6[:])
                    last.then_inc(s_pe, 1)
                    B.tag(s_pe, ("qT", ci, s))
                    te.wait_ge(s_act, B.at(("qev", ci, s)))
                    if (ci, s) != (0, 0):
                        prev = (ci, s - 1) if s > 0 else (ci - 1, 3)
                        te.wait_ge(s_dve, B.at(("zpre",) + prev))
                    for j in range(8):
                        te.matmul(pO0[:], q6b[:, j * 128:(j + 1) * 128],
                                  w6T[:, j * D:j * D + 512], start=(j == 0), stop=(j == 7))
                    last = None
                    for j in range(8):
                        last = te.matmul(pO1[:], q6b[:, j * 128:(j + 1) * 128],
                                         w6T[:, j * D + 512:j * D + 1024],
                                         start=(j == 0), stop=(j == 7))
                    last.then_inc(s_pe, 1)
                    B.tag(s_pe, ("out1", ci, s))

        two_pass(block, None, [("sync", f_sync), ("gpsimd", f_gpsimd),
                               ("vector", f_vector), ("scalar", f_scalar),
                               ("tensor", f_tensor)])


def build_stageD(T_loc):
    nc = bass.Bass("TRN2", target_bir_lowering=False)
    io = {
        "x3_in": nc.dram_tensor("x3_in", [T_loc, D], F32, kind="ExternalInput").ap(),
        "w6Tq": nc.dram_tensor("w6Tq", [8, 128, D], BF16, kind="ExternalInput").ap(),
        "bias_row": nc.dram_tensor("bias_row", [1, D], F32, kind="ExternalInput").ap(),
        "x4_out": nc.dram_tensor("x4_out", [T_loc, D], F32, kind="ExternalOutput").ap(),
    }
    emit_stageD(nc, io, T_loc)
    return nc


def emit_c2_pass1(nc, io, T_loc, EX, sst=None):
    ksp, vsp = io["ksp"], io["vsp"]
    c2td = io["c2td"]
    cst_my = io["cst_my"]

    with ExitStack() as st:
        SST = sst if sst is not None else st

        def sb(name, shape, dt=F32):
            return st.enter_context(nc.sbuf_tensor(_tname(name), shape, dt))

        kp = sb("kp", [128, 2 * T_loc])
        vp = sb("vp", [128, 2 * T_loc])
        ek = sb("ek", [128, T_loc])
        ekv = sb("ekv", [128, T_loc])
        As = sb("As", [128, T_loc])
        Bs = sb("Bs", [128, T_loc])
        dmat = sb("dmat", [128, T_loc])
        cvs = sb("cvs", [128, 16])
        w16 = sb("w16", [128, 8])
        st8 = sb("st8", [128, 32])

        s_k = [SST.enter_context(nc.semaphore(_semname())) for i in range(2)]
        s_v = [SST.enter_context(nc.semaphore(_semname())) for i in range(2)]
        s_c = SST.enter_context(nc.semaphore(_semname()))
        s_act = SST.enter_context(nc.semaphore(_semname()))
        s_dve = SST.enter_context(nc.semaphore(_semname()))
        s_st = SST.enter_context(nc.semaphore(_semname()))
        block = st.enter_context(nc.Block())

        def f_sync(sync, B):
            sync.dma_start(out=cvs[:], in_=c2td).then_inc(s_c, 16)
            for j in range(8):
                if j >= 2:
                    sync.wait_ge(s_dve, B.at(("p1", j - 2)))
                sync.dma_start(out=kp[:, (j % 2) * T_loc:(j % 2 + 1) * T_loc],
                               in_=ksp[j * 128:(j + 1) * 128, EX:]).then_inc(s_k[j % 2], 16)
                sync.dma_start(out=vp[:, (j % 2) * T_loc:(j % 2 + 1) * T_loc],
                               in_=vsp[j * 128:(j + 1) * 128, EX:]).then_inc(s_v[j % 2], 16)
            sync.wait_ge(s_dve, B.at(("p1", 7)))
            sync.dma_start(out=cst_my.rearrange("(j p) s -> p j s", p=128),
                           in_=st8[:].rearrange("p (j s) -> p j s", s=4)).then_inc(s_st, 16)
            sync.wait_ge(s_st, 16)

        def f_scalar(sc, B):
            sc.wait_ge(s_c, 16)
            sc.activation(out=w16[:], in_=cvs[:, 0:8], func=AF.Exp)
            sc.drain()
            sc.activation(out=w16[:], in_=w16[:], func=AF.Identity, scale=-1.0)
            sc.drain()
            sc.activation(out=cvs[:, 0:8], in_=w16[:], func=AF.Exp)
            sc.drain().then_inc(s_act, 1)
            B.tag(s_act, "ew1")
            for j in range(8):
                sc.wait_ge(s_k[j % 2], 16 * (j // 2 + 1))
                if j >= 1:
                    sc.wait_ge(s_dve, B.at(("p1", j - 1)))
                sc.activation(out=ek[:], in_=kp[:, (j % 2) * T_loc:(j % 2 + 1) * T_loc],
                              func=AF.Exp)
                sc.drain().then_inc(s_act, 1)
                B.tag(s_act, ("ek1", j))

        def f_vector(v, B):
            v.wait_ge(s_act, B.at("ew1"))
            for j in range(8):
                v.wait_ge(s_act, B.at(("ek1", j)))
                v.wait_ge(s_v[j % 2], 16 * (j // 2 + 1))
                v.memset(dmat[:], 1.0)
                v.drain()
                v.tensor_scalar(out=dmat[:], in0=dmat[:], scalar1=cvs[:, j:j + 1],
                                scalar2=None, op0=ALU.mult)
                v.drain()
                v.tensor_tensor(out=ekv[:], in0=ek[:],
                                in1=vp[:, (j % 2) * T_loc:(j % 2 + 1) * T_loc], op=ALU.mult)
                v.drain()
                v.tensor_tensor_scan(out=As[:], data0=dmat[:], data1=ekv[:], initial=0.0,
                                     op0=ALU.mult, op1=ALU.add)
                v.drain()
                v.tensor_tensor_scan(out=Bs[:], data0=dmat[:], data1=ek[:], initial=0.0,
                                     op0=ALU.mult, op1=ALU.add)
                v.drain()
                v.tensor_copy(st8[:, j * 4 + 0:j * 4 + 1], As[:, T_loc - 1:T_loc])
                v.tensor_copy(st8[:, j * 4 + 1:j * 4 + 2], Bs[:, T_loc - 1:T_loc])
                v.tensor_copy(st8[:, j * 4 + 2:j * 4 + 3], As[:, T_loc - 2:T_loc - 1])
                v.tensor_copy(st8[:, j * 4 + 3:j * 4 + 4], Bs[:, T_loc - 2:T_loc - 1])
                v.drain().then_inc(s_dve, 1)
                B.tag(s_dve, ("p1", j))

        def f_gpsimd(g, B):
            pass

        def f_tensor(te, B):
            pass

        two_pass(block, None, [("sync", f_sync), ("gpsimd", f_gpsimd),
                               ("scalar", f_scalar), ("vector", f_vector),
                               ("tensor", f_tensor)])


def emit_c2_pass2(nc, io, T_loc, EX, sst=None):
    ksp, vsp, ysp = io["ksp"], io["vsp"], io["ysp"]
    c2td, carr = io["c2td"], io["carr"]
    cst_all = io["cst_all"]

    with ExitStack() as st:
        SST = sst if sst is not None else st

        def sb(name, shape, dt=F32):
            return st.enter_context(nc.sbuf_tensor(_tname(name), shape, dt))

        kp = sb("kp", [128, 2 * T_loc])
        vp = sb("vp", [128, 2 * T_loc])
        ek = sb("ek", [128, T_loc])
        ekv = sb("ekv", [128, T_loc])
        As = sb("As", [128, T_loc])
        Bs = sb("Bs", [128, T_loc])
        den = sb("den", [128, T_loc])
        yb = sb("yb", [128, T_loc], BF16)
        dmat = sb("dmat", [128, T_loc])
        cvs = sb("cvs", [128, 16])
        w16 = sb("w16", [128, 8])
        eu16 = sb("eu16", [128, 8])
        carrs = sb("carrs", [128, 40])
        cstl = sb("cstl", [128, 256])
        accs = sb("accs", [128, 32])
        coef = sb("coef", [128, 8])
        coef2 = sb("coef2", [128, 8])
        kcol = sb("kcol", [128, 8])
        vcol = sb("vcol", [128, 8])
        scr8 = sb("scr8", [128, 8])
        ycol = sb("ycol", [128, 8], BF16)
        zer = sb("zer", [128, 8 * (EX - 1)], BF16)

        s_k = [SST.enter_context(nc.semaphore(_semname())) for i in range(2)]
        s_v = [SST.enter_context(nc.semaphore(_semname())) for i in range(2)]
        s_c = SST.enter_context(nc.semaphore(_semname()))
        s_act = SST.enter_context(nc.semaphore(_semname()))
        s_dve = SST.enter_context(nc.semaphore(_semname()))
        s_st = SST.enter_context(nc.semaphore(_semname()))
        block = st.enter_context(nc.Block())

        def f_sync(sync, B):
            sync.dma_start(out=cvs[:], in_=c2td).then_inc(s_c, 16)
            sync.dma_start(out=carrs[:], in_=carr).then_inc(s_c, 16)
            sync.dma_start(out=cstl[:].rearrange("p (r j s) -> p r j s", j=8, s=4),
                           in_=cst_all.rearrange("(r j p) s -> p r j s", p=128, j=8)
                           ).then_inc(s_c, 16)
            sync.dma_start(out=kcol[:].rearrange("p (j o) -> p j o", o=1),
                           in_=ksp[:, EX - 1:EX].rearrange("(j p) o -> p j o", p=128),
                           allow_slow_non_contiguous=True).then_inc(s_c, 16)
            sync.dma_start(out=vcol[:].rearrange("p (j o) -> p j o", o=1),
                           in_=vsp[:, EX - 1:EX].rearrange("(j p) o -> p j o", p=128),
                           allow_slow_non_contiguous=True).then_inc(s_c, 16)
            for j in range(8):
                if j >= 2:
                    sync.wait_ge(s_dve, B.at(("p2", j - 2)))
                sync.dma_start(out=kp[:, (j % 2) * T_loc:(j % 2 + 1) * T_loc],
                               in_=ksp[j * 128:(j + 1) * 128, EX:]).then_inc(s_k[j % 2], 16)
                sync.dma_start(out=vp[:, (j % 2) * T_loc:(j % 2 + 1) * T_loc],
                               in_=vsp[j * 128:(j + 1) * 128, EX:]).then_inc(s_v[j % 2], 16)
                cj = j - 1
                if cj >= 0:
                    sync.wait_ge(s_dve, B.at(("y", cj)))
                    sync.dma_start(out=ysp[cj * 128:(cj + 1) * 128, EX:],
                                   in_=yb[:]).then_inc(s_st, 16)
            sync.wait_ge(s_dve, B.at(("y", 7)))
            sync.dma_start(out=ysp[7 * 128:8 * 128, EX:], in_=yb[:]).then_inc(s_st, 16)
            sync.wait_ge(s_dve, B.at("zer"))
            sync.dma_start(out=ysp[:, 0:EX - 1].rearrange("(j p) t -> p j t", p=128),
                           in_=zer[:].rearrange("p (j t) -> p j t", t=EX - 1)
                           ).then_inc(s_st, 16)
            sync.wait_ge(s_dve, B.at("ycol"))
            sync.dma_start(out=ysp[:, EX - 1:EX].rearrange("(j p) o -> p j o", p=128),
                           in_=ycol[:].rearrange("p (j o) -> p j o", o=1),
                           allow_slow_non_contiguous=True).then_inc(s_st, 16)
            sync.wait_ge(s_st, 16 * 10)

        def f_gpsimd(g, B):
            pass

        def f_scalar(sc, B):
            sc.wait_ge(s_c, 32)
            sc.activation(out=w16[:], in_=cvs[:, 0:8], func=AF.Exp)
            sc.drain()
            sc.activation(out=eu16[:], in_=cvs[:, 8:16], func=AF.Exp)
            sc.drain()
            sc.activation(out=w16[:], in_=w16[:], func=AF.Identity, scale=-1.0)
            sc.drain()
            sc.activation(out=cvs[:, 0:8], in_=w16[:], func=AF.Exp)
            sc.drain().then_inc(s_act, 1)
            B.tag(s_act, "ew2")
            sc.wait_ge(s_c, 48)
            for r in range(8):
                if r >= 1:
                    sc.wait_ge(s_dve, B.at(("acc", r - 1)))
                sc.activation(out=coef[:], in_=w16[:], func=AF.Exp,
                              scale=carrs[:, 8 + r:9 + r])
                sc.drain()
                sc.activation(out=coef2[:], in_=w16[:], func=AF.Exp,
                              scale=carrs[:, 32 + r:33 + r])
                sc.drain().then_inc(s_act, 1)
                B.tag(s_act, ("cexp", r))
            sc.wait_ge(s_c, 80)
            sc.activation(out=kcol[:], in_=kcol[:], func=AF.Exp)
            sc.drain().then_inc(s_act, 1)
            B.tag(s_act, "ekcol")
            for j in range(8):
                sc.wait_ge(s_k[j % 2], 16 * (j // 2 + 1))
                if j >= 1:
                    sc.wait_ge(s_dve, B.at(("p2", j - 1)))
                sc.activation(out=ek[:], in_=kp[:, (j % 2) * T_loc:(j % 2 + 1) * T_loc],
                              func=AF.Exp)
                sc.drain().then_inc(s_act, 1)
                B.tag(s_act, ("ek2", j))

        def f_vector(v, B):
            v.wait_ge(s_act, B.at("ew2"))
            v.memset(zer[:], 0.0)
            v.drain().then_inc(s_dve, 1)
            B.tag(s_dve, "zer")
            v.memset(accs[:], 0.0)
            v.drain()
            cst4 = cstl[:].rearrange("p (r j s) -> p r j s", j=8, s=4)
            for r in range(8):
                v.wait_ge(s_act, B.at(("cexp", r)))
                v.tensor_scalar(out=coef[:], in0=coef[:], scalar1=carrs[:, r:r + 1],
                                scalar2=None, op0=ALU.mult)
                v.tensor_scalar(out=coef2[:], in0=coef2[:], scalar1=carrs[:, 24 + r:25 + r],
                                scalar2=None, op0=ALU.mult)
                v.drain()
                for s_i, acc_off in [(0, 0), (1, 8)]:
                    v.tensor_tensor(out=scr8[:], in0=coef[:], in1=cst4[:, r, :, s_i],
                                    op=ALU.mult)
                    v.drain()
                    v.tensor_tensor(out=accs[:, acc_off:acc_off + 8],
                                    in0=accs[:, acc_off:acc_off + 8], in1=scr8[:], op=ALU.add)
                    v.drain()
                for s_i, acc_off in [(2, 16), (3, 24)]:
                    v.tensor_scalar(out=scr8[:], in0=cst4[:, r, :, s_i],
                                    scalar1=carrs[:, 16 + r:17 + r], scalar2=None, op0=ALU.mult)
                    v.drain()
                    v.tensor_tensor(out=accs[:, acc_off:acc_off + 8],
                                    in0=accs[:, acc_off:acc_off + 8], in1=scr8[:], op=ALU.add)
                    v.drain()
                    v.tensor_tensor(out=scr8[:], in0=coef2[:], in1=cst4[:, r, :, s_i - 2],
                                    op=ALU.mult)
                    v.drain()
                    v.tensor_tensor(out=accs[:, acc_off:acc_off + 8],
                                    in0=accs[:, acc_off:acc_off + 8], in1=scr8[:], op=ALU.add)
                    v.drain()
                v.drain().then_inc(s_dve, 1)
                B.tag(s_dve, ("acc", r))
            v.wait_ge(s_act, B.at("ekcol"))
            v.tensor_tensor(out=vcol[:], in0=kcol[:], in1=vcol[:], op=ALU.mult)
            v.drain()
            v.tensor_tensor(out=kcol[:], in0=kcol[:], in1=eu16[:], op=ALU.mult)
            v.drain()
            v.tensor_tensor(out=vcol[:], in0=vcol[:], in1=eu16[:], op=ALU.mult)
            v.drain()
            v.tensor_tensor(out=vcol[:], in0=vcol[:], in1=accs[:, 16:24], op=ALU.add)
            v.drain()
            v.tensor_tensor(out=kcol[:], in0=kcol[:], in1=accs[:, 24:32], op=ALU.add)
            v.drain()
            v.reciprocal(out=kcol[:], in_=kcol[:])
            v.drain()
            v.tensor_tensor(out=ycol[:], in0=vcol[:], in1=kcol[:], op=ALU.mult)
            v.drain().then_inc(s_dve, 1)
            B.tag(s_dve, "ycol")
            for j in range(8):
                v.wait_ge(s_act, B.at(("ek2", j)))
                v.wait_ge(s_v[j % 2], 16 * (j // 2 + 1))
                v.memset(dmat[:], 1.0)
                v.drain()
                v.tensor_scalar(out=dmat[:], in0=dmat[:], scalar1=cvs[:, j:j + 1],
                                scalar2=None, op0=ALU.mult)
                v.drain()
                v.tensor_tensor(out=ekv[:], in0=ek[:],
                                in1=vp[:, (j % 2) * T_loc:(j % 2 + 1) * T_loc], op=ALU.mult)
                v.drain()
                v.tensor_tensor_scan(out=As[:], data0=dmat[:], data1=ekv[:],
                                     initial=accs[:, 0 + j:1 + j], op0=ALU.mult, op1=ALU.add)
                v.drain()
                v.tensor_tensor_scan(out=Bs[:], data0=dmat[:], data1=ek[:],
                                     initial=accs[:, 8 + j:9 + j], op0=ALU.mult, op1=ALU.add)
                v.drain()
                v.tensor_scalar(out=ek[:], in0=ek[:], scalar1=eu16[:, j:j + 1],
                                scalar2=None, op0=ALU.mult)
                v.drain()
                v.tensor_tensor(out=ekv[:], in0=ek[:],
                                in1=vp[:, (j % 2) * T_loc:(j % 2 + 1) * T_loc], op=ALU.mult)
                v.drain()
                v.tensor_tensor(out=ekv[:, 1:T_loc], in0=ekv[:, 1:T_loc],
                                in1=As[:, 0:T_loc - 1], op=ALU.add)
                v.tensor_tensor(out=ekv[:, 0:1], in0=ekv[:, 0:1], in1=accs[:, j:j + 1],
                                op=ALU.add)
                v.drain()
                v.tensor_tensor(out=den[:, 1:T_loc], in0=ek[:, 1:T_loc],
                                in1=Bs[:, 0:T_loc - 1], op=ALU.add)
                v.tensor_tensor(out=den[:, 0:1], in0=ek[:, 0:1], in1=accs[:, 8 + j:9 + j],
                                op=ALU.add)
                v.drain()
                v.reciprocal(out=den[:], in_=den[:])
                v.drain()
                if j >= 1:
                    v.wait_ge(s_st, 16 * j)
                v.tensor_tensor(out=yb[:], in0=ekv[:], in1=den[:],
                                op=ALU.mult).then_inc(s_dve, 1)
                B.tag(s_dve, ("y", j))
                v.drain().then_inc(s_dve, 1)
                B.tag(s_dve, ("p2", j))

        def f_tensor(te, B):
            pass

        two_pass(block, None, [("sync", f_sync), ("gpsimd", f_gpsimd),
                               ("scalar", f_scalar), ("vector", f_vector),
                               ("tensor", f_tensor)])




def emit_w6_shard(nc, io, sst=None):
    wrow = io["wrow"]
    w6q_my = io["w6q_my"]
    with ExitStack() as st:
        SST = sst if sst is not None else st

        def sb(name, shape, dt=F32):
            return st.enter_context(nc.sbuf_tensor(_tname(name), shape, dt))

        wl = sb("wl", [128, D])
        au = sb("au", [128, D])
        sg = sb("sg", [128, D])
        un = sb("un", [128, D])
        lad = sb("lad", [128, D])
        lad2 = sb("lad2", [128, D])
        stq = sb("stq", [128, D])
        bm = sb("bm", [128, 32])
        rb6 = sb("rb6", [128, 32])
        sc6 = sb("sc6", [128, 32])
        qq = sb("qq", [128, D], BF16)
        scr = dict(un=un, lad=lad, lad2=lad2, stage=stq, bm=bm, rb6=rb6, sc6=sc6)
        s_w = SST.enter_context(nc.semaphore(_semname()))
        s_act = SST.enter_context(nc.semaphore(_semname()))
        s_dve = SST.enter_context(nc.semaphore(_semname()))
        s_st = SST.enter_context(nc.semaphore(_semname()))
        block = st.enter_context(nc.Block())

        def f_sync(sync, B):
            sync.dma_start(out=wl[:], in_=wrow).then_inc(s_w, 16)
            sync.wait_ge(s_dve, 1)
            sync.dma_start(out=w6q_my, in_=qq[:]).then_inc(s_st, 16)
            sync.wait_ge(s_st, 16)

        def f_scalar(sc, B):
            sc.wait_ge(s_w, 16)
            sc.activation(out=au[:], in_=wl[:], func=AF.Abs)
            sc.drain()
            sc.activation(out=sg[:], in_=wl[:], func=AF.Sign)
            sc.drain().then_inc(s_act, 1)

        def f_vector(v, B):
            v.wait_ge(s_act, 1)
            emit_fp6_ladder(v, D, au[:], sg[:], qq[:], scr)
            v.drain().then_inc(s_dve, 1)

        def f_gpsimd(g, B):
            pass

        def f_tensor(te, B):
            pass

        two_pass(block, None, [("sync", f_sync), ("gpsimd", f_gpsimd),
                               ("scalar", f_scalar), ("vector", f_vector),
                               ("tensor", f_tensor)])


def build_fused1(T, T_loc, EX):
    DF = 2816
    nx = T_loc + EX + 32
    TE = T_loc + EX
    nc = bass.Bass("TRN2", target_bir_lowering=False)

    def ein(name, shape, dt=F32):
        return nc.dram_tensor(name, shape, dt, kind="ExternalInput").ap()

    def eout(name, shape, dt=F32):
        return nc.dram_tensor(name, shape, dt, kind="ExternalOutput").ap()

    def intl(name, shape, dt=F32):
        return nc.dram_tensor(name, shape, dt, kind="Internal").ap()

    x_in = ein("x_in", [nx, D])
    c_WT = ein("c_WT", [D, K32])
    c_b32 = ein("c_b32", [K32, 1])
    w_gate = ein("w_gate", [DF, D])
    w_up = ein("w_up", [DF, D])
    w_down = ein("w_down", [D, DF])
    WkT = ein("WkT", [8, 128, D])
    WvT = ein("WvT", [8, 128, D])
    WrT = ein("WrT", [8, 128, D])
    c5 = ein("c5", [128, 40])
    bmask = ein("bmask", [128, 1])
    c2td = ein("c2td", [128, 16])
    x2sp = eout("x2sp", [TE, D])
    ksp = eout("ksp", [D, TE])
    vsp = eout("vsp", [D, TE])
    srsp = eout("srsp", [D, TE], BF16)
    cst_my = eout("cst_my", [1024, 4])
    z1sp = intl("z1sp", [TE, D])
    qxsp = intl("qxsp", [D, TE], BF16)
    flatbuf = intl("flatbuf", [2, 128 * 161])

    wrow = ein("wrow", [128, D])
    w6q_my = eout("w6q_my", [128, D], BF16)
    qwhbm = intl("qwhbm", [68, 128, 1024], BF16)
    wio = (dict(w_gate=w_gate, w_up=w_up, w_down=w_down, qwhbm=qwhbm)
           if (T_loc + EX) // 128 >= 23 else None)
    with ExitStack() as sst:
        emit_w6_shard(nc, dict(wrow=wrow, w6q_my=w6q_my), sst=sst)
        emit_stageA(nc, dict(x_in=x_in, c_WT=c_WT, c_b32=c_b32, z1_out=z1sp,
                             qxb_out=qxsp, flatbuf=flatbuf), T_loc, EX, sst=sst,
                    wio=wio)
        emit_stageB(nc, dict(z1_in=z1sp, qxb_in=qxsp, w_gate=w_gate, w_up=w_up,
                             w_down=w_down, x2_out=x2sp), T_loc, EX, sst=sst,
                    wio=wio)
        emit_stageC1(nc, dict(x2_in=x2sp, WkT=WkT, WvT=WvT, WrT=WrT, c5=c5,
                              bmask=bmask, k_out=ksp, v_out=vsp, sr_out=srsp),
                     T_loc, EX, sst=sst)
        emit_c2_pass1(nc, dict(ksp=ksp, vsp=vsp, c2td=c2td, cst_my=cst_my),
                      T_loc, EX, sst=sst)
    return nc


def build_fused2(T, T_loc, EX):
    F1 = 4096
    TE = T_loc + EX
    nc = bass.Bass("TRN2", target_bir_lowering=False)

    def ein(name, shape, dt=F32):
        return nc.dram_tensor(name, shape, dt, kind="ExternalInput").ap()

    def intl(name, shape, dt=F32):
        return nc.dram_tensor(name, shape, dt, kind="Internal").ap()

    x2sp = ein("x2sp", [TE, D])
    ksp = ein("ksp", [D, TE])
    vsp = ein("vsp", [D, TE])
    srsp = ein("srsp", [D, TE], BF16)
    cst_all = ein("cst_all", [8192, 4])
    c2td = ein("c2td", [128, 16])
    carr = ein("carr", [128, 40])
    WoT = ein("WoT", [8, 128, D])
    FrT = ein("FrT", [8, 128, D])
    c4 = ein("c4", [128, 32])
    bmask = ein("bmask", [128, 1])
    FkT = ein("FkT", [8, 128, F1])
    FvT = ein("FvT", [32, 128, D])
    w6Tq = ein("w6Tq", [8, 128, D], BF16)
    bias_row = ein("bias_row", [1, D])
    x4_out = nc.dram_tensor("x4_out", [T_loc, D], F32, kind="ExternalOutput").ap()
    ysp = intl("ysp", [D, TE], BF16)
    s2msp = intl("s2msp", [D, TE])
    gksp = intl("gksp", [D, TE], BF16)
    sigsp = intl("sigsp", [D, TE], BF16)
    x3sp = intl("x3sp", [T_loc, D])

    with ExitStack() as sst:
        emit_c2_pass2(nc, dict(ksp=ksp, vsp=vsp, ysp=ysp, c2td=c2td, carr=carr,
                               cst_all=cst_all), T_loc, EX, sst=sst)
        emit_stageC3a(nc, dict(x2_in=x2sp, y_in=ysp, sr_in=srsp, WoT=WoT, FrT=FrT,
                               c4=c4, bmask=bmask, s2m_out=s2msp, gk_out=gksp,
                               sig_out=sigsp), T_loc, EX, sst=sst)
        emit_stageC3b(nc, dict(gk_in=gksp[:, EX:], sig_in=sigsp[:, EX:],
                               s2m_in=s2msp[:, EX:], FkT=FkT, FvT=FvT,
                               x3a_out=x3sp), T_loc, sst=sst)
        emit_stageD(nc, dict(x3_in=x3sp, w6Tq=w6Tq, bias_row=bias_row,
                             x4_out=x4_out), T_loc, sst=sst)
    return nc


def fused_inputs(inp, T, cid):
    b, t0, T_loc = shard_tokens(T, cid)
    A = stageA_inputs(inp, T, cid)

    def col8v(vec):
        return np.ascontiguousarray(vec.reshape(8, 128).T)

    c5v = np.concatenate([col8v(inp["L1_attn_ln1_g"]), col8v(inp["L1_attn_ln1_b"]),
                          col8v(inp["L1_attn_tm_k"]), col8v(inp["L1_attn_tm_v"]),
                          col8v(inp["L1_attn_tm_r"])], axis=1)
    c4v = np.concatenate([col8v(inp["L1_attn_ln2_g"]), col8v(inp["L1_attn_ln2_b"]),
                          col8v(inp["L1_attn_fm_k"]), col8v(inp["L1_attn_fm_r"])], axis=1)
    c2v = np.concatenate([col8v(inp["L1_attn_time_decay"]),
                          col8v(inp["L1_attn_time_first"])], axis=1)
    c = cid % 4
    m = np.zeros(8); p = np.zeros(8)
    m2a = np.zeros(8); m2b = np.zeros(8); q = np.zeros(8)
    for r in range(8):
        if r // 4 == cid // 4:
            rp = r % 4
            if rp < c:
                m[r] = 1.0
                p[r] = T_loc * (c - 1 - rp)
            if rp == c - 1:
                m2a[r] = 1.0
            if rp < c - 1:
                m2b[r] = 1.0
                q[r] = T_loc * (c - 1 - rp) - 1
    carrv = np.concatenate([np.tile(v.astype(np.float32), (128, 1))
                            for v in (m, p, m2a, m2b, q)], axis=1)
    msk = np.full((128, 1), 0.0 if cid % 4 == 0 else 1.0, np.float32)
    return {
        "x_in": A["x_in"], "c_WT": A["c_WT"], "c_b32": A["c_b32"],
        "w_gate": inp["L0_ffn_w_gate"], "w_up": inp["L0_ffn_w_up"],
        "w_down": inp["L0_ffn_w_down"],
        "WkT": np.ascontiguousarray(inp["L1_attn_Wk"].T.reshape(8, 128, D)),
        "WvT": np.ascontiguousarray(inp["L1_attn_Wv"].T.reshape(8, 128, D)),
        "WrT": np.ascontiguousarray(inp["L1_attn_Wr"].T.reshape(8, 128, D)),
        "c5": np.ascontiguousarray(c5v), "bmask": msk,
        "WoT": np.ascontiguousarray(inp["L1_attn_Wo"].T.reshape(8, 128, D)),
        "FrT": np.ascontiguousarray(inp["L1_attn_Fr"].T.reshape(8, 128, D)),
        "c4": np.ascontiguousarray(c4v),
        "c2td": np.ascontiguousarray(c2v),
        "carr": np.ascontiguousarray(carrv.astype(np.float32)),
        "FkT": np.ascontiguousarray(inp["L1_attn_Fk"].T.reshape(8, 128, 4096)),
        "FvT": np.ascontiguousarray(inp["L1_attn_Fv"].T.reshape(32, 128, D)),
        "wrow": np.ascontiguousarray(inp["L1_ffn_weight"][cid * 128:(cid + 1) * 128]),
        "bias_row": np.ascontiguousarray(inp["L1_ffn_bias"].reshape(1, D)),
    }


def run_fused(inp, T, n_cores=8, trace=False):
    import ml_dtypes
    T_loc = T // 4
    nc1 = build_fused1(T, T_loc, EX)
    nc2 = build_fused2(T, T_loc, EX)
    im1 = []
    im2 = []
    for cid in range(n_cores):
        m = fused_inputs(inp, T, cid)
        im1.append({k: m[k] for k in ["x_in", "c_WT", "c_b32", "w_gate", "w_up",
                                      "w_down", "WkT", "WvT", "WrT", "c5",
                                      "bmask", "c2td", "wrow"]})
        im2.append({k: m[k] for k in ["c2td", "carr", "WoT", "FrT", "c4", "bmask",
                                      "FkT", "FvT", "bias_row"]})
    res1 = run_bass_kernel_spmd(nc1, im1, core_ids=list(range(n_cores)), trace=trace)
    cst_all = np.concatenate([np.asarray(res1.results[c]["cst_my"])
                              for c in range(n_cores)], axis=0)
    w6q = np.concatenate([np.asarray(res1.results[c]["w6q_my"])
                          for c in range(n_cores)], axis=0)
    w6Tq = np.ascontiguousarray(w6q.T.reshape(8, 128, D))
    for cid in range(n_cores):
        r = res1.results[cid]
        im2[cid].update({"x2sp": np.asarray(r["x2sp"]), "ksp": np.asarray(r["ksp"]),
                         "vsp": np.asarray(r["vsp"]), "srsp": np.asarray(r["srsp"]),
                         "cst_all": cst_all, "w6Tq": w6Tq})
    res2 = run_bass_kernel_spmd(nc2, im2, core_ids=list(range(n_cores)), trace=trace)
    Bb = inp["L0_attn_x"].shape[0]
    out = np.zeros((Bb, T, D), np.float32)
    for cid in range(n_cores):
        b, t0, T_loc = shard_tokens(T, cid)
        out[b, t0:t0 + T_loc] = res2.results[cid]["x4_out"]
    return out, (res1, res2)



_PROG_CACHE = {}


def _prog(name, fn, *args):
    key = (name,) + args
    if key not in _PROG_CACHE:
        _PROG_CACHE[key] = fn(*args)
    return _PROG_CACHE[key]


def kernel(**inp):
    x = inp["L0_attn_x"]
    Bb, T, D_ = x.shape
    T_loc = T // 4
    n_cores = 8
    cores = list(range(n_cores))
    nc1 = _prog("F1", build_fused1, T, T_loc, EX)
    nc2 = _prog("F2", build_fused2, T, T_loc, EX)
    im1 = []
    im2 = []
    for cid in cores:
        m = fused_inputs(inp, T, cid)
        im1.append({k: m[k] for k in ["x_in", "c_WT", "c_b32", "w_gate", "w_up",
                                      "w_down", "WkT", "WvT", "WrT", "c5",
                                      "bmask", "c2td", "wrow"]})
        im2.append({k: m[k] for k in ["c2td", "carr", "WoT", "FrT", "c4", "bmask",
                                      "FkT", "FvT", "bias_row"]})
    res1 = run_bass_kernel_spmd(nc1, im1, core_ids=cores)
    cst_all = np.concatenate([np.asarray(res1.results[c]["cst_my"]) for c in cores],
                             axis=0)
    w6q = np.concatenate([np.asarray(res1.results[c]["w6q_my"]) for c in cores],
                         axis=0)
    w6Tq = np.ascontiguousarray(w6q.T.reshape(8, 128, D))
    for cid in cores:
        r = res1.results[cid]
        im2[cid].update({"x2sp": np.asarray(r["x2sp"]), "ksp": np.asarray(r["ksp"]),
                         "vsp": np.asarray(r["vsp"]), "srsp": np.asarray(r["srsp"]),
                         "cst_all": cst_all, "w6Tq": w6Tq})
    res2 = run_bass_kernel_spmd(nc2, im2, core_ids=cores)
    out = np.zeros((Bb, T, D), np.float32)
    for cid in cores:
        b, t0, T_loc = shard_tokens(T, cid)
        out[b, t0:t0 + T_loc] = res2.results[cid]["x4_out"]
    return out
